# version 07
# baseline (speedup 1.0000x reference)
import numpy as np
import ml_dtypes
import concourse.bass as bass
import concourse.mybir as mybir
from concourse.bass_utils import run_bass_kernel_spmd

F32 = mybir.dt.float32
BF16 = mybir.dt.bfloat16
AF = mybir.ActivationFunctionType
ALU = mybir.AluOpType
AX = mybir.AxisListType.X

NCORE = 8
B, L, D = 2, 16384, 2048
DI = 4096
NST = 16
KC = 4
IN = 1024
TLOC = 4096
CLOC = 1024
EPS = 1e-5
BF = ml_dtypes.bfloat16

SZ = {
    "win": (2048, 1024), "w1": (2048, 1024), "w3": (2048, 1024),
    "w2": (1024, 2048), "wo": (512, 2048), "ws1": (2048, 128),
    "ws3": (2048, 128), "ws2": (1024, 256),
}
OFF = {}
_o = 0
for _k, (_r, _c) in SZ.items():
    OFF[_k] = _o
    _o += _r * _c
CHUNK = _o

ENGS = ("sync", "gpsimd", "scalar", "vector", "tensor")
GROUPS4 = [[0, 1, 2, 3], [4, 5, 6, 7]]
GROUPS8 = [list(range(8))]


class Prog:
    def __init__(self):
        self.seq = {e: [] for e in ENGS}
        self.cnt = {}

    def val(self, sem):
        return self.cnt.get(sem, 0)

    def last(self, sem):
        return (sem, self.val(sem))

    def add(self, eng, fn, wait=None, inc=None):
        if inc is not None:
            self.cnt[inc[0]] = self.cnt.get(inc[0], 0) + inc[1]
            iv = self.cnt[inc[0]]
        else:
            iv = None
        self.seq[eng].append((list(wait or []), fn, inc, iv))


class SbAlloc:
    def __init__(self, nc, base):
        self.nc = nc
        self.off = base

    def t(self, name, shape, dtype, align=64):
        nbytes = int(np.prod(shape[1:])) * (2 if dtype == BF16 else 4)
        self.off = (self.off + align - 1) // align * align
        h = self.nc.alloc_sbuf_tensor_at(name, list(shape), dtype, offset=self.off)
        self.off += nbytes
        return h


def build_program(stop=None, shared_arena=True):
    nc = bass.Bass("TRN2", target_bir_lowering=False, debug=False)

    EIN = lambda n, s, d=F32: nc.dram_tensor(n, s, d, kind="ExternalInput")
    x_loc = EIN("x_loc", [TLOC + 3, D])
    ip_sl = EIN("ip_sl", [1024, 2048])
    w1_sl = EIN("w1_sl", [1024, 2048])
    w2_sl = EIN("w2_sl", [2048, 1024])
    w3_sl = EIN("w3_sl", [1024, 2048])
    wo_sl = EIN("wo_sl", [2048, 512])
    ws1_sl = EIN("ws1_sl", [128, 2048])
    ws3_sl = EIN("ws3_sl", [128, 2048])
    ws2_sl = EIN("ws2_sl", [256, 1024])
    xp_full = EIN("xp_full", [33, DI])
    gw_full = EIN("gw_full", [8, D])
    nw_full = EIN("nw_full", [D])
    cw_full = EIN("cw_full", [DI, KC])
    cb_full = EIN("cb_full", [DI])
    alog_sl = EIN("alog_sl", [CLOC, NST])
    dpar_sl = EIN("dpar_sl", [CLOC])
    dtw_sl = EIN("dtw_sl", [CLOC])
    dtb_sl = EIN("dtb_sl", [CLOC])
    ident_bf = EIN("ident_bf", [128, 128], BF16)
    ident_f = EIN("ident_f", [128, 128], F32)

    out_loc = nc.dram_tensor("out_loc", [TLOC, D], F32, kind="ExternalOutput")

    arena_in = nc.dram_tensor("arena_in", [CHUNK], BF16)
    arena = nc.dram_tensor(
        "arena", [NCORE, CHUNK], BF16,
        addr_space="Shared" if shared_arena else "Local")
    u_in0 = nc.dram_tensor("u_in0", [DI, TLOC // 2], BF16)
    u_in1 = nc.dram_tensor("u_in1", [DI, TLOC // 2], BF16)
    u_ag2 = nc.dram_tensor("u_ag2", [4, DI, TLOC // 2], BF16)
    u_ag3 = nc.dram_tensor("u_ag3", [4, DI, TLOC // 2], BF16)
    xs_in = nc.dram_tensor("xs_in", [33, TLOC], BF16)
    xs_ag = nc.dram_tensor("xs_ag", [4, 33, TLOC], BF16)
    z_scr = nc.dram_tensor("z_scr", [DI, TLOC], BF16)
    y_in = nc.dram_tensor("y_in", [CLOC, L], BF16)
    y_ag = nc.dram_tensor("y_ag", [4, CLOC, L], BF16)
    u_loc = nc.dram_tensor("u_loc", [4, CLOC, TLOC], BF16)
    y_loc = nc.dram_tensor("y_loc", [4, CLOC, TLOC], BF16)
    xn3_scr = nc.dram_tensor("xn3_scr", [D, TLOC], BF16)

    def arena_view(r, key):
        rows, cols = SZ[key]
        return arena.ap()[r, OFF[key]:OFF[key] + rows * cols].rearrange(
            "(r c) -> r c", c=cols)

    SB_BASE = 16512
    ga = SbAlloc(nc, SB_BASE)
    s_idb = ga.t("s_idb", [128, 128], BF16)
    s_idf = ga.t("s_idf", [128, 128], F32)
    s_nw = ga.t("s_nw", [128, 16], F32)
    s_xpTb = ga.t("s_xpTb", [128, 32 * 33], BF16)
    s_gwT = ga.t("s_gwT", [128, 16 * 8], F32)
    s_cw = ga.t("s_cw", [128, 32 * KC], F32)
    s_cb = ga.t("s_cb", [128, 32], F32)
    s_al = ga.t("s_al", [128, 8 * NST], F32)
    s_dtw = ga.t("s_dtw", [128, 8], F32)
    s_dtb = ga.t("s_dtb", [128, 8], F32)
    s_dpr = ga.t("s_dpr", [128, 8], F32)
    s_mv = ga.t("s_mv", [128, 8], F32)
    s_comb = ga.t("s_comb", [128, 32 * 8], F32)
    s_cols = ga.t("s_cols", [128, 32], F32)
    s_kon = ga.t("s_kon", [128, 2], F32)
    GEN_END = ga.off

    p0 = SbAlloc(nc, GEN_END)
    p0_ld = [p0.t(f"p0_ld{i}", [128, 2048], F32) for i in range(2)]
    p0_bf = [p0.t(f"p0_bf{i}", [128, 2048], BF16) for i in range(2)]
    p0_ev = [p0.t(f"p0_ev{i}", [128, 128], BF16) for i in range(2)]
    P0_END = p0.off

    p1a = SbAlloc(nc, P0_END)
    s_xn2T = p1a.t("s_xn2T", [128, 16 * 4099], BF16)
    s_xld = [p1a.t(f"s_xld{i}", [128, 2048], F32) for i in range(2)]
    s_sq = p1a.t("s_sq", [128, 2048], F32)
    s_xn2 = p1a.t("s_xn2", [128, 2048], BF16)
    P1A_END = p1a.off
    _sxld_off = P0_END + ((16 * 4099 * 2 + 63) // 64) * 64
    _ssq_off = _sxld_off + 2 * 8192
    _sxn2_off = _ssq_off + 8192

    p1b = SbAlloc(nc, GEN_END)
    s_win = [p1b.t(f"s_win{i}", [128, 16 * 128], BF16) for i in range(2)]
    s_xin = p1b.t("s_xin", [128, 2052], F32)
    assert p1b.off <= P0_END, (p1b.off, P0_END)
    s_xs = nc.alloc_sbuf_tensor_at("s_xs", [33, 2 * 2048], F32, offset=_sxld_off)
    s_cv = nc.alloc_sbuf_tensor_at("s_cv", [128, 2048], F32, offset=_ssq_off)
    s_u = nc.alloc_sbuf_tensor_at("s_u", [128, 2048], BF16, offset=_sxn2_off)
    p1t_ = SbAlloc(nc, P1A_END)
    s_zbf = p1t_.t("s_zbf", [128, 2048], BF16)
    s_xs_bf = p1t_.t("s_xs_bf", [33, 2 * 2048], BF16)
    P1_END = p1t_.off

    p2 = SbAlloc(nc, GEN_END)
    s_Ball = [p2.t(f"s_Ball{i}", [128, NST * 1024], BF16) for i in range(2)]
    s_Call = [p2.t(f"s_Call{i}", [128, NST * 1024], BF16) for i in range(2)]
    s_rbc = [p2.t(f"s_rbc{i}", [128, 1024], BF16) for i in range(2)]
    s_u2 = [p2.t(f"s_u2{i}", [128, 1024], BF16) for i in range(2)]
    s_arg = p2.t("s_arg", [128, 1024], F32)
    s_t2 = p2.t("s_t2", [128, 1024], F32)
    s_dt = p2.t("s_dt", [128, 1024], F32)
    s_w2 = p2.t("s_w2", [128, 1024], BF16)
    s_a = [p2.t(f"s_a{i}", [128, 1024], F32) for i in range(2)]
    s_b = [p2.t(f"s_b{i}", [128, 1024], BF16) for i in range(2)]
    s_h = [p2.t(f"s_h{i}", [128, 1024], BF16) for i in range(2)]
    s_tmp = [p2.t(f"s_tmp{i}", [128, 1024], BF16) for i in range(2)]
    s_st = p2.t("s_st", [128, 8 * NST], F32)
    s_y2 = [p2.t(f"s_y2{i}", [128, 1024], BF16) for i in range(2)]
    P2_END = p2.off

    p3 = SbAlloc(nc, GEN_END)
    s_yg = [p3.t(f"s_yg{i}", [128, 512], BF16) for i in range(32)]
    s_sz = p3.t("s_sz", [128, 512], F32)
    s_zt = [p3.t(f"s_zt{i}", [128, 512], BF16) for i in range(2)]
    s_wo = [p3.t(f"s_wo{i}", [128, 512], BF16) for i in range(32)]
    s_x2 = [p3.t(f"s_x2{i}", [128, 2048], F32) for i in range(4)]
    s_xn3 = p3.t("s_xn3", [128, 2048], F32)
    s_x3f = p3.t("s_x3f", [128, 16 * 128], F32)
    s_x3b = p3.t("s_x3b", [128, 16 * 512], BF16)
    s_sc8 = p3.t("s_sc8", [128, 8], F32)
    s_sc8b = p3.t("s_sc8b", [128, 8], F32)
    s_m8 = p3.t("s_m8", [128, 8], F32)
    s_mr = p3.t("s_mr", [128, 8], F32)
    P3A_END = p3.off
    assert P3A_END <= _sxld_off, (P3A_END, _sxld_off)

    p3b = SbAlloc(nc, GEN_END)
    s_w1e = [p3b.t("s_w1e", [128, 16 * 1024], BF16)]
    s_w3e = p3b.t("s_w3e", [128, 16 * 1024], BF16)
    s_w2e = p3b.t("s_w2e", [128, 8 * 2048], BF16)
    s_xt = [p3b.t(f"s_xt{i}", [128, 16 * 512], BF16) for i in range(2)]
    s_g = [p3b.t(f"s_g{i}", [128, 512], BF16) for i in range(8)]
    s_sh = [p3b.t(f"s_sh{i}", [128, 512], F32) for i in range(2)]
    s_ye = p3b.t("s_ye", [128, 2048], F32)
    P3B_END = p3b.off

    assert max(P1A_END, P1_END, P2_END, P3A_END, P3B_END) <= 229300, (
        P0_END, P1A_END, P1_END, P2_END, P3A_END, P3B_END)

    pA = nc.alloc_psum_tensor("pA", [128, 512], F32)
    pB = nc.alloc_psum_tensor("pB", [128, 512], F32)
    pC = nc.alloc_psum_tensor("pC", [128, 512], F32)
    pD = nc.alloc_psum_tensor("pD", [128, 512], F32)
    pE1 = nc.alloc_psum_tensor("pE1", [128, 1024], F32)
    pE2 = nc.alloc_psum_tensor("pE2", [128, 1024], F32)
    pTR2 = [pC.bitcast(BF16)[:, 0:128], pD.bitcast(BF16)[:, 0:128]]

    pg = Prog()
    global _LAST_PG
    _LAST_PG = pg
    L_ = pg.last

    semnames = [
        "smalls", "p0ld", "p0v", "p0t", "p0st", "cwag",
        "xld0", "xld1", "p1v", "p1s", "p1t", "winld0", "winld1",
        "uzst", "cuag", "cuagB", "cxag",
        "uld0", "uld1", "bcB0", "bcB1", "bcC0", "bcC1",
        "p2v", "p2s", "p2g", "p2t", "yst", "cyag",
        "yld0", "yld1", "zld0", "zld1", "wold0", "wold1",
        "p3v", "p3s", "p3t", "xnst", "xnld0", "xnld1",
        "wexp", "wexpg", "outst", "oacc0", "oacc1", "oacc2", "oacc3",
        "ucp0", "ucp1", "ucp2", "ucp3", "ycp0", "ycp1", "ycp2", "ycp3",
    ]

    _regs = {}

    def qdyn(eng, name):
        if name not in _regs:
            _regs[name] = eng.compute_val(eng.partition_id() % 4)
        return _regs[name]

    for j in range(16):
        pg.add("sync", (lambda j=j: lambda e: e.dma_start(
            out=s_nw[:, j:j + 1],
            in_=nw_full.ap()[128 * j:128 * (j + 1)].unsqueeze(1)))(),
            inc=("smalls", 16))
    pg.add("sync", lambda e: e.dma_start(out=s_idb[:], in_=ident_bf.ap()),
           inc=("smalls", 16))
    pg.add("sync", lambda e: e.dma_start(out=s_idf[:], in_=ident_f.ap()),
           inc=("smalls", 16))
    for ci in range(32):
        pg.add("sync", (lambda ci=ci: lambda e: e.dma_start(
            out=s_cw[:, ci * KC:(ci + 1) * KC],
            in_=cw_full.ap()[128 * ci:128 * (ci + 1), :]))(), inc=("smalls", 16))
        pg.add("sync", (lambda ci=ci: lambda e: e.dma_start(
            out=s_cb[:, ci:ci + 1],
            in_=cb_full.ap()[128 * ci:128 * (ci + 1)].unsqueeze(1)))(),
            inc=("smalls", 16))
    for ct in range(8):
        pg.add("sync", (lambda ct=ct: lambda e: e.dma_start(
            out=s_al[:, ct * NST:(ct + 1) * NST],
            in_=alog_sl.ap()[128 * ct:128 * (ct + 1), :]))(), inc=("smalls", 16))
        for (dst, src) in ((s_dtw, dtw_sl), (s_dtb, dtb_sl), (s_dpr, dpar_sl)):
            pg.add("sync", (lambda ct=ct, dst=dst, src=src: lambda e: e.dma_start(
                out=dst[:, ct:ct + 1],
                in_=src.ap()[128 * ct:128 * (ct + 1)].unsqueeze(1)))(),
                inc=("smalls", 16))
    SM = pg.val("smalls")
    pg.add("vector", lambda e: e.memset(s_kon[:, 0:1], EPS), inc=("p1v", 1))
    pg.add("vector", lambda e: e.memset(s_kon[:, 1:2], -0.5), inc=("p1v", 1))
    pg.add("vector", lambda e: e.memset(s_mv[:], -1e30), inc=("p3v", 1))
    pg.add("scalar", lambda e: e.activation(s_al[:], s_al[:], AF.Exp),
           wait=[("smalls", SM)], inc=("p2s", 1))
    pg.add("vector", lambda e: e.tensor_scalar(s_al[:], s_al[:], -1.0, None,
                                               ALU.mult),
           wait=[L_("p2s")], inc=("p2v", 1))
    ANEG_V = pg.val("p2v")

    p0n = dict(tr=0)
    ev_marks = [0, 0]
    slot_marks = [0, 0]

    def prep_weight(src_ap, key, scale_nw=False):
        rows, cols = SZ[key][1], SZ[key][0]
        rows, cols = src_ap.shape
        for rt in range(rows // 128 if rows >= 128 else 1):
            nr = min(128, rows)
            lb, bb = p0_ld[rt % 2], p0_bf[rt % 2]
            pg.add("sync", (lambda rt=rt, lb=lb, nr=nr, cols=cols: lambda e: e.dma_start(
                out=lb[0:nr, :cols], in_=src_ap[128 * rt:128 * rt + nr, :]))(),
                wait=[L_("p0v")], inc=("p0ld", 16))
            pg.add("vector", (lambda lb=lb, bb=bb, nr=nr, cols=cols: lambda e: e.tensor_copy(
                bb[0:nr, :cols], lb[0:nr, :cols]))(),
                wait=[L_("p0ld")], inc=("p0v", 1))
            castv = pg.val("p0v")
            for jt in range(cols // 128):
                slot = p0n["tr"] % 2
                p0n["tr"] += 1
                w = [("p0v", castv), ("smalls", SM), ("p0v", slot_marks[slot])]
                pg.add("tensor", (lambda bb=bb, jt=jt, slot=slot: lambda e: e.transpose(
                    pTR2[slot], bb[:, jt * 128:(jt + 1) * 128], s_idb[:]))(),
                    wait=w, inc=("p0t", 1))
                ev = p0_ev[p0n["tr"] % 2]
                wv = [L_("p0t"), L_("p0st")]
                if scale_nw:
                    pg.add("vector", (lambda ev=ev, slot=slot, jt=jt: lambda e: e.tensor_scalar(
                        ev[:], pTR2[slot], s_nw[:, jt:jt + 1], None, ALU.mult))(),
                        wait=wv, inc=("p0v", 1))
                else:
                    pg.add("vector", (lambda ev=ev, slot=slot: lambda e: e.tensor_copy(
                        ev[:], pTR2[slot]))(),
                        wait=wv, inc=("p0v", 1))
                slot_marks[slot] = pg.val("p0v")
                ocols = rows
                dst = arena_in.ap()[OFF[key]:OFF[key] + cols * rows].rearrange(
                    "(c r) -> c r", r=rows)[jt * 128:(jt + 1) * 128,
                                            128 * rt:128 * rt + nr]
                pg.add("sync", (lambda dst=dst, ev=ev, nr=nr: lambda e: e.dma_start(
                    out=dst, in_=ev[:, 0:nr]))(),
                    wait=[L_("p0v")], inc=("p0st", 16))
                ev_marks[p0n["tr"] % 2] = pg.val("p0st")

    pg.cuts = {}
    pg.cuts["sm"] = {e: len(pg.seq[e]) for e in ENGS}
    prep_weight(ip_sl.ap(), "win", scale_nw=True)
    pg.cuts["w_ip"] = {e: len(pg.seq[e]) for e in ENGS}
    prep_weight(w1_sl.ap(), "w1")
    prep_weight(w3_sl.ap(), "w3")
    prep_weight(w2_sl.ap(), "w2")
    prep_weight(wo_sl.ap(), "wo")
    prep_weight(ws1_sl.ap(), "ws1")
    prep_weight(ws3_sl.ap(), "ws3")
    prep_weight(ws2_sl.ap(), "ws2")

    pg.cuts["p0w"] = {e: len(pg.seq[e]) for e in ENGS}
    pg.add("gpsimd", lambda e: e.collective_compute(
        "AllGather", ALU.bypass, replica_groups=GROUPS8,
        ins=[arena_in.ap().opt()], outs=[arena.ap().opt()]),
        wait=[L_("p0st")], inc=("cwag", 1))
    pg.cuts["p0ag"] = {e: len(pg.seq[e]) for e in ENGS}

    pg.add("sync", lambda e: e.dma_start(out=p0_ld[0][0:8, :], in_=gw_full.ap()),
           wait=[L_("p0v")], inc=("p0ld", 16))
    gldv = pg.val("p0ld")
    for jt in range(16):
        pg.add("tensor", (lambda jt=jt: lambda e: e.transpose(
            pE2[:, 0:128], p0_ld[0][:, jt * 128:(jt + 1) * 128], s_idf[:]))(),
            wait=[("p0ld", gldv), ("smalls", SM), L_("p0v")], inc=("p0t", 1))
        pg.add("vector", (lambda jt=jt: lambda e: e.tensor_copy(
            s_gwT[:, jt * 8:(jt + 1) * 8], pE2[:, 0:8]))(),
            wait=[L_("p0t")], inc=("p0v", 1))
    for half in range(2):
        pg.add("sync", (lambda half=half: lambda e: e.dma_start(
            out=p0_ld[1][0:33, :],
            in_=xp_full.ap()[:, half * 2048:(half + 1) * 2048]))(),
            wait=[L_("p0v")], inc=("p0ld", 16))
        xldv = pg.val("p0ld")
        for jt in range(16):
            ci = half * 16 + jt
            pg.add("tensor", (lambda jt=jt: lambda e: e.transpose(
                pE2[:, 0:128], p0_ld[1][:, jt * 128:(jt + 1) * 128], s_idf[:]))(),
                wait=[("p0ld", xldv), L_("p0v")], inc=("p0t", 1))
            pg.add("vector", (lambda ci=ci: lambda e: e.tensor_copy(
                s_xpTb[:, ci * 33:(ci + 1) * 33], pE2[:, 0:33]))(),
                wait=[L_("p0t")], inc=("p0v", 1))
    P0V_DONE = pg.val("p0v")
    P0T_DONE = pg.val("p0t")

    pg.cuts["p0"] = {e: len(pg.seq[e]) for e in ENGS}

    pg.add("vector", lambda e: e.memset(s_xld[0][:], 0.0), inc=("p1v", 1))
    pg.add("vector", lambda e: e.memset(s_xld[1][:], 0.0), inc=("p1v", 1))
    XLD_INIT = pg.val("p1v")

    xmark = {}
    sqmark = 0
    p1slot = [0, 0]

    for ti in range(33):
        r0 = 0 if ti == 0 else 3 + 128 * (ti - 1)
        nr = 3 if ti == 0 else 128
        xb = s_xld[ti % 2]
        w = []
        if ti >= 2:
            w.append(("p1v", xmark[ti - 2]))
        xsem = f"xld{ti % 2}"
        if ti < 2:
            w.append(("p1v", XLD_INIT))
        pg.add("sync", (lambda xb=xb, r0=r0, nr=nr: lambda e: e.dma_start(
            out=xb[0:nr, :], in_=x_loc.ap()[r0:r0 + nr, :]))(),
            wait=w, inc=(xsem, 16))
        xv = pg.val(xsem)
        pg.add("vector", (lambda xb=xb: lambda e: e.reduce_sum(
            s_cols[:, 0:1], xb[:], axis=AX))(),
            wait=[(xsem, xv)], inc=("p1v", 1))
        pg.add("scalar", (lambda xb=xb: lambda e: e.activation(
            s_sq[:], xb[:], AF.Square))(),
            wait=[(xsem, xv), ("p1v", sqmark)], inc=("p1s", 1))
        pg.add("vector", lambda e: e.reduce_sum(s_cols[:, 1:2], s_sq[:], axis=AX),
               wait=[L_("p1s")], inc=("p1v", 1))
        sqmark = pg.val("p1v")

        pg.add("vector", lambda e: e.tensor_scalar(s_cols[:, 2:3], s_cols[:, 0:1], 1.0 / D, None, ALU.mult), inc=("p1v", 1))
        pg.add("vector", lambda e: e.tensor_scalar(s_cols[:, 3:4], s_cols[:, 1:2], 1.0 / D, None, ALU.mult), inc=("p1v", 1))
        pg.add("vector", lambda e: e.tensor_tensor(s_cols[:, 4:5], s_cols[:, 2:3], s_cols[:, 2:3], ALU.mult), inc=("p1v", 1))
        pg.add("vector", lambda e: e.tensor_tensor(s_cols[:, 3:4], s_cols[:, 3:4], s_cols[:, 4:5], ALU.subtract), inc=("p1v", 1))
        pg.add("scalar", lambda e: e.activation(s_cols[:, 5:6], s_cols[:, 3:4],
                                                AF.Ln, bias=s_kon[:, 0:1]),
               wait=[L_("p1v")], inc=("p1s", 1))
        pg.add("scalar", lambda e: e.activation(s_cols[:, 6:7], s_cols[:, 5:6],
                                                AF.Exp, scale=s_kon[:, 1:2]), inc=("p1s", 1))
        pg.add("scalar", lambda e: e.activation(s_cols[:, 7:8], s_cols[:, 5:6],
                                                AF.Exp, scale=-1.0), inc=("p1s", 1))
        pg.add("vector", lambda e: e.tensor_tensor(s_cols[:, 8:9], s_cols[:, 3:4],
                                                   s_cols[:, 7:8], ALU.mult),
               wait=[L_("p1s")], inc=("p1v", 1))
        pg.add("scalar", lambda e: e.activation(s_cols[:, 9:10], s_cols[:, 8:9],
                                                AF.Ln, bias=s_kon[:, 0:1]),
               wait=[L_("p1v")], inc=("p1s", 1))
        pg.add("scalar", lambda e: e.activation(s_cols[:, 10:11], s_cols[:, 9:10],
                                                AF.Exp, scale=s_kon[:, 1:2]), inc=("p1s", 1))

        pg.add("vector", lambda e: e.tensor_tensor(s_cols[:, 11:12], s_cols[:, 6:7], s_cols[:, 10:11], ALU.mult), wait=[L_("p1s")], inc=("p1v", 1))
        pg.add("vector", lambda e: e.tensor_tensor(s_cols[:, 12:13], s_cols[:, 2:3], s_cols[:, 11:12], ALU.mult), inc=("p1v", 1))
        pg.add("vector", lambda e: e.tensor_scalar(s_cols[:, 12:13], s_cols[:, 12:13], -1.0, None, ALU.mult), inc=("p1v", 1))
        pg.add("vector", (lambda xb=xb: lambda e: e.scalar_tensor_tensor(
            s_xn2[:], xb[:], s_cols[:, 11:12],
            s_cols[:, 12:13].to_broadcast([128, 2048]),
            ALU.mult, ALU.add))(),
            wait=[L_("p1v")], inc=("p1v", 1))
        xmark[ti] = pg.val("p1v")
        xnv = pg.val("p1v")
        c0c = 0 if ti == 0 else 3 + 128 * (ti - 1)
        ncc = 3 if ti == 0 else 128
        for dt in range(16):
            sl = (dt + 16 * ti) % 2
            pg.add("tensor", (lambda dt=dt, sl=sl: lambda e: e.transpose(
                pTR2[sl], s_xn2[:, dt * 128:(dt + 1) * 128], s_idb[:]))(),
                wait=[("p1v", xnv), ("p1v", p1slot[sl]), ("smalls", SM),
                      ("p0v", P0V_DONE), ("p0t", P0T_DONE)],
                inc=("p1t", 1))
            pg.add("vector", (lambda dt=dt, sl=sl, c0c=c0c, ncc=ncc: lambda e: e.tensor_copy(
                s_xn2T[:, dt * 4099 + c0c:dt * 4099 + c0c + ncc],
                pTR2[sl][:, 0:ncc]))(),
                wait=[L_("p1t")], inc=("p1v", 1))
            p1slot[sl] = pg.val("p1v")
    LNT_V = pg.val("p1v")
    pg.cuts["p1a"] = {e: len(pg.seq[e]) for e in ENGS}

    pg.add("vector", lambda e: e.memset(s_xs[:], 0.0),
           wait=[("p1v", LNT_V)], inc=("p1v", 1))
    XSMEM_V = pg.val("p1v")
    win_marks = [0, 0]
    psum_marks = [0, 0, 0, 0]
    xin_mark = 0
    zbf_mark = 0
    u_mark = 0
    u_st_mark = 0
    mmctr = 0
    UZ_SB0 = None
    for sb in range(2):
        for ci in range(64):
            is_z = ci >= 32
            wb = s_win[ci % 2]
            rk, coff = ci // 8, (ci % 8) * 128
            wv_ap = arena_view(rk, "win")
            w0 = [("cwag", 1), ("p1t", win_marks[ci % 2]),
                  ("p0v", P0V_DONE), ("p0t", P0T_DONE)]
            wsem = f"winld{ci % 2}"
            for dt in range(16):
                pg.add("sync", (lambda wb=wb, wv_ap=wv_ap, dt=dt, coff=coff: lambda e: e.dma_start(
                    out=wb[:, dt * 128:(dt + 1) * 128],
                    in_=wv_ap[dt * 128:(dt + 1) * 128, coff:coff + 128]))(),
                    wait=w0 if dt == 0 else [], inc=(wsem, 16))
            wldv = pg.val(wsem)
            chunks = ([(0, 512), (512, 512), (1024, 512), (1536, 512), (2048, 3)]
                      if not is_z else
                      [(3, 512), (515, 512), (1027, 512), (1539, 512)])
            for (co, cw_) in chunks:
                pidx = mmctr % 4
                ps = [pA, pB, pE2[:, 0:512], pE2[:, 512:1024]][pidx]
                mmctr += 1
                base = sb * 2048
                for dt in range(16):
                    w = []
                    if dt == 0:
                        w = [(wsem, wldv), ("p1v", LNT_V),
                             ("p1v", psum_marks[pidx])]
                    pg.add("tensor", (lambda ps=ps, wb=wb, dt=dt, base=base, co=co, cw_=cw_: lambda e: e.matmul(
                        ps[:, 0:cw_], wb[:, dt * 128:(dt + 1) * 128],
                        s_xn2T[:, dt * 4099 + base + co:dt * 4099 + base + co + cw_],
                        start=(dt == 0), stop=(dt == 15)))(),
                        wait=w, inc=("p1t", 1) if dt == 15 else None)
                mmd = pg.val("p1t")
                if not is_z:
                    pg.add("vector", (lambda ps=ps, co=co, cw_=cw_: lambda e: e.tensor_copy(
                        s_xin[:, co:co + cw_], ps[:, 0:cw_]))(),
                        wait=[("p1t", mmd), ("p1s", xin_mark)], inc=("p1v", 1))
                else:
                    pg.add("vector", (lambda ps=ps, co=co, cw_=cw_: lambda e: e.tensor_copy(
                        s_zbf[:, co - 3:co - 3 + cw_], ps[:, 0:cw_]))(),
                        wait=[("p1t", mmd), L_("uzst")], inc=("p1v", 1))
                psum_marks[pidx] = pg.val("p1v")
            win_marks[ci % 2] = pg.val("p1t")
            evd = pg.val("p1v")
            pg.cuts[f"p1b_{sb}_{ci}_mm"] = {e: len(pg.seq[e]) for e in ENGS}
            if is_z:
                zi = ci - 32
                pg.add("sync", (lambda zi=zi, sb=sb: lambda e: e.dma_start(
                    out=z_scr.ap()[zi * 128:(zi + 1) * 128,
                                   sb * 2048:(sb + 1) * 2048],
                    in_=s_zbf[:]))(),
                    wait=[("p1v", evd)], inc=("uzst", 16))
                zbf_mark = pg.val("uzst")
            else:
                pg.add("vector", (lambda ci=ci: lambda e: e.tensor_scalar(
                    s_cv[:], s_xin[:, 0:2048], s_cw[:, ci * KC:ci * KC + 1],
                    None, ALU.mult))(),
                    wait=[("p1v", evd), ("p1s", xin_mark)], inc=("p1v", 1))
                for k in range(1, 4):
                    pg.add("vector", (lambda ci=ci, k=k: lambda e: e.scalar_tensor_tensor(
                        s_cv[:], s_xin[:, k:k + 2048],
                        s_cw[:, ci * KC + k:ci * KC + k + 1],
                        s_cv[:], ALU.mult, ALU.add))(), inc=("p1v", 1))
                cvd = pg.val("p1v")
                pg.add("scalar", (lambda ci=ci: lambda e: e.activation(
                    s_u[:], s_cv[:], AF.Silu, bias=s_cb[:, ci:ci + 1]))(),
                    wait=[("p1v", cvd), ("p1t", u_mark), L_("uzst")],
                    inc=("p1s", 1))
                xin_mark = pg.val("p1s")
                uv = pg.val("p1s")
                u_dst = u_in0 if sb == 0 else u_in1
                pg.add("sync", (lambda ci=ci, u_dst=u_dst: lambda e: e.dma_start(
                    out=u_dst.ap()[ci * 128:(ci + 1) * 128, :],
                    in_=s_u[:]))(),
                    wait=[("p1s", uv)], inc=("uzst", 16))
                u_st_mark = pg.val("uzst")
                for q in range(4):
                    w = [("p1s", uv), ("p0v", P0V_DONE)]
                    if q > 0:
                        w = [L_("p1v")]
                    pg.add("tensor", (lambda ci=ci, q=q: lambda e: e.matmul(
                        pE1[0:33, 0:512], s_xpTb[:, ci * 33:(ci + 1) * 33],
                        s_u[:, q * 512:(q + 1) * 512], start=True, stop=True))(),
                        wait=w, inc=("p1t", 1))
                    pg.add("vector", (lambda sb=sb, q=q: lambda e: e.tensor_tensor(
                        s_xs[:, sb * 2048 + q * 512:sb * 2048 + (q + 1) * 512],
                        s_xs[:, sb * 2048 + q * 512:sb * 2048 + (q + 1) * 512],
                        pE1[0:33, 0:512], ALU.add))(),
                        wait=[L_("p1t"), ("p1v", XSMEM_V)], inc=("p1v", 1))
                u_mark = pg.val("p1t")
            pg.cuts[f"p1b_{sb}_{ci}"] = {e: len(pg.seq[e]) for e in ENGS}
        if sb == 0:
            UZ_SB0 = pg.val("uzst")

    pg.add("vector", lambda e: e.tensor_copy(s_xs_bf[:], s_xs[:]),
           wait=[L_("p1v")], inc=("p1v", 1))
    pg.add("sync", lambda e: e.dma_start(out=xs_in.ap(), in_=s_xs_bf[:]),
           wait=[L_("p1v")], inc=("uzst", 16))
    UZ = pg.val("uzst")

    pg.add("gpsimd", lambda e: e.collective_compute(
        "AllGather", ALU.bypass, replica_groups=GROUPS4,
        ins=[u_in0.ap().opt()], outs=[u_ag2.ap().opt()]),
        wait=[("uzst", UZ_SB0)], inc=("cuag", 1))
    pg.add("gpsimd", lambda e: e.collective_compute(
        "AllGather", ALU.bypass, replica_groups=GROUPS4,
        ins=[u_in1.ap().opt()], outs=[u_ag3.ap().opt()]),
        wait=[("uzst", UZ)], inc=("cuagB", 1))
    pg.add("gpsimd", lambda e: e.collective_compute(
        "AllGather", ALU.bypass, replica_groups=GROUPS4,
        ins=[xs_in.ap().opt()], outs=[xs_ag.ap().opt()]),
        wait=[("uzst", UZ)], inc=("cxag", 1))

    pg.cuts["p1"] = {e: len(pg.seq[e]) for e in ENGS}

    pg.add("vector", lambda e: e.memset(s_st[:], 0.0),
           wait=[("cxag", 1)], inc=("p2v", 1))
    STMEM_V = pg.val("p2v")

    xs_v = xs_ag.ap()
    u_v2 = u_ag2.ap().rearrange("g (q c) t -> g q c t", q=4, c=CLOC)
    u_v3 = u_ag3.ap().rearrange("g (q c) t -> g q c t", q=4, c=CLOC)
    for rb in range(4):
        pg.add("gpsimd", (lambda rb=rb: lambda e: e.dma_start(
            out=u_loc.ap()[rb][:, 0:2048],
            in_=u_v2[rb][bass.ts(qdyn(e, "gpsimd"), 1)].squeeze(0)))(),
            wait=[("cuag", 1)], inc=(f"ucp{rb}", 16))
    for rb in range(4):
        pg.add("gpsimd", (lambda rb=rb: lambda e: e.dma_start(
            out=u_loc.ap()[rb][:, 2048:4096],
            in_=u_v3[rb][bass.ts(qdyn(e, "gpsimd"), 1)].squeeze(0)))(),
            wait=[("cuagB", 1)], inc=(f"ucp{rb}", 16))

    def bca(row, tt):
        r, toff = (tt * 1024) // TLOC, (tt * 1024) % TLOC
        return xs_v[r, row, toff:toff + 1024].unsqueeze(0).broadcast_to([128, 1024])

    st_mark = {}
    bc_marks = [0, 0]
    bcg_marks = [0, 0]
    u2_marks = [0, 0]
    pe_marks = [0, 0]
    y2_marks = [0, 0]
    abht_marks = {"a": [0, 0], "b": [0, 0], "h": [0, 0], "tmp": [0, 0]}

    for tt in range(16):
        bb = tt % 2
        w = [("cxag", 1),
             ("p2v", bc_marks[bb]), ("p2g", bcg_marks[bb])]
        bsem, csem = f"bcB{bb}", f"bcC{bb}"
        for n in range(NST):
            pg.add("sync", (lambda n=n, tt=tt, bb=bb: lambda e: e.dma_start(
                out=s_Ball[bb][:, n * 1024:(n + 1) * 1024], in_=bca(n, tt)))(),
                wait=w if n == 0 else [], inc=(bsem, 16))
            pg.add("scalar", (lambda n=n, tt=tt, bb=bb: lambda e: e.dma_start(
                out=s_Call[bb][:, n * 1024:(n + 1) * 1024],
                in_=bca(NST + n, tt)))(),
                wait=w if n == 0 else [], inc=(csem, 16))
        pg.add("sync", (lambda tt=tt, bb=bb: lambda e: e.dma_start(
            out=s_rbc[bb][:], in_=bca(32, tt)))(), inc=(bsem, 16))
        bBv, bCv = pg.val(bsem), pg.val(csem)
        pg.add("vector", lambda e: e.nop(),
               wait=[(bsem, bBv), (csem, bCv)], inc=("p2v", 1))
        pg.add("gpsimd", lambda e: e.nop(),
               wait=[(bsem, bBv), (csem, bCv)], inc=("p2g", 1))
        pg.add("scalar", lambda e: e.nop(),
               wait=[(bsem, bBv)], inc=("p2s", 1))
        r_blk, toff = (tt * 1024) // TLOC, (tt * 1024) % TLOC
        for ct in range(8):
            ub = s_u2[ct % 2]
            usem = f"uld{ct % 2}"
            pg.add("sync", (lambda ub=ub, r_blk=r_blk, toff=toff, ct=ct: lambda e: e.dma_start(
                out=ub[:],
                in_=u_loc.ap()[r_blk][128 * ct:128 * (ct + 1),
                                      toff:toff + 1024]))(),
                wait=[("p2v", u2_marks[ct % 2]), (f"ucp{r_blk}", 32)],
                inc=(usem, 16))
            uldv = pg.val(usem)
            pg.add("vector", (lambda bb=bb, ct=ct: lambda e: e.scalar_tensor_tensor(
                s_arg[:], s_rbc[bb][:], s_dtw[:, ct:ct + 1],
                s_dtb[:, ct:ct + 1].to_broadcast([128, 1024]),
                ALU.mult, ALU.add))(),
                wait=[L_("p2s"), ("smalls", SM)],
                inc=("p2v", 1))
            pg.add("scalar", lambda e: e.nop(),
                   wait=[L_("p2v")], inc=("p2s", 1))
            pg.add("scalar", lambda e: e.activation(s_t2[:], s_arg[:], AF.Abs),
                   inc=("p2s", 1))
            pg.add("scalar", lambda e: e.activation(s_t2[:], s_t2[:], AF.Exp,
                                                    scale=-1.0), inc=("p2s", 1))
            pg.add("scalar", lambda e: e.activation(s_t2[:], s_t2[:], AF.Ln,
                                                    bias=1.0), inc=("p2s", 1))
            dtsv = pg.val("p2s")
            pg.add("vector", lambda e: e.tensor_scalar(s_dt[:], s_arg[:], 0.0,
                                                       None, ALU.max),
                   wait=[("p2s", dtsv)], inc=("p2v", 1))
            pg.add("vector", lambda e: e.tensor_tensor(s_dt[:], s_dt[:], s_t2[:],
                                                       ALU.add), inc=("p2v", 1))
            pg.add("vector", (lambda ub=ub: lambda e: e.tensor_tensor(
                s_w2[:], s_dt[:], ub[:], ALU.mult))(),
                wait=[(usem, uldv), L_("p2g")], inc=("p2v", 1))
            w2v = pg.val("p2v")
            u2_marks[ct % 2] = w2v
            pidx = ct % 2
            ps = pE1 if pidx == 0 else pE2
            for n in range(NST):
                nb = n % 2
                ab, bbuf, hb, tb_ = s_a[nb], s_b[nb], s_h[nb], s_tmp[nb]
                b_on_g = n < 11
                tmp_on_g = n < 11
                pg.add("scalar", (lambda ab=ab, ct=ct, n=n: lambda e: e.activation(
                    ab[:], s_dt[:], AF.Exp,
                    scale=s_al[:, ct * NST + n:ct * NST + n + 1]))(),
                    wait=[("p2v", max(w2v, abht_marks["a"][nb]))],
                    inc=("p2s", 1))
                if b_on_g:
                    pg.add("gpsimd", (lambda bbuf=bbuf, bb=bb, n=n: lambda e: e.tensor_tensor(
                        bbuf[:], s_w2[:], s_Ball[bb][:, n * 1024:(n + 1) * 1024],
                        ALU.mult))(),
                        wait=[("p2v", max(w2v, abht_marks["b"][nb]))],
                        inc=("p2g", 1))
                else:
                    pg.add("vector", (lambda bbuf=bbuf, bb=bb, n=n: lambda e: e.tensor_tensor(
                        bbuf[:], s_w2[:], s_Ball[bb][:, n * 1024:(n + 1) * 1024],
                        ALU.mult))(),
                        wait=[("p2v", max(w2v, abht_marks["b"][nb]))],
                        inc=("p2v", 1))
                wscan = [L_("p2s"), ("p2t", abht_marks["h"][nb]),
                         ("p2v", st_mark.get((ct, n), 0))]
                if b_on_g:
                    wscan.append(L_("p2g"))
                pg.add("vector", (lambda ab=ab, bbuf=bbuf, hb=hb, ct=ct, n=n: lambda e: e.tensor_tensor_scan(
                    hb[:], ab[:], bbuf[:],
                    s_st[:, ct * NST + n:ct * NST + n + 1], ALU.mult, ALU.add))(),
                    wait=wscan,
                    inc=("p2v", 1))
                scan_v = pg.val("p2v")
                abht_marks["a"][nb] = scan_v
                abht_marks["b"][nb] = scan_v
                pg.add("vector", (lambda hb=hb, ct=ct, n=n: lambda e: e.tensor_copy(
                    s_st[:, ct * NST + n:ct * NST + n + 1], hb[:, 1023:1024]))(),
                    inc=("p2v", 1))
                st_mark[(ct, n)] = pg.val("p2v")
                if tmp_on_g:
                    pg.add("gpsimd", (lambda hb=hb, tb_=tb_, bb=bb, n=n: lambda e: e.tensor_tensor(
                        tb_[:], hb[:], s_Call[bb][:, n * 1024:(n + 1) * 1024],
                        ALU.mult))(),
                        wait=[("p2v", scan_v), ("p2t", abht_marks["tmp"][nb])],
                        inc=("p2g", 1))
                    tmpw = [L_("p2g")]
                else:
                    pg.add("vector", (lambda hb=hb, tb_=tb_, bb=bb, n=n: lambda e: e.tensor_tensor(
                        tb_[:], hb[:], s_Call[bb][:, n * 1024:(n + 1) * 1024],
                        ALU.mult))(),
                        wait=[("p2t", abht_marks["tmp"][nb])],
                        inc=("p2v", 1))
                    tmpw = [L_("p2v")]
                for q in range(2):
                    w = list(tmpw)
                    if n == 0 and q == 0:
                        w.append(("p2v", pe_marks[pidx]))
                    pg.add("tensor", (lambda ps=ps, tb_=tb_, q=q, n=n: lambda e: e.matmul(
                        ps[:, q * 512:(q + 1) * 512], s_idb[:],
                        tb_[:, q * 512:(q + 1) * 512],
                        start=(n == 0), stop=(n == NST - 1)))(),
                        wait=w, inc=("p2t", 1))
                abht_marks["tmp"][nb] = pg.val("p2t")
                abht_marks["h"][nb] = pg.val("p2t")
            yb = s_y2[ct % 2]
            pg.add("vector", (lambda yb=yb, ub=ub, ps=ps, ct=ct: lambda e: e.scalar_tensor_tensor(
                yb[:], ub[:], s_dpr[:, ct:ct + 1], ps[:], ALU.mult, ALU.add))(),
                wait=[L_("p2t"), L_("yst")], inc=("p2v", 1))
            pe_marks[pidx] = pg.val("p2v")
            u2_marks[ct % 2] = pg.val("p2v")
            pg.add("scalar", (lambda yb=yb, ct=ct, tt=tt: lambda e: e.dma_start(
                out=y_in.ap()[ct * 128:(ct + 1) * 128,
                              tt * 1024:(tt + 1) * 1024], in_=yb[:]))(),
                wait=[L_("p2v")], inc=("yst", 16))
            y2_marks[ct % 2] = pg.val("yst")
        bc_marks[bb] = pg.val("p2v")
        bcg_marks[bb] = pg.val("p2g")

    YST = pg.val("yst")
    pg.add("gpsimd", lambda e: e.collective_compute(
        "AllGather", ALU.bypass, replica_groups=GROUPS4,
        ins=[y_in.ap().opt()], outs=[y_ag.ap().opt()]),
        wait=[("yst", YST)], inc=("cyag", 1))

    pg.cuts["p2"] = {e: len(pg.seq[e]) for e in ENGS}

    y_v = y_ag.ap().rearrange("g c (q t) -> g c q t", q=4, t=TLOC)
    for rb in range(4):
        pg.add("gpsimd", (lambda rb=rb: lambda e: e.dma_start(
            out=y_loc.ap()[rb], in_=y_v[rb][:, bass.ts(qdyn(e, "gpsimd"), 1)]
            .squeeze(1)))(),
            wait=[("cyag", 1)], inc=(f"ycp{rb}", 16))
    outst_total = 0
    xnst_total = 0
    x2_marks = {}
    yg_mark = 0
    wo_mark = 0
    zt_marks = [0, 0]
    xb_vmark = [0, 0]
    xb_smark = [0, 0]
    x2v_mark = 0
    x3f_mark = 0
    x3b_mark = 0
    sq3_mark = 0
    trslot_mark = [0, 0]

    for tb in range(8):
        ysem = f"yld{tb % 2}"
        for ci in range(32):
            rr, cc = ci // 8, (ci % 8) * 128
            w = []
            if ci % 8 == 0:
                w = [(f"ycp{rr}", 16)]
                if ci == 0:
                    w.append(("p3t", yg_mark))
            pg.add("sync", (lambda ci=ci, rr=rr, cc=cc, tb=tb: lambda e: e.dma_start(
                out=s_yg[ci][:],
                in_=y_loc.ap()[rr][cc:cc + 128, tb * 512:(tb + 1) * 512]))(),
                wait=w, inc=(ysem, 16))
        yldv = pg.val(ysem)
        for ci in range(32):
            zb = s_zt[ci % 2]
            zsem = f"zld{ci % 2}"
            pg.add("scalar", (lambda ci=ci, zb=zb, tb=tb: lambda e: e.dma_start(
                out=zb[:], in_=z_scr.ap()[ci * 128:(ci + 1) * 128,
                                          tb * 512:(tb + 1) * 512]))(),
                wait=[("p3v", zt_marks[ci % 2])], inc=(zsem, 16))
            pg.add("scalar", (lambda zb=zb: lambda e: e.activation(
                s_sz[:], zb[:], AF.Silu))(),
                wait=[L_(zsem), L_("p3v")], inc=("p3s", 1))
            pg.add("vector", (lambda ci=ci: lambda e: e.tensor_tensor(
                s_yg[ci][:], s_yg[ci][:], s_sz[:], ALU.mult))(),
                wait=[L_("p3s"), (ysem, yldv)], inc=("p3v", 1))
            zt_marks[ci % 2] = pg.val("p3v")
        ygv = pg.val("p3v")
        for tc in range(4):
            xb = s_xld[tc % 2]
            xsem = f"xld{tc % 2}"
            if tb == 0 and tc < 2:
                w = [("cyag", 1)]
            else:
                w = [("p3v", xb_vmark[tc % 2]), ("p3s", xb_smark[tc % 2])]
            pg.add("sync", (lambda xb=xb, tb=tb, tc=tc: lambda e: e.dma_start(
                out=xb[:], in_=x_loc.ap()[3 + tb * 512 + tc * 128:
                                          3 + tb * 512 + (tc + 1) * 128, :]))(),
                wait=w, inc=(xsem, 16))
            xv = pg.val(xsem)
            pg.add("vector", (lambda xb=xb: lambda e: e.reduce_sum(
                s_cols[:, 0:1], xb[:], axis=AX))(),
                wait=[(xsem, xv)], inc=("p3v", 1))
            pg.add("scalar", (lambda xb=xb: lambda e: e.activation(
                s_sq[:], xb[:], AF.Square))(),
                wait=[(xsem, xv), ("p3v", sq3_mark)], inc=("p3s", 1))
            xb_smark[tc % 2] = pg.val("p3s")
            pg.add("vector", lambda e: e.reduce_sum(s_cols[:, 1:2], s_sq[:], axis=AX),
                   wait=[L_("p3s")], inc=("p3v", 1))
            sq3_mark = pg.val("p3v")

            pg.add("vector", lambda e: e.tensor_scalar(s_cols[:, 2:3], s_cols[:, 0:1], 1.0 / D, None, ALU.mult), inc=("p3v", 1))
            pg.add("vector", lambda e: e.tensor_scalar(s_cols[:, 3:4], s_cols[:, 1:2], 1.0 / D, None, ALU.mult), inc=("p3v", 1))
            pg.add("vector", lambda e: e.tensor_tensor(s_cols[:, 4:5], s_cols[:, 2:3], s_cols[:, 2:3], ALU.mult), inc=("p3v", 1))
            pg.add("vector", lambda e: e.tensor_tensor(s_cols[:, 3:4], s_cols[:, 3:4], s_cols[:, 4:5], ALU.subtract), inc=("p3v", 1))
            pg.add("scalar", lambda e: e.activation(s_cols[:, 5:6], s_cols[:, 3:4],
                                                    AF.Ln, bias=s_kon[:, 0:1]),
                   wait=[L_("p3v")], inc=("p3s", 1))
            pg.add("scalar", lambda e: e.activation(s_cols[:, 6:7], s_cols[:, 5:6],
                                                    AF.Exp, scale=s_kon[:, 1:2]),
                   inc=("p3s", 1))

            pg.add("vector", lambda e: e.tensor_scalar(s_cols[:, 7:8], s_cols[:, 6:7], 1.0, None, ALU.add), wait=[L_("p3s")], inc=("p3v", 1))
            pg.add("vector", lambda e: e.tensor_tensor(s_cols[:, 8:9], s_cols[:, 2:3], s_cols[:, 6:7], ALU.mult), inc=("p3v", 1))
            pg.add("vector", lambda e: e.tensor_scalar(s_cols[:, 8:9], s_cols[:, 8:9], -1.0, None, ALU.mult), inc=("p3v", 1))
            pg.add("vector", (lambda xb=xb, tc=tc: lambda e: e.scalar_tensor_tensor(
                s_x2[tc][:], xb[:], s_cols[:, 7:8],
                s_cols[:, 8:9].to_broadcast([128, 2048]),
                ALU.mult, ALU.add))(),
                wait=[L_("p3v"), L_("outst")], inc=("p3v", 1))
            xb_vmark[tc % 2] = pg.val("p3v")
        resv = pg.val("p3v")
        for dc in range(4):
            w = [("cwag", 1), ("p3t", wo_mark)]
            osem2 = f"wold{dc % 2}"
            for ci in range(32):
                rr = (ci * 128) // 512
                ro = (ci * 128) % 512
                wv_ap = arena_view(rr, "wo")
                pg.add("sync", (lambda ci=ci, wv_ap=wv_ap, ro=ro, dc=dc: lambda e: e.dma_start(
                    out=s_wo[ci][:],
                    in_=wv_ap[ro:ro + 128, dc * 512:(dc + 1) * 512]))(),
                    wait=w if ci == 0 else [], inc=(osem2, 16))
            woldv = pg.val(osem2)
            for tc in range(4):
                pidx = (dc * 4 + tc) % 4
                ps = [pA, pB, pE2[:, 0:512], pE2[:, 512:1024]][pidx]
                for ci in range(32):
                    w = []
                    if ci == 0:
                        w = [(osem2, woldv), ("p3v", ygv), ("p3v", resv),
                             L_("p3v")]
                    pg.add("tensor", (lambda ps=ps, ci=ci, tc=tc: lambda e: e.matmul(
                        ps[:], s_yg[ci][:, tc * 128:(tc + 1) * 128],
                        s_wo[ci][:], start=(ci == 0), stop=(ci == 31)))(),
                        wait=w, inc=("p3t", 1) if ci == 31 else None)
                pg.add("vector", (lambda ps=ps, tc=tc, dc=dc: lambda e: e.tensor_tensor(
                    s_x2[tc][:, dc * 512:(dc + 1) * 512],
                    s_x2[tc][:, dc * 512:(dc + 1) * 512], ps[:], ALU.add))(),
                    wait=[L_("p3t")], inc=("p3v", 1))
            wo_mark = pg.val("p3t")
        yg_mark = pg.val("p3t")
        x2v = pg.val("p3v")
        for tc in range(4):
            pg.add("sync", (lambda tc=tc, tb=tb: lambda e: e.dma_start(
                out=out_loc.ap()[tb * 512 + tc * 128:tb * 512 + (tc + 1) * 128, :],
                in_=s_x2[tc][:]))(),
                wait=[("p3v", x2v)], inc=("outst", 16))
            outst_total += 16
            x2_marks[(tb, tc)] = outst_total
            pg.add("vector", (lambda tc=tc: lambda e: e.reduce_sum(
                s_cols[:, 0:1], s_x2[tc][:], axis=AX))(),
                wait=[("p3v", x2v)], inc=("p3v", 1))
            pg.add("scalar", (lambda tc=tc: lambda e: e.activation(
                s_sq[:], s_x2[tc][:], AF.Square))(),
                wait=[("p3v", x2v), ("p3v", sq3_mark)], inc=("p3s", 1))
            pg.add("vector", lambda e: e.reduce_sum(s_cols[:, 1:2], s_sq[:], axis=AX),
                   wait=[L_("p3s")], inc=("p3v", 1))
            sq3_mark = pg.val("p3v")

            pg.add("vector", lambda e: e.tensor_scalar(s_cols[:, 2:3], s_cols[:, 0:1], 1.0 / D, None, ALU.mult), inc=("p3v", 1))
            pg.add("vector", lambda e: e.tensor_scalar(s_cols[:, 3:4], s_cols[:, 1:2], 1.0 / D, None, ALU.mult), inc=("p3v", 1))
            pg.add("vector", lambda e: e.tensor_tensor(s_cols[:, 4:5], s_cols[:, 2:3], s_cols[:, 2:3], ALU.mult), inc=("p3v", 1))
            pg.add("vector", lambda e: e.tensor_tensor(s_cols[:, 3:4], s_cols[:, 3:4], s_cols[:, 4:5], ALU.subtract), inc=("p3v", 1))
            pg.add("scalar", lambda e: e.activation(s_cols[:, 5:6], s_cols[:, 3:4],
                                                    AF.Ln, bias=s_kon[:, 0:1]),
                   wait=[L_("p3v")], inc=("p3s", 1))
            pg.add("scalar", lambda e: e.activation(s_cols[:, 6:7], s_cols[:, 5:6],
                                                    AF.Exp, scale=s_kon[:, 1:2]),
                   inc=("p3s", 1))

            pg.add("vector", lambda e: e.tensor_tensor(s_cols[:, 8:9], s_cols[:, 2:3], s_cols[:, 6:7], ALU.mult), wait=[L_("p3s")], inc=("p3v", 1))
            pg.add("vector", lambda e: e.tensor_scalar(s_cols[:, 8:9], s_cols[:, 8:9], -1.0, None, ALU.mult), inc=("p3v", 1))
            pg.add("vector", (lambda tc=tc: lambda e: e.scalar_tensor_tensor(
                s_xn3[:], s_x2[tc][:], s_cols[:, 6:7],
                s_cols[:, 8:9].to_broadcast([128, 2048]),
                ALU.mult, ALU.add))(),
                wait=[L_("p3v")], inc=("p3v", 1))
            x2v_mark = pg.val("p3v")
            xn3v = pg.val("p3v")
            for dt in range(16):
                sl = dt % 2
                ptr_ = pC if sl == 0 else pD
                pg.add("tensor", (lambda dt=dt, ptr_=ptr_: lambda e: e.transpose(
                    ptr_[:, 0:128],
                    s_xn3[:, dt * 128:(dt + 1) * 128], s_idf[:]))(),
                    wait=[("p3v", xn3v), ("p3v", trslot_mark[sl]),
                          ("p3t", x3f_mark if dt == 0 else 0)],
                    inc=("p3t", 1))
                pg.add("vector", (lambda dt=dt, ptr_=ptr_: lambda e: e.tensor_copy(
                    s_x3f[:, dt * 128:(dt + 1) * 128],
                    ptr_[:, 0:128]))(),
                    wait=[L_("p3t")], inc=("p3v", 1))
                trslot_mark[sl] = pg.val("p3v")
                pg.add("vector", (lambda dt=dt, tc=tc: lambda e: e.tensor_copy(
                    s_x3b[:, dt * 512 + tc * 128:dt * 512 + (tc + 1) * 128],
                    s_x3f[:, dt * 128:(dt + 1) * 128]))(),
                    wait=[("xnst", x3b_mark)] if dt == 0 else [],
                    inc=("p3v", 1))
            trv = pg.val("p3v")
            for dt in range(16):
                w = [("p3v", trv), ("p0v", P0V_DONE)] if dt == 0 else []
                pg.add("tensor", (lambda dt=dt: lambda e: e.matmul(
                    pE1[:, 0:8], s_x3f[:, dt * 128:(dt + 1) * 128],
                    s_gwT[:, dt * 8:(dt + 1) * 8],
                    start=(dt == 0), stop=(dt == 15)))(),
                    wait=w, inc=("p3t", 1) if dt == 15 else None)
            x3f_mark = pg.val("p3t")
            gmv = pg.val("p3t")
            pg.add("vector", lambda e: e.reduce_max(s_cols[:, 16:17], pE1[:, 0:8],
                                                    axis=AX),
                   wait=[("p3t", gmv)], inc=("p3v", 1))
            pg.add("vector", lambda e: e.tensor_scalar(
                s_cols[:, 17:18], s_cols[:, 16:17], -1.0, None, ALU.mult),
                inc=("p3v", 1))
            pg.add("scalar", lambda e: e.activation(s_sc8[:], pE1[:, 0:8], AF.Exp,
                                                    bias=s_cols[:, 17:18]),
                   wait=[L_("p3v")], inc=("p3s", 1))
            pg.add("vector", lambda e: e.reduce_sum(s_cols[:, 18:19], s_sc8[:],
                                                    axis=AX),
                   wait=[L_("p3s")], inc=("p3v", 1))
            pg.add("vector", lambda e: e.reciprocal(s_cols[:, 19:20],
                                                    s_cols[:, 18:19]),
                   inc=("p3v", 1))
            pg.add("vector", lambda e: e.tensor_scalar(
                s_sc8[:], s_sc8[:], s_cols[:, 19:20], None, ALU.mult),
                inc=("p3v", 1))
            pg.add("vector", lambda e: e.max(s_m8[:], s_sc8[:]), inc=("p3v", 1))
            pg.add("vector", lambda e: e.tensor_copy(s_mv[:, 0:2], s_m8[:, 0:2]),
                   inc=("p3v", 1))
            pg.add("vector", lambda e: e.match_replace(s_mr[:], s_mv[:], s_sc8[:],
                                                       -1e30),
                   wait=[L_("p3v")], inc=("p3v", 1))
            pg.add("vector", lambda e: e.tensor_tensor(s_sc8b[:], s_sc8[:],
                                                       s_mr[:], ALU.not_equal),
                   inc=("p3v", 1))
            pg.add("vector", (lambda tb=tb, tc=tc: lambda e: e.tensor_tensor(
                s_comb[:, (tb * 4 + tc) * 8:(tb * 4 + tc + 1) * 8],
                s_sc8b[:], s_sc8[:], ALU.mult))(), inc=("p3v", 1))
        x3v = pg.val("p3v")
        for dt in range(16):
            pg.add("sync", (lambda dt=dt, tb=tb: lambda e: e.dma_start(
                out=xn3_scr.ap()[dt * 128:(dt + 1) * 128,
                                 tb * 512:(tb + 1) * 512],
                in_=s_x3b[:, dt * 512:(dt + 1) * 512]))(),
                wait=[("p3v", x3v)] if dt == 0 else [], inc=("xnst", 16))
            xnst_total += 16
        x3b_mark = xnst_total

    P3A_V = pg.val("p3v")
    P3A_T = pg.val("p3t")
    P3A_S = pg.val("p3s")

    pg.cuts["p3a"] = {e: len(pg.seq[e]) for e in ENGS}

    oacc_tot = {0: 0, 1: 0, 2: 0, 3: 0}
    last_acc = None
    g_mark = 0
    xt_marks = [0, 0]
    pe3_marks = [0, 0]
    pab_marks = {}
    sh_marks = [0, 0]
    wbuf_marks = [0]

    for e_idx in range(9):
        w1buf = s_w1e[0]
        w = [("p3v", P3A_V), ("p3t", P3A_T), ("p3s", P3A_S),
             ("outst", outst_total), ("xnst", xnst_total),
             ("p3t", wbuf_marks[0])]
        first = [set()]

        def wl(fn, q="sync"):
            ww = list(w) if q not in first[0] else []
            first[0].add(q)
            pg.add(q, fn, wait=ww, inc=("wexpg" if q == "gpsimd" else "wexp", 16))
        if e_idx < 8:
            v1 = arena_view(e_idx, "w1")
            v3 = arena_view(e_idx, "w3")
            v2 = arena_view(e_idx, "w2")
            for dt in range(16):
                wl((lambda dt=dt, v1=v1, w1buf=w1buf: lambda e: e.dma_start(
                    out=w1buf[:, dt * 1024:(dt + 1) * 1024],
                    in_=v1[dt * 128:(dt + 1) * 128, :]))())
            for dt in range(16):
                wl((lambda dt=dt, v3=v3: lambda e: e.dma_start(
                    out=s_w3e[:, dt * 1024:(dt + 1) * 1024],
                    in_=v3[dt * 128:(dt + 1) * 128, :]))(), q="scalar")
            for it in range(8):
                wl((lambda it=it, v2=v2: lambda e: e.dma_start(
                    out=s_w2e[:, it * 2048:(it + 1) * 2048],
                    in_=v2[it * 128:(it + 1) * 128, :]))(), q="gpsimd")
        else:
            for dt in range(16):
                for rr in range(8):
                    v = arena_view(rr, "ws1")
                    wl((lambda dt=dt, rr=rr, v=v, w1buf=w1buf: lambda e: e.dma_start(
                        out=w1buf[:, dt * 1024 + rr * 128:dt * 1024 + (rr + 1) * 128],
                        in_=v[dt * 128:(dt + 1) * 128, :]))())
            for dt in range(16):
                for rr in range(8):
                    v = arena_view(rr, "ws3")
                    wl((lambda dt=dt, rr=rr, v=v: lambda e: e.dma_start(
                        out=s_w3e[:, dt * 1024 + rr * 128:dt * 1024 + (rr + 1) * 128],
                        in_=v[dt * 128:(dt + 1) * 128, :]))(), q="scalar")
            for it in range(8):
                for rr in range(8):
                    v = arena_view(rr, "ws2")
                    wl((lambda it=it, rr=rr, v=v: lambda e: e.dma_start(
                        out=s_w2e[:, it * 2048 + rr * 256:it * 2048 + (rr + 1) * 256],
                        in_=v[it * 128:(it + 1) * 128, :]))(), q="gpsimd")
        wv = pg.val("wexp")
        wvg = pg.val("wexpg")
        for tb in range(8):
            xt = s_xt[tb % 2]
            xnsem = f"xnld{tb % 2}"
            w = [("xnst", xnst_total), ("p3t", xt_marks[tb % 2])]
            for dt in range(16):
                pg.add("sync", (lambda dt=dt, tb=tb, xt=xt: lambda e: e.dma_start(
                    out=xt[:, dt * 512:(dt + 1) * 512],
                    in_=xn3_scr.ap()[dt * 128:(dt + 1) * 128,
                                     tb * 512:(tb + 1) * 512]))(),
                    wait=w if dt == 0 else [], inc=(xnsem, 16))
            xnv = pg.val(xnsem)
            for it in range(8):
                ph = pA if it % 2 == 0 else pC
                pq = pB if it % 2 == 0 else pD
                hk = f"a{it % 2}"
                qk = f"b{it % 2}"
                for dt in range(16):
                    w = []
                    if dt == 0:
                        w = [("wexp", wv), (xnsem, xnv),
                             ("p3s", pab_marks.get(hk, 0))]
                    pg.add("tensor", (lambda w1buf=w1buf, xt=xt, it=it, dt=dt, ph=ph: lambda e: e.matmul(
                        ph[:], w1buf[:, dt * 1024 + it * 128:dt * 1024 + (it + 1) * 128],
                        xt[:, dt * 512:(dt + 1) * 512],
                        start=(dt == 0), stop=(dt == 15)))(),
                        wait=w, inc=("p3t", 1) if dt == 15 else None)
                for dt in range(16):
                    w = [("p3v", pab_marks.get(qk, 0))] if dt == 0 else []
                    pg.add("tensor", (lambda xt=xt, it=it, dt=dt, pq=pq: lambda e: e.matmul(
                        pq[:], s_w3e[:, dt * 1024 + it * 128:dt * 1024 + (it + 1) * 128],
                        xt[:, dt * 512:(dt + 1) * 512],
                        start=(dt == 0), stop=(dt == 15)))(),
                        wait=w, inc=("p3t", 1) if dt == 15 else None)
                hgv = pg.val("p3t")
                shb = s_sh[it % 2]
                pg.add("scalar", (lambda ph=ph, shb=shb: lambda e: e.activation(
                    shb[:], ph[:, 0:512], AF.Silu))(),
                       wait=[("p3t", hgv), ("p3v", sh_marks[it % 2])],
                       inc=("p3s", 1))
                pab_marks[hk] = pg.val("p3s")
                pg.add("vector", (lambda it=it, pq=pq, shb=shb: lambda e: e.tensor_tensor(
                    s_g[it][:], shb[:], pq[:, 0:512], ALU.mult))(),
                    wait=[L_("p3s"), ("p3t", g_mark if it == 0 else 0)],
                    inc=("p3v", 1))
                pab_marks[qk] = pg.val("p3v")
                sh_marks[it % 2] = pg.val("p3v")
            gv = pg.val("p3v")
            for tc in range(4):
                for dh in range(2):
                    ps = pE1 if dh == 0 else pE2
                    for q in range(2):
                        for it in range(8):
                            w = []
                            if it == 0:
                                w = [("p3v", gv), ("p3v", pe3_marks[dh]),
                                     ("wexpg", wvg)]
                            pg.add("tensor", (lambda ps=ps, it=it, tc=tc, dh=dh, q=q: lambda e: e.matmul(
                                ps[:, q * 512:(q + 1) * 512],
                                s_g[it][:, tc * 128:(tc + 1) * 128],
                                s_w2e[:, it * 2048 + dh * 1024 + q * 512:
                                      it * 2048 + dh * 1024 + (q + 1) * 512],
                                start=(it == 0), stop=(it == 7)))(),
                                wait=w, inc=("p3t", 1) if it == 7 else None)
                eov = pg.val("p3t")
                ye_war = [last_acc] if last_acc else []
                for dh in range(2):
                    ps = pE1 if dh == 0 else pE2
                    if e_idx < 8:
                        cc_ = s_comb[:, (tb * 4 + tc) * 8 + e_idx:
                                     (tb * 4 + tc) * 8 + e_idx + 1]
                        pg.add("vector", (lambda ps=ps, dh=dh, cc_=cc_: lambda e: e.tensor_scalar(
                            s_ye[:, dh * 1024:(dh + 1) * 1024], ps[:], cc_,
                            None, ALU.mult))(),
                            wait=([("p3t", eov)] + ye_war if dh == 0 else []),
                            inc=("p3v", 1))
                    else:
                        pg.add("vector", (lambda ps=ps, dh=dh: lambda e: e.tensor_copy(
                            s_ye[:, dh * 1024:(dh + 1) * 1024], ps[:]))(),
                            wait=([("p3t", eov)] + ye_war if dh == 0 else []),
                            inc=("p3v", 1))
                pe3_marks[0] = pg.val("p3v")
                pe3_marks[1] = pg.val("p3v")
                yev = pg.val("p3v")
                osem = f"oacc{tc}"
                pg.add("gpsimd", (lambda tb=tb, tc=tc: lambda e: e.dma_start(
                    out=out_loc.ap()[tb * 512 + tc * 128:
                                     tb * 512 + (tc + 1) * 128, :],
                    in_=s_ye[:], accum_op=ALU.add))(),
                    wait=[("p3v", yev), ("outst", outst_total),
                          (osem, oacc_tot[tc])],
                    inc=(osem, 16))
                oacc_tot[tc] += 16
                last_acc = (osem, oacc_tot[tc])
            g_mark = pg.val("p3t")
            xt_marks[tb % 2] = pg.val("p3t")
        wbuf_marks[0] = pg.val("p3t")

    final_oacc = [(f"oacc{tc}", oacc_tot[tc]) for tc in range(4)]
    pg.add("sync", lambda e: e.nop(),
           wait=[("outst", outst_total)] + final_oacc)
    pg.add("gpsimd", lambda e: e.nop(), wait=final_oacc)
    pg.add("vector", lambda e: e.nop(), wait=final_oacc)

    if stop is not None:
        for e in ENGS:
            pg.seq[e] = pg.seq[e][:pg.cuts[stop][e]]

    from contextlib import ExitStack
    with ExitStack() as ctx:
        sems = {n: ctx.enter_context(nc.semaphore(n)) for n in semnames}
        block = ctx.enter_context(nc.Block())

        CHAINED = ("vector", "scalar", "gpsimd")

        def runner(name):
            def run(eng):
                prev = None
                for waits, fn, inc, iv in pg.seq[name]:
                    seen = {}
                    if name in CHAINED and prev is not None:
                        seen[prev[0]] = prev[1]
                    for (s, v) in waits:
                        if v > 0 and seen.get(s, -1) < v:
                            seen[s] = v
                    for s, v in seen.items():
                        eng.wait_ge(sems[s], v)
                    instr = fn(eng)
                    if inc is not None:
                        assert instr is not None, f"op on {name} returned None but has inc {inc}"
                        instr.then_inc(sems[inc[0]], inc[1])
                        prev = (inc[0], iv)
            return run

        block.sync(runner("sync"))
        block.gpsimd(runner("gpsimd"))
        block.scalar(runner("scalar"))
        block.vector(runner("vector"))
        block.tensor(runner("tensor"))

    return nc


_LAST_PG = None
_NC_CACHE = None


def _get_nc():
    global _NC_CACHE
    if _NC_CACHE is None:
        _NC_CACHE = build_program()
    return _NC_CACHE


def kernel(L0_attn_x, L0_attn_norm_w, L0_attn_in_proj_w, L0_attn_conv_w,
           L0_attn_conv_b, L0_attn_x_proj_w, L0_attn_dt_w, L0_attn_dt_b,
           L0_attn_A_log, L0_attn_D_param, L0_attn_out_proj_w,
           L0_ffn_gate_w, L0_ffn_w1, L0_ffn_w2, L0_ffn_w3,
           L0_ffn_ws1, L0_ffn_ws2, L0_ffn_ws3):
    x = np.ascontiguousarray(L0_attn_x)
    identb = np.eye(128, dtype=BF)
    identf = np.eye(128, dtype=np.float32)
    in_maps = []
    for k in range(NCORE):
        b, q = k // 4, k % 4
        t0 = q * TLOC
        if t0 == 0:
            halo = np.zeros((3, D), np.float32)
        else:
            halo = x[b, t0 - 3:t0]
        x_l = np.ascontiguousarray(
            np.concatenate([halo, x[b, t0:t0 + TLOC]], 0).astype(np.float32))
        c0 = q * CLOC
        m = dict(
            x_loc=x_l,
            ip_sl=np.ascontiguousarray(L0_attn_in_proj_w[1024 * k:1024 * (k + 1)]),
            w1_sl=np.ascontiguousarray(L0_ffn_w1[k]),
            w2_sl=np.ascontiguousarray(L0_ffn_w2[k]),
            w3_sl=np.ascontiguousarray(L0_ffn_w3[k]),
            wo_sl=np.ascontiguousarray(L0_attn_out_proj_w[:, 512 * k:512 * (k + 1)]),
            ws1_sl=np.ascontiguousarray(L0_ffn_ws1[128 * k:128 * (k + 1)]),
            ws3_sl=np.ascontiguousarray(L0_ffn_ws3[128 * k:128 * (k + 1)]),
            ws2_sl=np.ascontiguousarray(L0_ffn_ws2[256 * k:256 * (k + 1)]),
            xp_full=np.ascontiguousarray(L0_attn_x_proj_w),
            gw_full=np.ascontiguousarray(L0_ffn_gate_w),
            nw_full=np.ascontiguousarray(L0_attn_norm_w),
            cw_full=np.ascontiguousarray(L0_attn_conv_w.reshape(DI, KC)),
            cb_full=np.ascontiguousarray(L0_attn_conv_b),
            alog_sl=np.ascontiguousarray(L0_attn_A_log[c0:c0 + CLOC]),
            dpar_sl=np.ascontiguousarray(L0_attn_D_param[c0:c0 + CLOC]),
            dtw_sl=np.ascontiguousarray(L0_attn_dt_w.reshape(DI)[c0:c0 + CLOC]),
            dtb_sl=np.ascontiguousarray(L0_attn_dt_b[c0:c0 + CLOC]),
            ident_bf=identb,
            ident_f=identf,
        )
        in_maps.append(m)

    nc = _get_nc()
    res = run_bass_kernel_spmd(nc, in_maps, core_ids=list(range(NCORE)))
    out = np.empty((B, L, D), np.float32)
    for k in range(NCORE):
        b, q = k // 4, k % 4
        out[b, q * TLOC:(q + 1) * TLOC] = res.results[k]["out_loc"]
    return out


# version 08
# speedup vs baseline: 1.0322x; 1.0322x over previous
import numpy as np
import ml_dtypes
import concourse.bass as bass
import concourse.mybir as mybir
from concourse.bass_utils import run_bass_kernel_spmd

F32 = mybir.dt.float32
BF16 = mybir.dt.bfloat16
AF = mybir.ActivationFunctionType
ALU = mybir.AluOpType
AX = mybir.AxisListType.X

NCORE = 8
B, L, D = 2, 16384, 2048
DI = 4096
NST = 16
KC = 4
IN = 1024
TLOC = 4096
CLOC = 1024
EPS = 1e-5
BF = ml_dtypes.bfloat16

SZ = {
    "win": (2048, 1024), "w1": (2048, 1024), "w3": (2048, 1024),
    "w2": (1024, 2048), "wo": (512, 2048), "ws1": (2048, 128),
    "ws3": (2048, 128), "ws2": (1024, 256),
}
OFF = {}
_o = 0
for _k, (_r, _c) in SZ.items():
    OFF[_k] = _o
    _o += _r * _c
CHUNK = _o

ENGS = ("sync", "gpsimd", "scalar", "vector", "tensor")
GROUPS4 = [[0, 1, 2, 3], [4, 5, 6, 7]]
GROUPS8 = [list(range(8))]


class Prog:
    def __init__(self):
        self.seq = {e: [] for e in ENGS}
        self.cnt = {}

    def val(self, sem):
        return self.cnt.get(sem, 0)

    def last(self, sem):
        return (sem, self.val(sem))

    def add(self, eng, fn, wait=None, inc=None):
        if inc is not None:
            self.cnt[inc[0]] = self.cnt.get(inc[0], 0) + inc[1]
            iv = self.cnt[inc[0]]
        else:
            iv = None
        self.seq[eng].append((list(wait or []), fn, inc, iv))


class SbAlloc:
    def __init__(self, nc, base):
        self.nc = nc
        self.off = base

    def t(self, name, shape, dtype, align=64):
        nbytes = int(np.prod(shape[1:])) * (2 if dtype == BF16 else 4)
        self.off = (self.off + align - 1) // align * align
        h = self.nc.alloc_sbuf_tensor_at(name, list(shape), dtype, offset=self.off)
        self.off += nbytes
        return h


def build_program(stop=None, shared_arena=True):
    nc = bass.Bass("TRN2", target_bir_lowering=False, debug=False)

    EIN = lambda n, s, d=F32: nc.dram_tensor(n, s, d, kind="ExternalInput")
    x_loc = EIN("x_loc", [TLOC + 3, D])
    ip_sl = EIN("ip_sl", [1024, 2048])
    w1_sl = EIN("w1_sl", [1024, 2048])
    w2_sl = EIN("w2_sl", [2048, 1024])
    w3_sl = EIN("w3_sl", [1024, 2048])
    wo_sl = EIN("wo_sl", [2048, 512])
    ws1_sl = EIN("ws1_sl", [128, 2048])
    ws3_sl = EIN("ws3_sl", [128, 2048])
    ws2_sl = EIN("ws2_sl", [256, 1024])
    xp_full = EIN("xp_full", [33, DI])
    gw_full = EIN("gw_full", [8, D])
    nw_full = EIN("nw_full", [D])
    cw_full = EIN("cw_full", [DI, KC])
    cb_full = EIN("cb_full", [DI])
    alog_sl = EIN("alog_sl", [CLOC, NST])
    dpar_sl = EIN("dpar_sl", [CLOC])
    dtw_sl = EIN("dtw_sl", [CLOC])
    dtb_sl = EIN("dtb_sl", [CLOC])
    ident_bf = EIN("ident_bf", [128, 128], BF16)
    ident_f = EIN("ident_f", [128, 128], F32)

    out_loc = nc.dram_tensor("out_loc", [TLOC, D], F32, kind="ExternalOutput")

    arena_in = nc.dram_tensor("arena_in", [CHUNK], BF16)
    arena = nc.dram_tensor(
        "arena", [NCORE, CHUNK], BF16,
        addr_space="Shared" if shared_arena else "Local")
    u_in0 = nc.dram_tensor("u_in0", [DI, TLOC // 2], BF16)
    u_in1 = nc.dram_tensor("u_in1", [DI, TLOC // 2], BF16)
    u_ag2 = nc.dram_tensor("u_ag2", [4, DI, TLOC // 2], BF16)
    u_ag3 = nc.dram_tensor("u_ag3", [4, DI, TLOC // 2], BF16)
    xs_in = nc.dram_tensor("xs_in", [33, TLOC], BF16)
    xs_ag = nc.dram_tensor("xs_ag", [4, 33, TLOC], BF16)
    z_scr = nc.dram_tensor("z_scr", [DI, TLOC], BF16)
    y_in = nc.dram_tensor("y_in", [CLOC, L], BF16)
    y_ag = nc.dram_tensor("y_ag", [4, CLOC, L], BF16)
    u_loc = nc.dram_tensor("u_loc", [4, CLOC, TLOC], BF16)
    y_loc = nc.dram_tensor("y_loc", [4, CLOC, TLOC], BF16)
    xn3_scr = nc.dram_tensor("xn3_scr", [D, TLOC], BF16)

    def arena_view(r, key):
        rows, cols = SZ[key]
        return arena.ap()[r, OFF[key]:OFF[key] + rows * cols].rearrange(
            "(r c) -> r c", c=cols)

    SB_BASE = 16512
    ga = SbAlloc(nc, SB_BASE)
    s_idb = ga.t("s_idb", [128, 128], BF16)
    s_idf = ga.t("s_idf", [128, 128], F32)
    s_nw = ga.t("s_nw", [128, 16], F32)
    s_xpTb = ga.t("s_xpTb", [128, 32 * 33], BF16)
    s_gwT = ga.t("s_gwT", [128, 16 * 8], F32)
    s_cw = ga.t("s_cw", [128, 32 * KC], F32)
    s_cb = ga.t("s_cb", [128, 32], F32)
    s_al = ga.t("s_al", [128, 8 * NST], F32)
    s_dtw = ga.t("s_dtw", [128, 8], F32)
    s_dtb = ga.t("s_dtb", [128, 8], F32)
    s_dpr = ga.t("s_dpr", [128, 8], F32)
    s_mv = ga.t("s_mv", [128, 8], F32)
    s_comb = ga.t("s_comb", [128, 32 * 8], F32)
    s_cols = ga.t("s_cols", [128, 32], F32)
    s_kon = ga.t("s_kon", [128, 2], F32)
    s_ones = ga.t("s_ones", [65, 128], BF16)
    GEN_END = ga.off

    p0 = SbAlloc(nc, GEN_END)
    p0_ld = [p0.t(f"p0_ld{i}", [128, 2048], F32) for i in range(2)]
    p0_bf = [p0.t(f"p0_bf{i}", [128, 2048], BF16) for i in range(2)]
    p0_ev = [p0.t(f"p0_ev{i}", [128, 128], BF16) for i in range(2)]
    P0_END = p0.off

    p1a = SbAlloc(nc, P0_END)
    s_xn2T = p1a.t("s_xn2T", [128, 16 * 4099], BF16)
    s_xld = [p1a.t(f"s_xld{i}", [128, 2048], F32) for i in range(2)]
    s_sq = p1a.t("s_sq", [128, 2048], F32)
    s_xn2 = p1a.t("s_xn2", [128, 2048], BF16)
    P1A_END = p1a.off
    _sxld_off = P0_END + ((16 * 4099 * 2 + 63) // 64) * 64
    _ssq_off = _sxld_off + 2 * 8192
    _sxn2_off = _ssq_off + 8192

    p1b = SbAlloc(nc, GEN_END)
    s_win = [p1b.t(f"s_win{i}", [128, 16 * 128], BF16) for i in range(2)]
    s_xin = p1b.t("s_xin", [128, 2052], F32)
    assert p1b.off <= P0_END, (p1b.off, P0_END)
    s_xs = nc.alloc_sbuf_tensor_at("s_xs", [33, 2 * 2048], F32, offset=_sxld_off)
    s_cv = nc.alloc_sbuf_tensor_at("s_cv", [128, 2048], F32, offset=_ssq_off)
    s_u = nc.alloc_sbuf_tensor_at("s_u", [128, 2048], BF16, offset=_sxn2_off)
    p1t_ = SbAlloc(nc, P1A_END)
    s_zbf = p1t_.t("s_zbf", [128, 2048], BF16)
    s_xs_bf = p1t_.t("s_xs_bf", [33, 2 * 2048], BF16)
    P1_END = p1t_.off

    p2 = SbAlloc(nc, GEN_END)
    s_Ball = [p2.t(f"s_Ball{i}", [128, NST * 1024], BF16) for i in range(2)]
    s_Call = [p2.t(f"s_Call{i}", [128, NST * 1024], BF16) for i in range(2)]
    s_rbc = [p2.t(f"s_rbc{i}", [128, 1024], BF16) for i in range(2)]
    s_brow = p2.t("s_brow", [65, 11 * 1024], BF16)
    s_u2 = [p2.t(f"s_u2{i}", [128, 1024], BF16) for i in range(2)]
    s_arg = p2.t("s_arg", [128, 1024], F32)
    s_t2 = p2.t("s_t2", [128, 1024], F32)
    s_dt = p2.t("s_dt", [128, 1024], F32)
    s_w2 = p2.t("s_w2", [128, 1024], BF16)
    s_a = [p2.t(f"s_a{i}", [128, 1024], F32) for i in range(2)]
    s_b = [p2.t(f"s_b{i}", [128, 1024], BF16) for i in range(2)]
    s_h = [p2.t(f"s_h{i}", [128, 1024], BF16) for i in range(2)]
    s_tmp = [p2.t(f"s_tmp{i}", [128, 1024], BF16) for i in range(2)]
    s_st = p2.t("s_st", [128, 8 * NST], F32)
    s_y2 = [p2.t(f"s_y2{i}", [128, 1024], BF16) for i in range(2)]
    P2_END = p2.off

    p3 = SbAlloc(nc, GEN_END)
    s_yg = [p3.t(f"s_yg{i}", [128, 512], BF16) for i in range(32)]
    s_sz = p3.t("s_sz", [128, 512], F32)
    s_zt = [p3.t(f"s_zt{i}", [128, 512], BF16) for i in range(2)]
    s_wo = [p3.t(f"s_wo{i}", [128, 512], BF16) for i in range(32)]
    s_x2 = [p3.t(f"s_x2{i}", [128, 2048], F32) for i in range(4)]
    s_xn3 = p3.t("s_xn3", [128, 2048], F32)
    s_x3f = p3.t("s_x3f", [128, 16 * 128], F32)
    s_x3b = p3.t("s_x3b", [128, 16 * 512], BF16)
    s_sc8 = p3.t("s_sc8", [128, 8], F32)
    s_sc8b = p3.t("s_sc8b", [128, 8], F32)
    s_m8 = p3.t("s_m8", [128, 8], F32)
    s_mr = p3.t("s_mr", [128, 8], F32)
    P3A_END = p3.off
    assert P3A_END <= _sxld_off, (P3A_END, _sxld_off)

    p3b = SbAlloc(nc, GEN_END)
    s_w1e = [p3b.t("s_w1e", [128, 16 * 1024], BF16)]
    s_w3e = p3b.t("s_w3e", [128, 16 * 1024], BF16)
    s_w2e = p3b.t("s_w2e", [128, 8 * 2048], BF16)
    s_xt = [p3b.t(f"s_xt{i}", [128, 16 * 512], BF16) for i in range(2)]
    s_g = [p3b.t(f"s_g{i}", [128, 512], BF16) for i in range(8)]
    s_sh = [p3b.t(f"s_sh{i}", [128, 512], F32) for i in range(2)]
    s_ye = p3b.t("s_ye", [128, 2048], F32)
    P3B_END = p3b.off

    assert max(P1A_END, P1_END, P2_END, P3A_END, P3B_END) <= 229300, (
        P0_END, P1A_END, P1_END, P2_END, P3A_END, P3B_END)

    pA = nc.alloc_psum_tensor("pA", [128, 512], F32)
    pB = nc.alloc_psum_tensor("pB", [128, 512], F32)
    pC = nc.alloc_psum_tensor("pC", [128, 512], F32)
    pD = nc.alloc_psum_tensor("pD", [128, 512], F32)
    pE1 = nc.alloc_psum_tensor("pE1", [128, 1024], F32)
    pE2 = nc.alloc_psum_tensor("pE2", [128, 1024], F32)
    pTR2 = [pC.bitcast(BF16)[:, 0:128], pD.bitcast(BF16)[:, 0:128]]

    pg = Prog()
    global _LAST_PG
    _LAST_PG = pg
    L_ = pg.last

    semnames = [
        "smalls", "p0ld", "p0v", "p0t", "p0st", "cwag",
        "xld0", "xld1", "p1v", "p1s", "p1t", "winld0", "winld1",
        "uzst", "cuag", "cuagB", "cxag",
        "uld0", "uld1", "brow", "bcev",
        "p2v", "p2s", "p2g", "p2t", "yst", "cyag",
        "yld0", "yld1", "zld0", "zld1", "wold0", "wold1",
        "p3v", "p3s", "p3t", "xnst", "xnld0", "xnld1",
        "wexp", "wexpg", "outst", "oacc0", "oacc1", "oacc2", "oacc3",
        "ucp0", "ucp1", "ucp2", "ucp3", "ycp0", "ycp1", "ycp2", "ycp3",
    ]

    _regs = {}

    def qdyn(eng, name):
        if name not in _regs:
            _regs[name] = eng.compute_val(eng.partition_id() % 4)
        return _regs[name]

    for j in range(16):
        pg.add("sync", (lambda j=j: lambda e: e.dma_start(
            out=s_nw[:, j:j + 1],
            in_=nw_full.ap()[128 * j:128 * (j + 1)].unsqueeze(1)))(),
            inc=("smalls", 16))
    pg.add("sync", lambda e: e.dma_start(out=s_idb[:], in_=ident_bf.ap()),
           inc=("smalls", 16))
    pg.add("sync", lambda e: e.dma_start(out=s_idf[:], in_=ident_f.ap()),
           inc=("smalls", 16))
    for ci in range(32):
        pg.add("sync", (lambda ci=ci: lambda e: e.dma_start(
            out=s_cw[:, ci * KC:(ci + 1) * KC],
            in_=cw_full.ap()[128 * ci:128 * (ci + 1), :]))(), inc=("smalls", 16))
        pg.add("sync", (lambda ci=ci: lambda e: e.dma_start(
            out=s_cb[:, ci:ci + 1],
            in_=cb_full.ap()[128 * ci:128 * (ci + 1)].unsqueeze(1)))(),
            inc=("smalls", 16))
    for ct in range(8):
        pg.add("sync", (lambda ct=ct: lambda e: e.dma_start(
            out=s_al[:, ct * NST:(ct + 1) * NST],
            in_=alog_sl.ap()[128 * ct:128 * (ct + 1), :]))(), inc=("smalls", 16))
        for (dst, src) in ((s_dtw, dtw_sl), (s_dtb, dtb_sl), (s_dpr, dpar_sl)):
            pg.add("sync", (lambda ct=ct, dst=dst, src=src: lambda e: e.dma_start(
                out=dst[:, ct:ct + 1],
                in_=src.ap()[128 * ct:128 * (ct + 1)].unsqueeze(1)))(),
                inc=("smalls", 16))
    SM = pg.val("smalls")
    pg.add("vector", lambda e: e.memset(s_kon[:, 0:1], EPS), inc=("p1v", 1))
    pg.add("vector", lambda e: e.memset(s_ones[:], 1.0), inc=("p1v", 1))
    ONES_V = pg.val("p1v")
    pg.add("vector", lambda e: e.memset(s_kon[:, 1:2], -0.5), inc=("p1v", 1))
    pg.add("vector", lambda e: e.memset(s_mv[:], -1e30), inc=("p3v", 1))
    pg.add("scalar", lambda e: e.activation(s_al[:], s_al[:], AF.Exp),
           wait=[("smalls", SM)], inc=("p2s", 1))
    pg.add("vector", lambda e: e.tensor_scalar(s_al[:], s_al[:], -1.0, None,
                                               ALU.mult),
           wait=[L_("p2s")], inc=("p2v", 1))
    ANEG_V = pg.val("p2v")

    p0n = dict(tr=0)
    ev_marks = [0, 0]
    slot_marks = [0, 0]

    def prep_weight(src_ap, key, scale_nw=False):
        rows, cols = SZ[key][1], SZ[key][0]
        rows, cols = src_ap.shape
        for rt in range(rows // 128 if rows >= 128 else 1):
            nr = min(128, rows)
            lb, bb = p0_ld[rt % 2], p0_bf[rt % 2]
            pg.add("sync", (lambda rt=rt, lb=lb, nr=nr, cols=cols: lambda e: e.dma_start(
                out=lb[0:nr, :cols], in_=src_ap[128 * rt:128 * rt + nr, :]))(),
                wait=[L_("p0v")], inc=("p0ld", 16))
            pg.add("vector", (lambda lb=lb, bb=bb, nr=nr, cols=cols: lambda e: e.tensor_copy(
                bb[0:nr, :cols], lb[0:nr, :cols]))(),
                wait=[L_("p0ld")], inc=("p0v", 1))
            castv = pg.val("p0v")
            for jt in range(cols // 128):
                slot = p0n["tr"] % 2
                p0n["tr"] += 1
                w = [("p0v", castv), ("smalls", SM), ("p0v", slot_marks[slot])]
                pg.add("tensor", (lambda bb=bb, jt=jt, slot=slot: lambda e: e.transpose(
                    pTR2[slot], bb[:, jt * 128:(jt + 1) * 128], s_idb[:]))(),
                    wait=w, inc=("p0t", 1))
                ev = p0_ev[p0n["tr"] % 2]
                wv = [L_("p0t"), L_("p0st")]
                if scale_nw:
                    pg.add("vector", (lambda ev=ev, slot=slot, jt=jt: lambda e: e.tensor_scalar(
                        ev[:], pTR2[slot], s_nw[:, jt:jt + 1], None, ALU.mult))(),
                        wait=wv, inc=("p0v", 1))
                else:
                    pg.add("vector", (lambda ev=ev, slot=slot: lambda e: e.tensor_copy(
                        ev[:], pTR2[slot]))(),
                        wait=wv, inc=("p0v", 1))
                slot_marks[slot] = pg.val("p0v")
                if key == "win":
                    dst = arena_in.ap()[
                        OFF[key] + rt * (128 * 2048):
                        OFF[key] + (rt + 1) * (128 * 2048)].rearrange(
                        "(p t j) -> p t j", p=128, t=16)[:, jt, :]
                else:
                    dst = arena_in.ap()[OFF[key]:OFF[key] + cols * rows].rearrange(
                        "(c r) -> c r", r=rows)[jt * 128:(jt + 1) * 128,
                                                128 * rt:128 * rt + nr]
                pg.add("sync", (lambda dst=dst, ev=ev, nr=nr: lambda e: e.dma_start(
                    out=dst, in_=ev[:, 0:nr]))(),
                    wait=[L_("p0v")], inc=("p0st", 16))
                ev_marks[p0n["tr"] % 2] = pg.val("p0st")

    pg.cuts = {}
    pg.cuts["sm"] = {e: len(pg.seq[e]) for e in ENGS}
    prep_weight(ip_sl.ap(), "win", scale_nw=True)
    pg.cuts["w_ip"] = {e: len(pg.seq[e]) for e in ENGS}
    prep_weight(w1_sl.ap(), "w1")
    prep_weight(w3_sl.ap(), "w3")
    prep_weight(w2_sl.ap(), "w2")
    prep_weight(wo_sl.ap(), "wo")
    prep_weight(ws1_sl.ap(), "ws1")
    prep_weight(ws3_sl.ap(), "ws3")
    prep_weight(ws2_sl.ap(), "ws2")

    pg.cuts["p0w"] = {e: len(pg.seq[e]) for e in ENGS}
    pg.add("gpsimd", lambda e: e.collective_compute(
        "AllGather", ALU.bypass, replica_groups=GROUPS8,
        ins=[arena_in.ap().opt()], outs=[arena.ap().opt()]),
        wait=[L_("p0st")], inc=("cwag", 1))
    pg.cuts["p0ag"] = {e: len(pg.seq[e]) for e in ENGS}

    pg.add("sync", lambda e: e.dma_start(out=p0_ld[0][0:8, :], in_=gw_full.ap()),
           wait=[L_("p0v")], inc=("p0ld", 16))
    gldv = pg.val("p0ld")
    for jt in range(16):
        pg.add("tensor", (lambda jt=jt: lambda e: e.transpose(
            pE2[:, 0:128], p0_ld[0][:, jt * 128:(jt + 1) * 128], s_idf[:]))(),
            wait=[("p0ld", gldv), ("smalls", SM), L_("p0v")], inc=("p0t", 1))
        pg.add("vector", (lambda jt=jt: lambda e: e.tensor_copy(
            s_gwT[:, jt * 8:(jt + 1) * 8], pE2[:, 0:8]))(),
            wait=[L_("p0t")], inc=("p0v", 1))
    for half in range(2):
        pg.add("sync", (lambda half=half: lambda e: e.dma_start(
            out=p0_ld[1][0:33, :],
            in_=xp_full.ap()[:, half * 2048:(half + 1) * 2048]))(),
            wait=[L_("p0v")], inc=("p0ld", 16))
        xldv = pg.val("p0ld")
        for jt in range(16):
            ci = half * 16 + jt
            pg.add("tensor", (lambda jt=jt: lambda e: e.transpose(
                pE2[:, 0:128], p0_ld[1][:, jt * 128:(jt + 1) * 128], s_idf[:]))(),
                wait=[("p0ld", xldv), L_("p0v")], inc=("p0t", 1))
            pg.add("vector", (lambda ci=ci: lambda e: e.tensor_copy(
                s_xpTb[:, ci * 33:(ci + 1) * 33], pE2[:, 0:33]))(),
                wait=[L_("p0t")], inc=("p0v", 1))
    P0V_DONE = pg.val("p0v")
    P0T_DONE = pg.val("p0t")

    pg.cuts["p0"] = {e: len(pg.seq[e]) for e in ENGS}

    pg.add("vector", lambda e: e.memset(s_xld[0][:], 0.0), inc=("p1v", 1))
    pg.add("vector", lambda e: e.memset(s_xld[1][:], 0.0), inc=("p1v", 1))
    XLD_INIT = pg.val("p1v")

    xmark = {}
    sqmark = 0
    p1slot = [0, 0]

    for ti in range(33):
        r0 = 0 if ti == 0 else 3 + 128 * (ti - 1)
        nr = 3 if ti == 0 else 128
        xb = s_xld[ti % 2]
        w = []
        if ti >= 2:
            w.append(("p1v", xmark[ti - 2]))
        xsem = f"xld{ti % 2}"
        if ti < 2:
            w.append(("p1v", XLD_INIT))
        pg.add("sync", (lambda xb=xb, r0=r0, nr=nr: lambda e: e.dma_start(
            out=xb[0:nr, :], in_=x_loc.ap()[r0:r0 + nr, :]))(),
            wait=w, inc=(xsem, 16))
        xv = pg.val(xsem)
        pg.add("vector", (lambda xb=xb: lambda e: e.reduce_sum(
            s_cols[:, 0:1], xb[:], axis=AX))(),
            wait=[(xsem, xv)], inc=("p1v", 1))
        pg.add("scalar", (lambda xb=xb: lambda e: e.activation(
            s_sq[:], xb[:], AF.Square))(),
            wait=[(xsem, xv), ("p1v", sqmark)], inc=("p1s", 1))
        pg.add("vector", lambda e: e.reduce_sum(s_cols[:, 1:2], s_sq[:], axis=AX),
               wait=[L_("p1s")], inc=("p1v", 1))
        sqmark = pg.val("p1v")

        pg.add("vector", lambda e: e.tensor_scalar(s_cols[:, 2:3], s_cols[:, 0:1], 1.0 / D, None, ALU.mult), inc=("p1v", 1))
        pg.add("vector", lambda e: e.tensor_scalar(s_cols[:, 3:4], s_cols[:, 1:2], 1.0 / D, None, ALU.mult), inc=("p1v", 1))
        pg.add("vector", lambda e: e.tensor_tensor(s_cols[:, 4:5], s_cols[:, 2:3], s_cols[:, 2:3], ALU.mult), inc=("p1v", 1))
        pg.add("vector", lambda e: e.tensor_tensor(s_cols[:, 3:4], s_cols[:, 3:4], s_cols[:, 4:5], ALU.subtract), inc=("p1v", 1))
        pg.add("scalar", lambda e: e.activation(s_cols[:, 5:6], s_cols[:, 3:4],
                                                AF.Ln, bias=s_kon[:, 0:1]),
               wait=[L_("p1v")], inc=("p1s", 1))
        pg.add("scalar", lambda e: e.activation(s_cols[:, 6:7], s_cols[:, 5:6],
                                                AF.Exp, scale=s_kon[:, 1:2]), inc=("p1s", 1))
        pg.add("scalar", lambda e: e.activation(s_cols[:, 7:8], s_cols[:, 5:6],
                                                AF.Exp, scale=-1.0), inc=("p1s", 1))
        pg.add("vector", lambda e: e.tensor_tensor(s_cols[:, 8:9], s_cols[:, 3:4],
                                                   s_cols[:, 7:8], ALU.mult),
               wait=[L_("p1s")], inc=("p1v", 1))
        pg.add("scalar", lambda e: e.activation(s_cols[:, 9:10], s_cols[:, 8:9],
                                                AF.Ln, bias=s_kon[:, 0:1]),
               wait=[L_("p1v")], inc=("p1s", 1))
        pg.add("scalar", lambda e: e.activation(s_cols[:, 10:11], s_cols[:, 9:10],
                                                AF.Exp, scale=s_kon[:, 1:2]), inc=("p1s", 1))

        pg.add("vector", lambda e: e.tensor_tensor(s_cols[:, 11:12], s_cols[:, 6:7], s_cols[:, 10:11], ALU.mult), wait=[L_("p1s")], inc=("p1v", 1))
        pg.add("vector", lambda e: e.tensor_tensor(s_cols[:, 12:13], s_cols[:, 2:3], s_cols[:, 11:12], ALU.mult), inc=("p1v", 1))
        pg.add("vector", lambda e: e.tensor_scalar(s_cols[:, 12:13], s_cols[:, 12:13], -1.0, None, ALU.mult), inc=("p1v", 1))
        pg.add("vector", (lambda xb=xb: lambda e: e.scalar_tensor_tensor(
            s_xn2[:], xb[:], s_cols[:, 11:12],
            s_cols[:, 12:13].to_broadcast([128, 2048]),
            ALU.mult, ALU.add))(),
            wait=[L_("p1v")], inc=("p1v", 1))
        xmark[ti] = pg.val("p1v")
        xnv = pg.val("p1v")
        c0c = 0 if ti == 0 else 3 + 128 * (ti - 1)
        ncc = 3 if ti == 0 else 128
        for dt in range(16):
            sl = (dt + 16 * ti) % 2
            pg.add("tensor", (lambda dt=dt, sl=sl: lambda e: e.transpose(
                pTR2[sl], s_xn2[:, dt * 128:(dt + 1) * 128], s_idb[:]))(),
                wait=[("p1v", xnv), ("p1v", p1slot[sl]), ("smalls", SM),
                      ("p0v", P0V_DONE), ("p0t", P0T_DONE)],
                inc=("p1t", 1))
            pg.add("vector", (lambda dt=dt, sl=sl, c0c=c0c, ncc=ncc: lambda e: e.tensor_copy(
                s_xn2T[:, dt * 4099 + c0c:dt * 4099 + c0c + ncc],
                pTR2[sl][:, 0:ncc]))(),
                wait=[L_("p1t")], inc=("p1v", 1))
            p1slot[sl] = pg.val("p1v")
    LNT_V = pg.val("p1v")
    pg.cuts["p1a"] = {e: len(pg.seq[e]) for e in ENGS}

    pg.add("vector", lambda e: e.memset(s_xs[:], 0.0),
           wait=[("p1v", LNT_V)], inc=("p1v", 1))
    XSMEM_V = pg.val("p1v")
    win_marks = [0, 0]
    psum_marks = [0, 0, 0, 0]
    xin_mark = 0
    zbf_mark = 0
    u_mark = 0
    u_st_mark = 0
    mmctr = 0
    UZ_SB0 = None
    for sb in range(2):
        for ci in range(64):
            is_z = ci >= 32
            wb = s_win[ci % 2]
            rk = ci // 8
            w0 = [("cwag", 1), ("p1t", win_marks[ci % 2]),
                  ("p0v", P0V_DONE), ("p0t", P0T_DONE)]
            wsem = f"winld{ci % 2}"
            cb = ci % 8
            wv_t = arena.ap()[rk, OFF["win"] + cb * (128 * 2048):
                              OFF["win"] + (cb + 1) * (128 * 2048)].rearrange(
                "(p f) -> p f", p=128)
            pg.add("sync", (lambda wb=wb, wv_t=wv_t: lambda e: e.dma_start(
                out=wb[:], in_=wv_t))(),
                wait=w0, inc=(wsem, 16))
            wldv = pg.val(wsem)
            chunks = ([(0, 512), (512, 512), (1024, 512), (1536, 512), (2048, 3)]
                      if not is_z else
                      [(3, 512), (515, 512), (1027, 512), (1539, 512)])
            for (co, cw_) in chunks:
                pidx = mmctr % 4
                ps = [pA, pB, pE2[:, 0:512], pE2[:, 512:1024]][pidx]
                mmctr += 1
                base = sb * 2048
                for dt in range(16):
                    w = []
                    if dt == 0:
                        w = [(wsem, wldv), ("p1v", LNT_V),
                             ("p1v", psum_marks[pidx])]
                    pg.add("tensor", (lambda ps=ps, wb=wb, dt=dt, base=base, co=co, cw_=cw_: lambda e: e.matmul(
                        ps[:, 0:cw_], wb[:, dt * 128:(dt + 1) * 128],
                        s_xn2T[:, dt * 4099 + base + co:dt * 4099 + base + co + cw_],
                        start=(dt == 0), stop=(dt == 15)))(),
                        wait=w, inc=("p1t", 1) if dt == 15 else None)
                mmd = pg.val("p1t")
                if not is_z:
                    pg.add("vector", (lambda ps=ps, co=co, cw_=cw_: lambda e: e.tensor_copy(
                        s_xin[:, co:co + cw_], ps[:, 0:cw_]))(),
                        wait=[("p1t", mmd), ("p1s", xin_mark)], inc=("p1v", 1))
                else:
                    pg.add("vector", (lambda ps=ps, co=co, cw_=cw_: lambda e: e.tensor_copy(
                        s_zbf[:, co - 3:co - 3 + cw_], ps[:, 0:cw_]))(),
                        wait=[("p1t", mmd), L_("uzst")], inc=("p1v", 1))
                psum_marks[pidx] = pg.val("p1v")
            win_marks[ci % 2] = pg.val("p1t")
            evd = pg.val("p1v")
            pg.cuts[f"p1b_{sb}_{ci}_mm"] = {e: len(pg.seq[e]) for e in ENGS}
            if is_z:
                zi = ci - 32
                pg.add("sync", (lambda zi=zi, sb=sb: lambda e: e.dma_start(
                    out=z_scr.ap()[zi * 128:(zi + 1) * 128,
                                   sb * 2048:(sb + 1) * 2048],
                    in_=s_zbf[:]))(),
                    wait=[("p1v", evd)], inc=("uzst", 16))
                zbf_mark = pg.val("uzst")
            else:
                pg.add("vector", (lambda ci=ci: lambda e: e.tensor_scalar(
                    s_cv[:], s_xin[:, 0:2048], s_cw[:, ci * KC:ci * KC + 1],
                    None, ALU.mult))(),
                    wait=[("p1v", evd), ("p1s", xin_mark)], inc=("p1v", 1))
                for k in range(1, 4):
                    pg.add("vector", (lambda ci=ci, k=k: lambda e: e.scalar_tensor_tensor(
                        s_cv[:], s_xin[:, k:k + 2048],
                        s_cw[:, ci * KC + k:ci * KC + k + 1],
                        s_cv[:], ALU.mult, ALU.add))(), inc=("p1v", 1))
                cvd = pg.val("p1v")
                pg.add("scalar", (lambda ci=ci: lambda e: e.activation(
                    s_u[:], s_cv[:], AF.Silu, bias=s_cb[:, ci:ci + 1]))(),
                    wait=[("p1v", cvd), ("p1t", u_mark), L_("uzst")],
                    inc=("p1s", 1))
                xin_mark = pg.val("p1s")
                uv = pg.val("p1s")
                u_dst = u_in0 if sb == 0 else u_in1
                pg.add("sync", (lambda ci=ci, u_dst=u_dst: lambda e: e.dma_start(
                    out=u_dst.ap()[ci * 128:(ci + 1) * 128, :],
                    in_=s_u[:]))(),
                    wait=[("p1s", uv)], inc=("uzst", 16))
                u_st_mark = pg.val("uzst")
                for q in range(4):
                    w = [("p1s", uv), ("p0v", P0V_DONE)]
                    if q > 0:
                        w = [L_("p1v")]
                    pg.add("tensor", (lambda ci=ci, q=q: lambda e: e.matmul(
                        pE1[0:33, 0:512], s_xpTb[:, ci * 33:(ci + 1) * 33],
                        s_u[:, q * 512:(q + 1) * 512], start=True, stop=True))(),
                        wait=w, inc=("p1t", 1))
                    pg.add("vector", (lambda sb=sb, q=q: lambda e: e.tensor_tensor(
                        s_xs[:, sb * 2048 + q * 512:sb * 2048 + (q + 1) * 512],
                        s_xs[:, sb * 2048 + q * 512:sb * 2048 + (q + 1) * 512],
                        pE1[0:33, 0:512], ALU.add))(),
                        wait=[L_("p1t"), ("p1v", XSMEM_V)], inc=("p1v", 1))
                u_mark = pg.val("p1t")
            pg.cuts[f"p1b_{sb}_{ci}"] = {e: len(pg.seq[e]) for e in ENGS}
        if sb == 0:
            UZ_SB0 = pg.val("uzst")

    pg.add("vector", lambda e: e.tensor_copy(s_xs_bf[:], s_xs[:]),
           wait=[L_("p1v")], inc=("p1v", 1))
    pg.add("sync", lambda e: e.dma_start(out=xs_in.ap(), in_=s_xs_bf[:]),
           wait=[L_("p1v")], inc=("uzst", 16))
    UZ = pg.val("uzst")

    pg.add("gpsimd", lambda e: e.collective_compute(
        "AllGather", ALU.bypass, replica_groups=GROUPS4,
        ins=[u_in0.ap().opt()], outs=[u_ag2.ap().opt()]),
        wait=[("uzst", UZ_SB0)], inc=("cuag", 1))
    pg.add("gpsimd", lambda e: e.collective_compute(
        "AllGather", ALU.bypass, replica_groups=GROUPS4,
        ins=[u_in1.ap().opt()], outs=[u_ag3.ap().opt()]),
        wait=[("uzst", UZ)], inc=("cuagB", 1))
    pg.add("gpsimd", lambda e: e.collective_compute(
        "AllGather", ALU.bypass, replica_groups=GROUPS4,
        ins=[xs_in.ap().opt()], outs=[xs_ag.ap().opt()]),
        wait=[("uzst", UZ)], inc=("cxag", 1))

    pg.cuts["p1"] = {e: len(pg.seq[e]) for e in ENGS}

    pg.add("vector", lambda e: e.memset(s_st[:], 0.0),
           wait=[("cxag", 1)], inc=("p2v", 1))
    STMEM_V = pg.val("p2v")

    xs_v = xs_ag.ap()
    u_v2 = u_ag2.ap().rearrange("g (q c) t -> g q c t", q=4, c=CLOC)
    u_v3 = u_ag3.ap().rearrange("g (q c) t -> g q c t", q=4, c=CLOC)
    for rb in range(4):
        pg.add("gpsimd", (lambda rb=rb: lambda e: e.dma_start(
            out=u_loc.ap()[rb][:, 0:2048],
            in_=u_v2[rb][bass.ts(qdyn(e, "gpsimd"), 1)].squeeze(0)))(),
            wait=[("cuag", 1)], inc=(f"ucp{rb}", 16))
    for rb in range(4):
        pg.add("gpsimd", (lambda rb=rb: lambda e: e.dma_start(
            out=u_loc.ap()[rb][:, 2048:4096],
            in_=u_v3[rb][bass.ts(qdyn(e, "gpsimd"), 1)].squeeze(0)))(),
            wait=[("cuagB", 1)], inc=(f"ucp{rb}", 16))

    st_mark = {}
    bc_marks = [0, 0]
    bcg_marks = [0, 0]
    u2_marks = [0, 0]
    pe_marks = [0, 0]
    y2_marks = [0, 0]
    abht_marks = {"a": [0, 0], "b": [0, 0], "h": [0, 0], "tmp": [0, 0]}

    bc_ps = [pA, pB, pC, pD]
    bc_ps_vm = [0, 0, 0, 0]
    bc_ps_sm = [0, 0, 0, 0]
    browm = 0
    for tt in range(16):
        bb = tt % 2
        r_blk, toff = (tt * 1024) // TLOC, (tt * 1024) % TLOC
        w = [("cxag", 1), ("p2t", browm)]
        for m3 in range(3):
            nrows = 11 if m3 != 2 else 11
            pg.add("sync", (lambda m3=m3, r_blk=r_blk, toff=toff: lambda e: e.dma_start(
                out=s_brow[32 * m3:32 * m3 + 1, 0:11 * 1024].rearrange(
                    "p (r t) -> p r t", t=1024),
                in_=xs_v[r_blk, m3:33:3, toff:toff + 1024].unsqueeze(0)))(),
                wait=w if m3 == 0 else [], inc=("brow", 16))
        brv = pg.val("brow")
        nev_v = 0
        nev_s = 0
        for row in range(33):
            m3, slot = row % 3, row // 3
            if row < NST:
                dsts = [(s_Ball[bb], row)]
            elif row < 2 * NST:
                dsts = [(s_Call[bb], row - NST)]
            else:
                dsts = [(s_rbc[bb], 0)]
            for q in range(2):
                pi = (tt * 66 + row * 2 + q) % 4
                ps = bc_ps[pi]
                pg.add("tensor", (lambda m3=m3, slot=slot, q=q, ps=ps: lambda e: e.matmul(
                    ps[:, 0:512], s_ones[32 * m3:32 * m3 + 1, :],
                    s_brow[32 * m3:32 * m3 + 1,
                           slot * 1024 + q * 512:slot * 1024 + (q + 1) * 512],
                    start=True, stop=True))(),
                    wait=[("brow", brv), ("p1v", ONES_V),
                          ("p2v", bc_ps_vm[pi]), ("p2s", bc_ps_sm[pi])],
                    inc=("p2t", 1))
                mmv = pg.val("p2t")
                dst, drow = dsts[0]
                if row < NST:
                    pg.add("vector", (lambda dst=dst, drow=drow, q=q, ps=ps: lambda e: e.tensor_copy(
                        dst[:, drow * 1024 + q * 512:drow * 1024 + (q + 1) * 512],
                        ps[:, 0:512]))(),
                        wait=[("p2t", mmv), ("p2v", bc_marks[bb] if row == 0 and q == 0 else 0)],
                        inc=("p2v", 1))
                    bc_ps_vm[pi] = pg.val("p2v")
                    nev_v += 1
                else:
                    pg.add("scalar", (lambda dst=dst, drow=drow, q=q, ps=ps: lambda e: e.copy(
                        dst[:, drow * 1024 + q * 512:drow * 1024 + (q + 1) * 512],
                        ps[:, 0:512]))(),
                        wait=[("p2t", mmv), ("p2s", bcg_marks[bb] if row == NST and q == 0 else 0)],
                        inc=("p2s", 1))
                    bc_ps_sm[pi] = pg.val("p2s")
                    nev_s += 1
        browm = pg.val("p2t")
        bBv, bCv = pg.val("p2v"), pg.val("p2s")
        pg.add("vector", lambda e: e.nop(),
               wait=[("p2s", bCv)], inc=("p2v", 1))
        pg.add("gpsimd", lambda e: e.nop(),
               wait=[("p2v", bBv), ("p2s", bCv)], inc=("p2g", 1))
        for ct in range(8):
            ub = s_u2[ct % 2]
            usem = f"uld{ct % 2}"
            pg.add("sync", (lambda ub=ub, r_blk=r_blk, toff=toff, ct=ct: lambda e: e.dma_start(
                out=ub[:],
                in_=u_loc.ap()[r_blk][128 * ct:128 * (ct + 1),
                                      toff:toff + 1024]))(),
                wait=[("p2v", u2_marks[ct % 2]), (f"ucp{r_blk}", 32)],
                inc=(usem, 16))
            uldv = pg.val(usem)
            pg.add("vector", (lambda bb=bb, ct=ct: lambda e: e.scalar_tensor_tensor(
                s_arg[:], s_rbc[bb][:], s_dtw[:, ct:ct + 1],
                s_dtb[:, ct:ct + 1].to_broadcast([128, 1024]),
                ALU.mult, ALU.add))(),
                wait=[L_("p2s"), ("smalls", SM)],
                inc=("p2v", 1))
            pg.add("scalar", lambda e: e.nop(),
                   wait=[L_("p2v")], inc=("p2s", 1))
            pg.add("scalar", lambda e: e.activation(s_t2[:], s_arg[:], AF.Abs),
                   inc=("p2s", 1))
            pg.add("scalar", lambda e: e.activation(s_t2[:], s_t2[:], AF.Exp,
                                                    scale=-1.0), inc=("p2s", 1))
            pg.add("scalar", lambda e: e.activation(s_t2[:], s_t2[:], AF.Ln,
                                                    bias=1.0), inc=("p2s", 1))
            dtsv = pg.val("p2s")
            pg.add("vector", lambda e: e.tensor_scalar(s_dt[:], s_arg[:], 0.0,
                                                       None, ALU.max),
                   wait=[("p2s", dtsv)], inc=("p2v", 1))
            pg.add("vector", lambda e: e.tensor_tensor(s_dt[:], s_dt[:], s_t2[:],
                                                       ALU.add), inc=("p2v", 1))
            pg.add("vector", (lambda ub=ub: lambda e: e.tensor_tensor(
                s_w2[:], s_dt[:], ub[:], ALU.mult))(),
                wait=[(usem, uldv), L_("p2g")], inc=("p2v", 1))
            w2v = pg.val("p2v")
            u2_marks[ct % 2] = w2v
            pidx = ct % 2
            ps = pE1 if pidx == 0 else pE2
            for n in range(NST):
                nb = n % 2
                ab, bbuf, hb, tb_ = s_a[nb], s_b[nb], s_h[nb], s_tmp[nb]
                b_on_g = n < 11
                tmp_on_g = n < 11
                pg.add("scalar", (lambda ab=ab, ct=ct, n=n: lambda e: e.activation(
                    ab[:], s_dt[:], AF.Exp,
                    scale=s_al[:, ct * NST + n:ct * NST + n + 1]))(),
                    wait=[("p2v", max(w2v, abht_marks["a"][nb]))],
                    inc=("p2s", 1))
                if b_on_g:
                    pg.add("gpsimd", (lambda bbuf=bbuf, bb=bb, n=n: lambda e: e.tensor_tensor(
                        bbuf[:], s_w2[:], s_Ball[bb][:, n * 1024:(n + 1) * 1024],
                        ALU.mult))(),
                        wait=[("p2v", max(w2v, abht_marks["b"][nb]))],
                        inc=("p2g", 1))
                else:
                    pg.add("vector", (lambda bbuf=bbuf, bb=bb, n=n: lambda e: e.tensor_tensor(
                        bbuf[:], s_w2[:], s_Ball[bb][:, n * 1024:(n + 1) * 1024],
                        ALU.mult))(),
                        wait=[("p2v", max(w2v, abht_marks["b"][nb]))],
                        inc=("p2v", 1))
                wscan = [L_("p2s"), ("p2t", abht_marks["h"][nb]),
                         ("p2v", st_mark.get((ct, n), 0))]
                if b_on_g:
                    wscan.append(L_("p2g"))
                pg.add("vector", (lambda ab=ab, bbuf=bbuf, hb=hb, ct=ct, n=n: lambda e: e.tensor_tensor_scan(
                    hb[:], ab[:], bbuf[:],
                    s_st[:, ct * NST + n:ct * NST + n + 1], ALU.mult, ALU.add))(),
                    wait=wscan,
                    inc=("p2v", 1))
                scan_v = pg.val("p2v")
                abht_marks["a"][nb] = scan_v
                abht_marks["b"][nb] = scan_v
                pg.add("vector", (lambda hb=hb, ct=ct, n=n: lambda e: e.tensor_copy(
                    s_st[:, ct * NST + n:ct * NST + n + 1], hb[:, 1023:1024]))(),
                    inc=("p2v", 1))
                st_mark[(ct, n)] = pg.val("p2v")
                if tmp_on_g:
                    pg.add("gpsimd", (lambda hb=hb, tb_=tb_, bb=bb, n=n: lambda e: e.tensor_tensor(
                        tb_[:], hb[:], s_Call[bb][:, n * 1024:(n + 1) * 1024],
                        ALU.mult))(),
                        wait=[("p2v", scan_v), ("p2t", abht_marks["tmp"][nb])],
                        inc=("p2g", 1))
                    tmpw = [L_("p2g")]
                else:
                    pg.add("vector", (lambda hb=hb, tb_=tb_, bb=bb, n=n: lambda e: e.tensor_tensor(
                        tb_[:], hb[:], s_Call[bb][:, n * 1024:(n + 1) * 1024],
                        ALU.mult))(),
                        wait=[("p2t", abht_marks["tmp"][nb])],
                        inc=("p2v", 1))
                    tmpw = [L_("p2v")]
                for q in range(2):
                    w = list(tmpw)
                    if n == 0 and q == 0:
                        w.append(("p2v", pe_marks[pidx]))
                    pg.add("tensor", (lambda ps=ps, tb_=tb_, q=q, n=n: lambda e: e.matmul(
                        ps[:, q * 512:(q + 1) * 512], s_idb[:],
                        tb_[:, q * 512:(q + 1) * 512],
                        start=(n == 0), stop=(n == NST - 1)))(),
                        wait=w, inc=("p2t", 1))
                abht_marks["tmp"][nb] = pg.val("p2t")
                abht_marks["h"][nb] = pg.val("p2t")
            yb = s_y2[ct % 2]
            pg.add("vector", (lambda yb=yb, ub=ub, ps=ps, ct=ct: lambda e: e.scalar_tensor_tensor(
                yb[:], ub[:], s_dpr[:, ct:ct + 1], ps[:], ALU.mult, ALU.add))(),
                wait=[L_("p2t"), L_("yst")], inc=("p2v", 1))
            pe_marks[pidx] = pg.val("p2v")
            u2_marks[ct % 2] = pg.val("p2v")
            pg.add("scalar", (lambda yb=yb, ct=ct, tt=tt: lambda e: e.dma_start(
                out=y_in.ap()[ct * 128:(ct + 1) * 128,
                              tt * 1024:(tt + 1) * 1024], in_=yb[:]))(),
                wait=[L_("p2v")], inc=("yst", 16))
            y2_marks[ct % 2] = pg.val("yst")
        bc_marks[bb] = pg.val("p2v")
        bcg_marks[bb] = pg.val("p2g")

    YST = pg.val("yst")
    pg.add("gpsimd", lambda e: e.collective_compute(
        "AllGather", ALU.bypass, replica_groups=GROUPS4,
        ins=[y_in.ap().opt()], outs=[y_ag.ap().opt()]),
        wait=[("yst", YST)], inc=("cyag", 1))

    pg.cuts["p2"] = {e: len(pg.seq[e]) for e in ENGS}

    y_v = y_ag.ap().rearrange("g c (q t) -> g c q t", q=4, t=TLOC)
    for rb in range(4):
        pg.add("gpsimd", (lambda rb=rb: lambda e: e.dma_start(
            out=y_loc.ap()[rb], in_=y_v[rb][:, bass.ts(qdyn(e, "gpsimd"), 1)]
            .squeeze(1)))(),
            wait=[("cyag", 1)], inc=(f"ycp{rb}", 16))
    outst_total = 0
    xnst_total = 0
    x2_marks = {}
    yg_mark = 0
    wo_mark = 0
    zt_marks = [0, 0]
    xb_vmark = [0, 0]
    xb_smark = [0, 0]
    x2v_mark = 0
    x3f_mark = 0
    x3b_mark = 0
    sq3_mark = 0
    trslot_mark = [0, 0]

    for tb in range(8):
        ysem = f"yld{tb % 2}"
        for ci in range(32):
            rr, cc = ci // 8, (ci % 8) * 128
            w = []
            if ci % 8 == 0:
                w = [(f"ycp{rr}", 16)]
                if ci == 0:
                    w.append(("p3t", yg_mark))
            pg.add("sync", (lambda ci=ci, rr=rr, cc=cc, tb=tb: lambda e: e.dma_start(
                out=s_yg[ci][:],
                in_=y_loc.ap()[rr][cc:cc + 128, tb * 512:(tb + 1) * 512]))(),
                wait=w, inc=(ysem, 16))
        yldv = pg.val(ysem)
        for ci in range(32):
            zb = s_zt[ci % 2]
            zsem = f"zld{ci % 2}"
            pg.add("scalar", (lambda ci=ci, zb=zb, tb=tb: lambda e: e.dma_start(
                out=zb[:], in_=z_scr.ap()[ci * 128:(ci + 1) * 128,
                                          tb * 512:(tb + 1) * 512]))(),
                wait=[("p3v", zt_marks[ci % 2])], inc=(zsem, 16))
            pg.add("scalar", (lambda zb=zb: lambda e: e.activation(
                s_sz[:], zb[:], AF.Silu))(),
                wait=[L_(zsem), L_("p3v")], inc=("p3s", 1))
            pg.add("vector", (lambda ci=ci: lambda e: e.tensor_tensor(
                s_yg[ci][:], s_yg[ci][:], s_sz[:], ALU.mult))(),
                wait=[L_("p3s"), (ysem, yldv)], inc=("p3v", 1))
            zt_marks[ci % 2] = pg.val("p3v")
        ygv = pg.val("p3v")
        for tc in range(4):
            xb = s_xld[tc % 2]
            xsem = f"xld{tc % 2}"
            if tb == 0 and tc < 2:
                w = [("cyag", 1)]
            else:
                w = [("p3v", xb_vmark[tc % 2]), ("p3s", xb_smark[tc % 2])]
            pg.add("sync", (lambda xb=xb, tb=tb, tc=tc: lambda e: e.dma_start(
                out=xb[:], in_=x_loc.ap()[3 + tb * 512 + tc * 128:
                                          3 + tb * 512 + (tc + 1) * 128, :]))(),
                wait=w, inc=(xsem, 16))
            xv = pg.val(xsem)
            pg.add("vector", (lambda xb=xb: lambda e: e.reduce_sum(
                s_cols[:, 0:1], xb[:], axis=AX))(),
                wait=[(xsem, xv)], inc=("p3v", 1))
            pg.add("scalar", (lambda xb=xb: lambda e: e.activation(
                s_sq[:], xb[:], AF.Square))(),
                wait=[(xsem, xv), ("p3v", sq3_mark)], inc=("p3s", 1))
            xb_smark[tc % 2] = pg.val("p3s")
            pg.add("vector", lambda e: e.reduce_sum(s_cols[:, 1:2], s_sq[:], axis=AX),
                   wait=[L_("p3s")], inc=("p3v", 1))
            sq3_mark = pg.val("p3v")

            pg.add("vector", lambda e: e.tensor_scalar(s_cols[:, 2:3], s_cols[:, 0:1], 1.0 / D, None, ALU.mult), inc=("p3v", 1))
            pg.add("vector", lambda e: e.tensor_scalar(s_cols[:, 3:4], s_cols[:, 1:2], 1.0 / D, None, ALU.mult), inc=("p3v", 1))
            pg.add("vector", lambda e: e.tensor_tensor(s_cols[:, 4:5], s_cols[:, 2:3], s_cols[:, 2:3], ALU.mult), inc=("p3v", 1))
            pg.add("vector", lambda e: e.tensor_tensor(s_cols[:, 3:4], s_cols[:, 3:4], s_cols[:, 4:5], ALU.subtract), inc=("p3v", 1))
            pg.add("scalar", lambda e: e.activation(s_cols[:, 5:6], s_cols[:, 3:4],
                                                    AF.Ln, bias=s_kon[:, 0:1]),
                   wait=[L_("p3v")], inc=("p3s", 1))
            pg.add("scalar", lambda e: e.activation(s_cols[:, 6:7], s_cols[:, 5:6],
                                                    AF.Exp, scale=s_kon[:, 1:2]),
                   inc=("p3s", 1))

            pg.add("vector", lambda e: e.tensor_scalar(s_cols[:, 7:8], s_cols[:, 6:7], 1.0, None, ALU.add), wait=[L_("p3s")], inc=("p3v", 1))
            pg.add("vector", lambda e: e.tensor_tensor(s_cols[:, 8:9], s_cols[:, 2:3], s_cols[:, 6:7], ALU.mult), inc=("p3v", 1))
            pg.add("vector", lambda e: e.tensor_scalar(s_cols[:, 8:9], s_cols[:, 8:9], -1.0, None, ALU.mult), inc=("p3v", 1))
            pg.add("vector", (lambda xb=xb, tc=tc: lambda e: e.scalar_tensor_tensor(
                s_x2[tc][:], xb[:], s_cols[:, 7:8],
                s_cols[:, 8:9].to_broadcast([128, 2048]),
                ALU.mult, ALU.add))(),
                wait=[L_("p3v"), L_("outst")], inc=("p3v", 1))
            xb_vmark[tc % 2] = pg.val("p3v")
        resv = pg.val("p3v")
        for dc in range(4):
            w = [("cwag", 1), ("p3t", wo_mark)]
            osem2 = f"wold{dc % 2}"
            for ci in range(32):
                rr = (ci * 128) // 512
                ro = (ci * 128) % 512
                wv_ap = arena_view(rr, "wo")
                pg.add("sync", (lambda ci=ci, wv_ap=wv_ap, ro=ro, dc=dc: lambda e: e.dma_start(
                    out=s_wo[ci][:],
                    in_=wv_ap[ro:ro + 128, dc * 512:(dc + 1) * 512]))(),
                    wait=w if ci == 0 else [], inc=(osem2, 16))
            woldv = pg.val(osem2)
            for tc in range(4):
                pidx = (dc * 4 + tc) % 4
                ps = [pA, pB, pE2[:, 0:512], pE2[:, 512:1024]][pidx]
                for ci in range(32):
                    w = []
                    if ci == 0:
                        w = [(osem2, woldv), ("p3v", ygv), ("p3v", resv),
                             L_("p3v")]
                    pg.add("tensor", (lambda ps=ps, ci=ci, tc=tc: lambda e: e.matmul(
                        ps[:], s_yg[ci][:, tc * 128:(tc + 1) * 128],
                        s_wo[ci][:], start=(ci == 0), stop=(ci == 31)))(),
                        wait=w, inc=("p3t", 1) if ci == 31 else None)
                pg.add("vector", (lambda ps=ps, tc=tc, dc=dc: lambda e: e.tensor_tensor(
                    s_x2[tc][:, dc * 512:(dc + 1) * 512],
                    s_x2[tc][:, dc * 512:(dc + 1) * 512], ps[:], ALU.add))(),
                    wait=[L_("p3t")], inc=("p3v", 1))
            wo_mark = pg.val("p3t")
        yg_mark = pg.val("p3t")
        x2v = pg.val("p3v")
        for tc in range(4):
            pg.add("sync", (lambda tc=tc, tb=tb: lambda e: e.dma_start(
                out=out_loc.ap()[tb * 512 + tc * 128:tb * 512 + (tc + 1) * 128, :],
                in_=s_x2[tc][:]))(),
                wait=[("p3v", x2v)], inc=("outst", 16))
            outst_total += 16
            x2_marks[(tb, tc)] = outst_total
            pg.add("vector", (lambda tc=tc: lambda e: e.reduce_sum(
                s_cols[:, 0:1], s_x2[tc][:], axis=AX))(),
                wait=[("p3v", x2v)], inc=("p3v", 1))
            pg.add("scalar", (lambda tc=tc: lambda e: e.activation(
                s_sq[:], s_x2[tc][:], AF.Square))(),
                wait=[("p3v", x2v), ("p3v", sq3_mark)], inc=("p3s", 1))
            pg.add("vector", lambda e: e.reduce_sum(s_cols[:, 1:2], s_sq[:], axis=AX),
                   wait=[L_("p3s")], inc=("p3v", 1))
            sq3_mark = pg.val("p3v")

            pg.add("vector", lambda e: e.tensor_scalar(s_cols[:, 2:3], s_cols[:, 0:1], 1.0 / D, None, ALU.mult), inc=("p3v", 1))
            pg.add("vector", lambda e: e.tensor_scalar(s_cols[:, 3:4], s_cols[:, 1:2], 1.0 / D, None, ALU.mult), inc=("p3v", 1))
            pg.add("vector", lambda e: e.tensor_tensor(s_cols[:, 4:5], s_cols[:, 2:3], s_cols[:, 2:3], ALU.mult), inc=("p3v", 1))
            pg.add("vector", lambda e: e.tensor_tensor(s_cols[:, 3:4], s_cols[:, 3:4], s_cols[:, 4:5], ALU.subtract), inc=("p3v", 1))
            pg.add("scalar", lambda e: e.activation(s_cols[:, 5:6], s_cols[:, 3:4],
                                                    AF.Ln, bias=s_kon[:, 0:1]),
                   wait=[L_("p3v")], inc=("p3s", 1))
            pg.add("scalar", lambda e: e.activation(s_cols[:, 6:7], s_cols[:, 5:6],
                                                    AF.Exp, scale=s_kon[:, 1:2]),
                   inc=("p3s", 1))

            pg.add("vector", lambda e: e.tensor_tensor(s_cols[:, 8:9], s_cols[:, 2:3], s_cols[:, 6:7], ALU.mult), wait=[L_("p3s")], inc=("p3v", 1))
            pg.add("vector", lambda e: e.tensor_scalar(s_cols[:, 8:9], s_cols[:, 8:9], -1.0, None, ALU.mult), inc=("p3v", 1))
            pg.add("vector", (lambda tc=tc: lambda e: e.scalar_tensor_tensor(
                s_xn3[:], s_x2[tc][:], s_cols[:, 6:7],
                s_cols[:, 8:9].to_broadcast([128, 2048]),
                ALU.mult, ALU.add))(),
                wait=[L_("p3v")], inc=("p3v", 1))
            x2v_mark = pg.val("p3v")
            xn3v = pg.val("p3v")
            for dt in range(16):
                sl = dt % 2
                ptr_ = pC if sl == 0 else pD
                pg.add("tensor", (lambda dt=dt, ptr_=ptr_: lambda e: e.transpose(
                    ptr_[:, 0:128],
                    s_xn3[:, dt * 128:(dt + 1) * 128], s_idf[:]))(),
                    wait=[("p3v", xn3v), ("p3v", trslot_mark[sl]),
                          ("p3t", x3f_mark if dt == 0 else 0)],
                    inc=("p3t", 1))
                pg.add("vector", (lambda dt=dt, ptr_=ptr_: lambda e: e.tensor_copy(
                    s_x3f[:, dt * 128:(dt + 1) * 128],
                    ptr_[:, 0:128]))(),
                    wait=[L_("p3t")], inc=("p3v", 1))
                trslot_mark[sl] = pg.val("p3v")
                pg.add("vector", (lambda dt=dt, tc=tc: lambda e: e.tensor_copy(
                    s_x3b[:, dt * 512 + tc * 128:dt * 512 + (tc + 1) * 128],
                    s_x3f[:, dt * 128:(dt + 1) * 128]))(),
                    wait=[("xnst", x3b_mark)] if dt == 0 else [],
                    inc=("p3v", 1))
            trv = pg.val("p3v")
            for dt in range(16):
                w = [("p3v", trv), ("p0v", P0V_DONE)] if dt == 0 else []
                pg.add("tensor", (lambda dt=dt: lambda e: e.matmul(
                    pE1[:, 0:8], s_x3f[:, dt * 128:(dt + 1) * 128],
                    s_gwT[:, dt * 8:(dt + 1) * 8],
                    start=(dt == 0), stop=(dt == 15)))(),
                    wait=w, inc=("p3t", 1) if dt == 15 else None)
            x3f_mark = pg.val("p3t")
            gmv = pg.val("p3t")
            pg.add("vector", lambda e: e.reduce_max(s_cols[:, 16:17], pE1[:, 0:8],
                                                    axis=AX),
                   wait=[("p3t", gmv)], inc=("p3v", 1))
            pg.add("vector", lambda e: e.tensor_scalar(
                s_cols[:, 17:18], s_cols[:, 16:17], -1.0, None, ALU.mult),
                inc=("p3v", 1))
            pg.add("scalar", lambda e: e.activation(s_sc8[:], pE1[:, 0:8], AF.Exp,
                                                    bias=s_cols[:, 17:18]),
                   wait=[L_("p3v")], inc=("p3s", 1))
            pg.add("vector", lambda e: e.reduce_sum(s_cols[:, 18:19], s_sc8[:],
                                                    axis=AX),
                   wait=[L_("p3s")], inc=("p3v", 1))
            pg.add("vector", lambda e: e.reciprocal(s_cols[:, 19:20],
                                                    s_cols[:, 18:19]),
                   inc=("p3v", 1))
            pg.add("vector", lambda e: e.tensor_scalar(
                s_sc8[:], s_sc8[:], s_cols[:, 19:20], None, ALU.mult),
                inc=("p3v", 1))
            pg.add("vector", lambda e: e.max(s_m8[:], s_sc8[:]), inc=("p3v", 1))
            pg.add("vector", lambda e: e.tensor_copy(s_mv[:, 0:2], s_m8[:, 0:2]),
                   inc=("p3v", 1))
            pg.add("vector", lambda e: e.match_replace(s_mr[:], s_mv[:], s_sc8[:],
                                                       -1e30),
                   wait=[L_("p3v")], inc=("p3v", 1))
            pg.add("vector", lambda e: e.tensor_tensor(s_sc8b[:], s_sc8[:],
                                                       s_mr[:], ALU.not_equal),
                   inc=("p3v", 1))
            pg.add("vector", (lambda tb=tb, tc=tc: lambda e: e.tensor_tensor(
                s_comb[:, (tb * 4 + tc) * 8:(tb * 4 + tc + 1) * 8],
                s_sc8b[:], s_sc8[:], ALU.mult))(), inc=("p3v", 1))
        x3v = pg.val("p3v")
        for dt in range(16):
            pg.add("sync", (lambda dt=dt, tb=tb: lambda e: e.dma_start(
                out=xn3_scr.ap()[dt * 128:(dt + 1) * 128,
                                 tb * 512:(tb + 1) * 512],
                in_=s_x3b[:, dt * 512:(dt + 1) * 512]))(),
                wait=[("p3v", x3v)] if dt == 0 else [], inc=("xnst", 16))
            xnst_total += 16
        x3b_mark = xnst_total

    P3A_V = pg.val("p3v")
    P3A_T = pg.val("p3t")
    P3A_S = pg.val("p3s")

    pg.cuts["p3a"] = {e: len(pg.seq[e]) for e in ENGS}

    oacc_tot = {0: 0, 1: 0, 2: 0, 3: 0}
    last_acc = None
    g_mark = 0
    xt_marks = [0, 0]
    pe3_marks = [0, 0]
    pab_marks = {}
    sh_marks = [0, 0]
    wbuf_marks = [0]

    for e_idx in range(9):
        w1buf = s_w1e[0]
        w = [("p3v", P3A_V), ("p3t", P3A_T), ("p3s", P3A_S),
             ("outst", outst_total), ("xnst", xnst_total),
             ("p3t", wbuf_marks[0])]
        first = [set()]

        def wl(fn, q="sync"):
            ww = list(w) if q not in first[0] else []
            first[0].add(q)
            pg.add(q, fn, wait=ww, inc=("wexpg" if q == "gpsimd" else "wexp", 16))
        if e_idx < 8:
            v1 = arena_view(e_idx, "w1")
            v3 = arena_view(e_idx, "w3")
            v2 = arena_view(e_idx, "w2")
            for dt in range(16):
                wl((lambda dt=dt, v1=v1, w1buf=w1buf: lambda e: e.dma_start(
                    out=w1buf[:, dt * 1024:(dt + 1) * 1024],
                    in_=v1[dt * 128:(dt + 1) * 128, :]))())
            for dt in range(16):
                wl((lambda dt=dt, v3=v3: lambda e: e.dma_start(
                    out=s_w3e[:, dt * 1024:(dt + 1) * 1024],
                    in_=v3[dt * 128:(dt + 1) * 128, :]))(), q="scalar")
            for it in range(8):
                wl((lambda it=it, v2=v2: lambda e: e.dma_start(
                    out=s_w2e[:, it * 2048:(it + 1) * 2048],
                    in_=v2[it * 128:(it + 1) * 128, :]))(), q="gpsimd")
        else:
            for dt in range(16):
                for rr in range(8):
                    v = arena_view(rr, "ws1")
                    wl((lambda dt=dt, rr=rr, v=v, w1buf=w1buf: lambda e: e.dma_start(
                        out=w1buf[:, dt * 1024 + rr * 128:dt * 1024 + (rr + 1) * 128],
                        in_=v[dt * 128:(dt + 1) * 128, :]))())
            for dt in range(16):
                for rr in range(8):
                    v = arena_view(rr, "ws3")
                    wl((lambda dt=dt, rr=rr, v=v: lambda e: e.dma_start(
                        out=s_w3e[:, dt * 1024 + rr * 128:dt * 1024 + (rr + 1) * 128],
                        in_=v[dt * 128:(dt + 1) * 128, :]))(), q="scalar")
            for it in range(8):
                for rr in range(8):
                    v = arena_view(rr, "ws2")
                    wl((lambda it=it, rr=rr, v=v: lambda e: e.dma_start(
                        out=s_w2e[:, it * 2048 + rr * 256:it * 2048 + (rr + 1) * 256],
                        in_=v[it * 128:(it + 1) * 128, :]))(), q="gpsimd")
        wv = pg.val("wexp")
        wvg = pg.val("wexpg")
        for tb in range(8):
            xt = s_xt[tb % 2]
            xnsem = f"xnld{tb % 2}"
            w = [("xnst", xnst_total), ("p3t", xt_marks[tb % 2])]
            for dt in range(16):
                pg.add("sync", (lambda dt=dt, tb=tb, xt=xt: lambda e: e.dma_start(
                    out=xt[:, dt * 512:(dt + 1) * 512],
                    in_=xn3_scr.ap()[dt * 128:(dt + 1) * 128,
                                     tb * 512:(tb + 1) * 512]))(),
                    wait=w if dt == 0 else [], inc=(xnsem, 16))
            xnv = pg.val(xnsem)
            for it in range(8):
                ph = pA if it % 2 == 0 else pC
                pq = pB if it % 2 == 0 else pD
                hk = f"a{it % 2}"
                qk = f"b{it % 2}"
                for dt in range(16):
                    w = []
                    if dt == 0:
                        w = [("wexp", wv), (xnsem, xnv),
                             ("p3s", pab_marks.get(hk, 0))]
                    pg.add("tensor", (lambda w1buf=w1buf, xt=xt, it=it, dt=dt, ph=ph: lambda e: e.matmul(
                        ph[:], w1buf[:, dt * 1024 + it * 128:dt * 1024 + (it + 1) * 128],
                        xt[:, dt * 512:(dt + 1) * 512],
                        start=(dt == 0), stop=(dt == 15)))(),
                        wait=w, inc=("p3t", 1) if dt == 15 else None)
                for dt in range(16):
                    w = [("p3v", pab_marks.get(qk, 0))] if dt == 0 else []
                    pg.add("tensor", (lambda xt=xt, it=it, dt=dt, pq=pq: lambda e: e.matmul(
                        pq[:], s_w3e[:, dt * 1024 + it * 128:dt * 1024 + (it + 1) * 128],
                        xt[:, dt * 512:(dt + 1) * 512],
                        start=(dt == 0), stop=(dt == 15)))(),
                        wait=w, inc=("p3t", 1) if dt == 15 else None)
                hgv = pg.val("p3t")
                shb = s_sh[it % 2]
                pg.add("scalar", (lambda ph=ph, shb=shb: lambda e: e.activation(
                    shb[:], ph[:, 0:512], AF.Silu))(),
                       wait=[("p3t", hgv), ("p3v", sh_marks[it % 2])],
                       inc=("p3s", 1))
                pab_marks[hk] = pg.val("p3s")
                pg.add("vector", (lambda it=it, pq=pq, shb=shb: lambda e: e.tensor_tensor(
                    s_g[it][:], shb[:], pq[:, 0:512], ALU.mult))(),
                    wait=[L_("p3s"), ("p3t", g_mark if it == 0 else 0)],
                    inc=("p3v", 1))
                pab_marks[qk] = pg.val("p3v")
                sh_marks[it % 2] = pg.val("p3v")
            gv = pg.val("p3v")
            for tc in range(4):
                for dh in range(2):
                    ps = pE1 if dh == 0 else pE2
                    for q in range(2):
                        for it in range(8):
                            w = []
                            if it == 0:
                                w = [("p3v", gv), ("p3v", pe3_marks[dh]),
                                     ("wexpg", wvg)]
                            pg.add("tensor", (lambda ps=ps, it=it, tc=tc, dh=dh, q=q: lambda e: e.matmul(
                                ps[:, q * 512:(q + 1) * 512],
                                s_g[it][:, tc * 128:(tc + 1) * 128],
                                s_w2e[:, it * 2048 + dh * 1024 + q * 512:
                                      it * 2048 + dh * 1024 + (q + 1) * 512],
                                start=(it == 0), stop=(it == 7)))(),
                                wait=w, inc=("p3t", 1) if it == 7 else None)
                eov = pg.val("p3t")
                ye_war = [last_acc] if last_acc else []
                for dh in range(2):
                    ps = pE1 if dh == 0 else pE2
                    if e_idx < 8:
                        cc_ = s_comb[:, (tb * 4 + tc) * 8 + e_idx:
                                     (tb * 4 + tc) * 8 + e_idx + 1]
                        pg.add("vector", (lambda ps=ps, dh=dh, cc_=cc_: lambda e: e.tensor_scalar(
                            s_ye[:, dh * 1024:(dh + 1) * 1024], ps[:], cc_,
                            None, ALU.mult))(),
                            wait=([("p3t", eov)] + ye_war if dh == 0 else []),
                            inc=("p3v", 1))
                    else:
                        pg.add("vector", (lambda ps=ps, dh=dh: lambda e: e.tensor_copy(
                            s_ye[:, dh * 1024:(dh + 1) * 1024], ps[:]))(),
                            wait=([("p3t", eov)] + ye_war if dh == 0 else []),
                            inc=("p3v", 1))
                pe3_marks[0] = pg.val("p3v")
                pe3_marks[1] = pg.val("p3v")
                yev = pg.val("p3v")
                osem = f"oacc{tc}"
                pg.add("gpsimd", (lambda tb=tb, tc=tc: lambda e: e.dma_start(
                    out=out_loc.ap()[tb * 512 + tc * 128:
                                     tb * 512 + (tc + 1) * 128, :],
                    in_=s_ye[:], accum_op=ALU.add))(),
                    wait=[("p3v", yev), ("outst", outst_total),
                          (osem, oacc_tot[tc])],
                    inc=(osem, 16))
                oacc_tot[tc] += 16
                last_acc = (osem, oacc_tot[tc])
            g_mark = pg.val("p3t")
            xt_marks[tb % 2] = pg.val("p3t")
        wbuf_marks[0] = pg.val("p3t")

    final_oacc = [(f"oacc{tc}", oacc_tot[tc]) for tc in range(4)]
    pg.add("sync", lambda e: e.nop(),
           wait=[("outst", outst_total)] + final_oacc)
    pg.add("gpsimd", lambda e: e.nop(), wait=final_oacc)
    pg.add("vector", lambda e: e.nop(), wait=final_oacc)

    if stop is not None:
        for e in ENGS:
            pg.seq[e] = pg.seq[e][:pg.cuts[stop][e]]

    from contextlib import ExitStack
    with ExitStack() as ctx:
        sems = {n: ctx.enter_context(nc.semaphore(n)) for n in semnames}
        block = ctx.enter_context(nc.Block())

        CHAINED = ("vector", "scalar", "gpsimd")

        def runner(name):
            def run(eng):
                prev = None
                for waits, fn, inc, iv in pg.seq[name]:
                    seen = {}
                    if name in CHAINED and prev is not None:
                        seen[prev[0]] = prev[1]
                    for (s, v) in waits:
                        if v > 0 and seen.get(s, -1) < v:
                            seen[s] = v
                    for s, v in seen.items():
                        eng.wait_ge(sems[s], v)
                    instr = fn(eng)
                    if inc is not None:
                        assert instr is not None, f"op on {name} returned None but has inc {inc}"
                        instr.then_inc(sems[inc[0]], inc[1])
                        prev = (inc[0], iv)
            return run

        block.sync(runner("sync"))
        block.gpsimd(runner("gpsimd"))
        block.scalar(runner("scalar"))
        block.vector(runner("vector"))
        block.tensor(runner("tensor"))

    return nc


_LAST_PG = None
_NC_CACHE = None


def _get_nc():
    global _NC_CACHE
    if _NC_CACHE is None:
        _NC_CACHE = build_program()
    return _NC_CACHE


def kernel(L0_attn_x, L0_attn_norm_w, L0_attn_in_proj_w, L0_attn_conv_w,
           L0_attn_conv_b, L0_attn_x_proj_w, L0_attn_dt_w, L0_attn_dt_b,
           L0_attn_A_log, L0_attn_D_param, L0_attn_out_proj_w,
           L0_ffn_gate_w, L0_ffn_w1, L0_ffn_w2, L0_ffn_w3,
           L0_ffn_ws1, L0_ffn_ws2, L0_ffn_ws3):
    x = np.ascontiguousarray(L0_attn_x)
    identb = np.eye(128, dtype=BF)
    identf = np.eye(128, dtype=np.float32)
    in_maps = []
    for k in range(NCORE):
        b, q = k // 4, k % 4
        t0 = q * TLOC
        if t0 == 0:
            halo = np.zeros((3, D), np.float32)
        else:
            halo = x[b, t0 - 3:t0]
        x_l = np.ascontiguousarray(
            np.concatenate([halo, x[b, t0:t0 + TLOC]], 0).astype(np.float32))
        c0 = q * CLOC
        m = dict(
            x_loc=x_l,
            ip_sl=np.ascontiguousarray(L0_attn_in_proj_w[1024 * k:1024 * (k + 1)]),
            w1_sl=np.ascontiguousarray(L0_ffn_w1[k]),
            w2_sl=np.ascontiguousarray(L0_ffn_w2[k]),
            w3_sl=np.ascontiguousarray(L0_ffn_w3[k]),
            wo_sl=np.ascontiguousarray(L0_attn_out_proj_w[:, 512 * k:512 * (k + 1)]),
            ws1_sl=np.ascontiguousarray(L0_ffn_ws1[128 * k:128 * (k + 1)]),
            ws3_sl=np.ascontiguousarray(L0_ffn_ws3[128 * k:128 * (k + 1)]),
            ws2_sl=np.ascontiguousarray(L0_ffn_ws2[256 * k:256 * (k + 1)]),
            xp_full=np.ascontiguousarray(L0_attn_x_proj_w),
            gw_full=np.ascontiguousarray(L0_ffn_gate_w),
            nw_full=np.ascontiguousarray(L0_attn_norm_w),
            cw_full=np.ascontiguousarray(L0_attn_conv_w.reshape(DI, KC)),
            cb_full=np.ascontiguousarray(L0_attn_conv_b),
            alog_sl=np.ascontiguousarray(L0_attn_A_log[c0:c0 + CLOC]),
            dpar_sl=np.ascontiguousarray(L0_attn_D_param[c0:c0 + CLOC]),
            dtw_sl=np.ascontiguousarray(L0_attn_dt_w.reshape(DI)[c0:c0 + CLOC]),
            dtb_sl=np.ascontiguousarray(L0_attn_dt_b[c0:c0 + CLOC]),
            ident_bf=identb,
            ident_f=identf,
        )
        in_maps.append(m)

    nc = _get_nc()
    res = run_bass_kernel_spmd(nc, in_maps, core_ids=list(range(NCORE)))
    out = np.empty((B, L, D), np.float32)
    for k in range(NCORE):
        b, q = k // 4, k % 4
        out[b, q * TLOC:(q + 1) * TLOC] = res.results[k]["out_loc"]
    return out


# version 09
# speedup vs baseline: 1.0487x; 1.0159x over previous
import numpy as np
import ml_dtypes
import concourse.bass as bass
import concourse.mybir as mybir
from concourse.bass_utils import run_bass_kernel_spmd

F32 = mybir.dt.float32
BF16 = mybir.dt.bfloat16
AF = mybir.ActivationFunctionType
ALU = mybir.AluOpType
AX = mybir.AxisListType.X

NCORE = 8
B, L, D = 2, 16384, 2048
DI = 4096
NST = 16
KC = 4
IN = 1024
TLOC = 4096
CLOC = 1024
EPS = 1e-5
BF = ml_dtypes.bfloat16

SZ = {
    "win": (2048, 1024), "w1": (2048, 1024), "w3": (2048, 1024),
    "w2": (1024, 2048), "wo": (512, 2048), "ws1": (2048, 128),
    "ws3": (2048, 128), "ws2": (1024, 256),
}
OFF = {}
_o = 0
for _k, (_r, _c) in SZ.items():
    OFF[_k] = _o
    _o += _r * _c
CHUNK = _o

ENGS = ("sync", "gpsimd", "scalar", "vector", "tensor")
GROUPS4 = [[0, 1, 2, 3], [4, 5, 6, 7]]
GROUPS8 = [list(range(8))]


class Prog:
    def __init__(self):
        self.seq = {e: [] for e in ENGS}
        self.cnt = {}

    def val(self, sem):
        return self.cnt.get(sem, 0)

    def last(self, sem):
        return (sem, self.val(sem))

    def add(self, eng, fn, wait=None, inc=None):
        if inc is not None:
            self.cnt[inc[0]] = self.cnt.get(inc[0], 0) + inc[1]
            iv = self.cnt[inc[0]]
        else:
            iv = None
        self.seq[eng].append((list(wait or []), fn, inc, iv))


class SbAlloc:
    def __init__(self, nc, base):
        self.nc = nc
        self.off = base

    def t(self, name, shape, dtype, align=64):
        nbytes = int(np.prod(shape[1:])) * (2 if dtype == BF16 else 4)
        self.off = (self.off + align - 1) // align * align
        h = self.nc.alloc_sbuf_tensor_at(name, list(shape), dtype, offset=self.off)
        self.off += nbytes
        return h


def build_program(stop=None, shared_arena=True):
    nc = bass.Bass("TRN2", target_bir_lowering=False, debug=False)

    EIN = lambda n, s, d=F32: nc.dram_tensor(n, s, d, kind="ExternalInput")
    x_loc = EIN("x_loc", [TLOC + 3, D])
    ip_sl = EIN("ip_sl", [1024, 2048])
    w1_sl = EIN("w1_sl", [1024, 2048])
    w2_sl = EIN("w2_sl", [2048, 1024])
    w3_sl = EIN("w3_sl", [1024, 2048])
    wo_sl = EIN("wo_sl", [2048, 512])
    ws1_sl = EIN("ws1_sl", [128, 2048])
    ws3_sl = EIN("ws3_sl", [128, 2048])
    ws2_sl = EIN("ws2_sl", [256, 1024])
    xp_full = EIN("xp_full", [33, DI])
    gw_full = EIN("gw_full", [8, D])
    nw_full = EIN("nw_full", [D])
    cw_full = EIN("cw_full", [DI, KC])
    cb_full = EIN("cb_full", [DI])
    alog_sl = EIN("alog_sl", [CLOC, NST])
    dpar_sl = EIN("dpar_sl", [CLOC])
    dtw_sl = EIN("dtw_sl", [CLOC])
    dtb_sl = EIN("dtb_sl", [CLOC])
    ident_bf = EIN("ident_bf", [128, 128], BF16)
    ident_f = EIN("ident_f", [128, 128], F32)

    out_loc = nc.dram_tensor("out_loc", [TLOC, D], F32, kind="ExternalOutput")

    arena_in = nc.dram_tensor("arena_in", [CHUNK], BF16)
    arena = nc.dram_tensor(
        "arena", [NCORE, CHUNK], BF16,
        addr_space="Shared" if shared_arena else "Local")
    u_in0 = nc.dram_tensor("u_in0", [DI, TLOC // 2], BF16)
    u_in1 = nc.dram_tensor("u_in1", [DI, TLOC // 2], BF16)
    u_ag2 = nc.dram_tensor("u_ag2", [4, DI, TLOC // 2], BF16)
    u_ag3 = nc.dram_tensor("u_ag3", [4, DI, TLOC // 2], BF16)
    xs_in = nc.dram_tensor("xs_in", [33, TLOC], BF16)
    xs_ag = nc.dram_tensor("xs_ag", [4, 33, TLOC], BF16)
    z_scr = nc.dram_tensor("z_scr", [DI, TLOC], BF16)
    y_in = nc.dram_tensor("y_in", [CLOC, L], BF16)
    y_ag = nc.dram_tensor("y_ag", [4, CLOC, L], BF16)
    u_loc = nc.dram_tensor("u_loc", [4, CLOC, TLOC], BF16)
    y_loc = nc.dram_tensor("y_loc", [4, CLOC, TLOC], BF16)
    xn3_scr = nc.dram_tensor("xn3_scr", [D, TLOC], BF16)

    def arena_view(r, key):
        rows, cols = SZ[key]
        return arena.ap()[r, OFF[key]:OFF[key] + rows * cols].rearrange(
            "(r c) -> r c", c=cols)

    SB_BASE = 16512
    ga = SbAlloc(nc, SB_BASE)
    s_idb = ga.t("s_idb", [128, 128], BF16)
    s_idf = ga.t("s_idf", [128, 128], F32)
    s_nw = ga.t("s_nw", [128, 16], F32)
    s_xpTb = ga.t("s_xpTb", [128, 32 * 33], BF16)
    s_gwT = ga.t("s_gwT", [128, 16 * 8], F32)
    s_cw = ga.t("s_cw", [128, 32 * KC], F32)
    s_cb = ga.t("s_cb", [128, 32], F32)
    s_al = ga.t("s_al", [128, 8 * NST], F32)
    s_dtw = ga.t("s_dtw", [128, 8], F32)
    s_dtb = ga.t("s_dtb", [128, 8], F32)
    s_dpr = ga.t("s_dpr", [128, 8], F32)
    s_mv = ga.t("s_mv", [128, 8], F32)
    s_comb = ga.t("s_comb", [128, 32 * 8], F32)
    s_cols = ga.t("s_cols", [128, 32], F32)
    s_kon = ga.t("s_kon", [128, 2], F32)
    s_ones = ga.t("s_ones", [65, 128], BF16)
    GEN_END = ga.off

    p0 = SbAlloc(nc, GEN_END)
    p0_ld = [p0.t(f"p0_ld{i}", [128, 2048], F32) for i in range(2)]
    p0_bf = [p0.t(f"p0_bf{i}", [128, 2048], BF16) for i in range(2)]
    p0_ev = [p0.t(f"p0_ev{i}", [128, 128], BF16) for i in range(2)]
    P0_END = p0.off

    p1a = SbAlloc(nc, P0_END)
    s_xn2T = p1a.t("s_xn2T", [128, 16 * 4099], BF16)
    s_xld = [p1a.t(f"s_xld{i}", [128, 2048], F32) for i in range(2)]
    s_sq = p1a.t("s_sq", [128, 2048], F32)
    s_xn2 = p1a.t("s_xn2", [128, 2048], BF16)
    P1A_END = p1a.off
    _sxld_off = P0_END + ((16 * 4099 * 2 + 63) // 64) * 64
    _ssq_off = _sxld_off + 2 * 8192
    _sxn2_off = _ssq_off + 8192

    p1b = SbAlloc(nc, GEN_END)
    s_win = [p1b.t(f"s_win{i}", [128, 16 * 128], BF16) for i in range(2)]
    s_xin = p1b.t("s_xin", [128, 2052], F32)
    assert p1b.off <= P0_END, (p1b.off, P0_END)
    s_xs = nc.alloc_sbuf_tensor_at("s_xs", [33, 2 * 2048], F32, offset=_sxld_off)
    s_cv = nc.alloc_sbuf_tensor_at("s_cv", [128, 2048], F32, offset=_ssq_off)
    s_u = nc.alloc_sbuf_tensor_at("s_u", [128, 2048], BF16, offset=_sxn2_off)
    p1t_ = SbAlloc(nc, P1A_END)
    s_zbf = p1t_.t("s_zbf", [128, 2048], BF16)
    s_xs_bf = p1t_.t("s_xs_bf", [33, 2 * 2048], BF16)
    P1_END = p1t_.off

    p2 = SbAlloc(nc, GEN_END)
    s_Ball = [p2.t(f"s_Ball{i}", [128, NST * 1024], BF16) for i in range(2)]
    s_Call = [p2.t(f"s_Call{i}", [128, NST * 1024], BF16) for i in range(2)]
    s_rbc = [p2.t(f"s_rbc{i}", [128, 1024], BF16) for i in range(2)]
    s_brow = p2.t("s_brow", [65, 11 * 1024], BF16)
    s_u2 = [p2.t(f"s_u2{i}", [128, 1024], BF16) for i in range(2)]
    s_arg = p2.t("s_arg", [128, 1024], F32)
    s_t2 = p2.t("s_t2", [128, 1024], F32)
    s_dt = p2.t("s_dt", [128, 1024], F32)
    s_w2 = p2.t("s_w2", [128, 1024], BF16)
    s_a = [p2.t(f"s_a{i}", [128, 1024], F32) for i in range(2)]
    s_b = [p2.t(f"s_b{i}", [128, 1024], BF16) for i in range(2)]
    s_h = [p2.t(f"s_h{i}", [128, 1024], BF16) for i in range(2)]
    s_tmp = [p2.t(f"s_tmp{i}", [128, 1024], BF16) for i in range(2)]
    s_st = p2.t("s_st", [128, 8 * NST], F32)
    s_y2 = [p2.t(f"s_y2{i}", [128, 1024], BF16) for i in range(2)]
    P2_END = p2.off

    p3 = SbAlloc(nc, GEN_END)
    s_yg = [p3.t(f"s_yg{i}", [128, 512], BF16) for i in range(32)]
    s_sz = p3.t("s_sz", [128, 512], F32)
    s_zt = [p3.t(f"s_zt{i}", [128, 512], BF16) for i in range(2)]
    s_wo = [p3.t(f"s_wo{i}", [128, 512], BF16) for i in range(32)]
    s_x2 = [p3.t(f"s_x2{i}", [128, 2048], F32) for i in range(4)]
    s_xn3 = p3.t("s_xn3", [128, 2048], F32)
    s_x3f = p3.t("s_x3f", [128, 16 * 128], F32)
    s_x3b = p3.t("s_x3b", [128, 16 * 512], BF16)
    s_sc8 = p3.t("s_sc8", [128, 8], F32)
    s_sc8b = p3.t("s_sc8b", [128, 8], F32)
    s_m8 = p3.t("s_m8", [128, 8], F32)
    s_mr = p3.t("s_mr", [128, 8], F32)
    P3A_END = p3.off
    assert P3A_END <= _sxld_off, (P3A_END, _sxld_off)

    p3b = SbAlloc(nc, GEN_END)
    s_w1e = [p3b.t("s_w1e", [128, 16 * 1024], BF16)]
    s_w3e = p3b.t("s_w3e", [128, 16 * 1024], BF16)
    s_w2e = p3b.t("s_w2e", [128, 8 * 2048], BF16)
    s_xt = [p3b.t(f"s_xt{i}", [128, 16 * 512], BF16) for i in range(2)]
    s_g = [p3b.t(f"s_g{i}", [128, 512], BF16) for i in range(8)]
    s_sh = [p3b.t(f"s_sh{i}", [128, 512], F32) for i in range(2)]
    s_ye = p3b.t("s_ye", [128, 2048], F32)
    P3B_END = p3b.off

    assert max(P1A_END, P1_END, P2_END, P3A_END, P3B_END) <= 229300, (
        P0_END, P1A_END, P1_END, P2_END, P3A_END, P3B_END)

    pA = nc.alloc_psum_tensor("pA", [128, 512], F32)
    pB = nc.alloc_psum_tensor("pB", [128, 512], F32)
    pC = nc.alloc_psum_tensor("pC", [128, 512], F32)
    pD = nc.alloc_psum_tensor("pD", [128, 512], F32)
    pE1 = nc.alloc_psum_tensor("pE1", [128, 1024], F32)
    pE2 = nc.alloc_psum_tensor("pE2", [128, 1024], F32)
    pTR2 = [pC.bitcast(BF16)[:, 0:128], pD.bitcast(BF16)[:, 0:128]]

    pg = Prog()
    global _LAST_PG
    _LAST_PG = pg
    L_ = pg.last

    semnames = [
        "smalls", "p0ld", "p0v", "p0t", "p0st", "cwag",
        "xld0", "xld1", "p1v", "p1s", "p1t", "winld0", "winld1",
        "uzst", "cuag", "cuagB", "cxag",
        "uld0", "uld1", "brow", "bcev",
        "p2v", "p2s", "p2g", "p2t", "yst", "cyag", "cyagB",
        "yld0", "yld1", "zld0", "zld1", "wold0", "wold1",
        "p3v", "p3s", "p3t", "xnst", "xnld0", "xnld1",
        "wexp", "wexpg", "outst", "oacc0", "oacc1", "oacc2", "oacc3",
        "ucp0", "ucp1", "ucp2", "ucp3", "ycp0", "ycp1", "ycp2", "ycp3",
    ]

    _regs = {}

    def qdyn(eng, name):
        if name not in _regs:
            _regs[name] = eng.compute_val(eng.partition_id() % 4)
        return _regs[name]

    for j in range(16):
        pg.add("sync", (lambda j=j: lambda e: e.dma_start(
            out=s_nw[:, j:j + 1],
            in_=nw_full.ap()[128 * j:128 * (j + 1)].unsqueeze(1)))(),
            inc=("smalls", 16))
    pg.add("sync", lambda e: e.dma_start(out=s_idb[:], in_=ident_bf.ap()),
           inc=("smalls", 16))
    pg.add("sync", lambda e: e.dma_start(out=s_idf[:], in_=ident_f.ap()),
           inc=("smalls", 16))
    for ci in range(32):
        pg.add("sync", (lambda ci=ci: lambda e: e.dma_start(
            out=s_cw[:, ci * KC:(ci + 1) * KC],
            in_=cw_full.ap()[128 * ci:128 * (ci + 1), :]))(), inc=("smalls", 16))
        pg.add("sync", (lambda ci=ci: lambda e: e.dma_start(
            out=s_cb[:, ci:ci + 1],
            in_=cb_full.ap()[128 * ci:128 * (ci + 1)].unsqueeze(1)))(),
            inc=("smalls", 16))
    for ct in range(8):
        pg.add("sync", (lambda ct=ct: lambda e: e.dma_start(
            out=s_al[:, ct * NST:(ct + 1) * NST],
            in_=alog_sl.ap()[128 * ct:128 * (ct + 1), :]))(), inc=("smalls", 16))
        for (dst, src) in ((s_dtw, dtw_sl), (s_dtb, dtb_sl), (s_dpr, dpar_sl)):
            pg.add("sync", (lambda ct=ct, dst=dst, src=src: lambda e: e.dma_start(
                out=dst[:, ct:ct + 1],
                in_=src.ap()[128 * ct:128 * (ct + 1)].unsqueeze(1)))(),
                inc=("smalls", 16))
    SM = pg.val("smalls")
    pg.add("vector", lambda e: e.memset(s_kon[:, 0:1], EPS), inc=("p1v", 1))
    pg.add("vector", lambda e: e.memset(s_ones[:], 1.0), inc=("p1v", 1))
    ONES_V = pg.val("p1v")
    pg.add("vector", lambda e: e.memset(s_kon[:, 1:2], -0.5), inc=("p1v", 1))
    pg.add("vector", lambda e: e.memset(s_mv[:], -1e30), inc=("p3v", 1))
    pg.add("scalar", lambda e: e.activation(s_al[:], s_al[:], AF.Exp),
           wait=[("smalls", SM)], inc=("p2s", 1))
    pg.add("vector", lambda e: e.tensor_scalar(s_al[:], s_al[:], -1.0, None,
                                               ALU.mult),
           wait=[L_("p2s")], inc=("p2v", 1))
    ANEG_V = pg.val("p2v")

    p0n = dict(tr=0)
    ev_marks = [0, 0]
    slot_marks = [0, 0]

    def prep_weight(src_ap, key, scale_nw=False):
        rows, cols = SZ[key][1], SZ[key][0]
        rows, cols = src_ap.shape
        for rt in range(rows // 128 if rows >= 128 else 1):
            nr = min(128, rows)
            lb, bb = p0_ld[rt % 2], p0_bf[rt % 2]
            pg.add("sync", (lambda rt=rt, lb=lb, nr=nr, cols=cols: lambda e: e.dma_start(
                out=lb[0:nr, :cols], in_=src_ap[128 * rt:128 * rt + nr, :]))(),
                wait=[L_("p0v")], inc=("p0ld", 16))
            pg.add("vector", (lambda lb=lb, bb=bb, nr=nr, cols=cols: lambda e: e.tensor_copy(
                bb[0:nr, :cols], lb[0:nr, :cols]))(),
                wait=[L_("p0ld")], inc=("p0v", 1))
            castv = pg.val("p0v")
            for jt in range(cols // 128):
                slot = p0n["tr"] % 2
                p0n["tr"] += 1
                w = [("p0v", castv), ("smalls", SM), ("p0v", slot_marks[slot])]
                pg.add("tensor", (lambda bb=bb, jt=jt, slot=slot: lambda e: e.transpose(
                    pTR2[slot], bb[:, jt * 128:(jt + 1) * 128], s_idb[:]))(),
                    wait=w, inc=("p0t", 1))
                ev = p0_ev[p0n["tr"] % 2]
                wv = [L_("p0t"), L_("p0st")]
                if scale_nw:
                    pg.add("vector", (lambda ev=ev, slot=slot, jt=jt: lambda e: e.tensor_scalar(
                        ev[:], pTR2[slot], s_nw[:, jt:jt + 1], None, ALU.mult))(),
                        wait=wv, inc=("p0v", 1))
                else:
                    pg.add("vector", (lambda ev=ev, slot=slot: lambda e: e.tensor_copy(
                        ev[:], pTR2[slot]))(),
                        wait=wv, inc=("p0v", 1))
                slot_marks[slot] = pg.val("p0v")
                if key == "win":
                    dst = arena_in.ap()[
                        OFF[key] + rt * (128 * 2048):
                        OFF[key] + (rt + 1) * (128 * 2048)].rearrange(
                        "(p t j) -> p t j", p=128, t=16)[:, jt, :]
                else:
                    dst = arena_in.ap()[OFF[key]:OFF[key] + cols * rows].rearrange(
                        "(c r) -> c r", r=rows)[jt * 128:(jt + 1) * 128,
                                                128 * rt:128 * rt + nr]
                pg.add("sync", (lambda dst=dst, ev=ev, nr=nr: lambda e: e.dma_start(
                    out=dst, in_=ev[:, 0:nr]))(),
                    wait=[L_("p0v")], inc=("p0st", 16))
                ev_marks[p0n["tr"] % 2] = pg.val("p0st")

    pg.cuts = {}
    pg.cuts["sm"] = {e: len(pg.seq[e]) for e in ENGS}
    prep_weight(ip_sl.ap(), "win", scale_nw=True)
    pg.cuts["w_ip"] = {e: len(pg.seq[e]) for e in ENGS}
    prep_weight(w1_sl.ap(), "w1")
    prep_weight(w3_sl.ap(), "w3")
    prep_weight(w2_sl.ap(), "w2")
    prep_weight(wo_sl.ap(), "wo")
    prep_weight(ws1_sl.ap(), "ws1")
    prep_weight(ws3_sl.ap(), "ws3")
    prep_weight(ws2_sl.ap(), "ws2")

    pg.cuts["p0w"] = {e: len(pg.seq[e]) for e in ENGS}
    pg.add("gpsimd", lambda e: e.collective_compute(
        "AllGather", ALU.bypass, replica_groups=GROUPS8,
        ins=[arena_in.ap().opt()], outs=[arena.ap().opt()]),
        wait=[L_("p0st")], inc=("cwag", 1))
    pg.cuts["p0ag"] = {e: len(pg.seq[e]) for e in ENGS}

    pg.add("sync", lambda e: e.dma_start(out=p0_ld[0][0:8, :], in_=gw_full.ap()),
           wait=[L_("p0v")], inc=("p0ld", 16))
    gldv = pg.val("p0ld")
    for jt in range(16):
        pg.add("tensor", (lambda jt=jt: lambda e: e.transpose(
            pE2[:, 0:128], p0_ld[0][:, jt * 128:(jt + 1) * 128], s_idf[:]))(),
            wait=[("p0ld", gldv), ("smalls", SM), L_("p0v")], inc=("p0t", 1))
        pg.add("vector", (lambda jt=jt: lambda e: e.tensor_copy(
            s_gwT[:, jt * 8:(jt + 1) * 8], pE2[:, 0:8]))(),
            wait=[L_("p0t")], inc=("p0v", 1))
    for half in range(2):
        pg.add("sync", (lambda half=half: lambda e: e.dma_start(
            out=p0_ld[1][0:33, :],
            in_=xp_full.ap()[:, half * 2048:(half + 1) * 2048]))(),
            wait=[L_("p0v")], inc=("p0ld", 16))
        xldv = pg.val("p0ld")
        for jt in range(16):
            ci = half * 16 + jt
            pg.add("tensor", (lambda jt=jt: lambda e: e.transpose(
                pE2[:, 0:128], p0_ld[1][:, jt * 128:(jt + 1) * 128], s_idf[:]))(),
                wait=[("p0ld", xldv), L_("p0v")], inc=("p0t", 1))
            pg.add("vector", (lambda ci=ci: lambda e: e.tensor_copy(
                s_xpTb[:, ci * 33:(ci + 1) * 33], pE2[:, 0:33]))(),
                wait=[L_("p0t")], inc=("p0v", 1))
    P0V_DONE = pg.val("p0v")
    P0T_DONE = pg.val("p0t")

    pg.cuts["p0"] = {e: len(pg.seq[e]) for e in ENGS}

    pg.add("vector", lambda e: e.memset(s_xld[0][:], 0.0), inc=("p1v", 1))
    pg.add("vector", lambda e: e.memset(s_xld[1][:], 0.0), inc=("p1v", 1))
    XLD_INIT = pg.val("p1v")

    xmark = {}
    sqmark = 0
    p1slot = [0, 0]

    for ti in range(33):
        r0 = 0 if ti == 0 else 3 + 128 * (ti - 1)
        nr = 3 if ti == 0 else 128
        xb = s_xld[ti % 2]
        w = []
        if ti >= 2:
            w.append(("p1v", xmark[ti - 2]))
        xsem = f"xld{ti % 2}"
        if ti < 2:
            w.append(("p1v", XLD_INIT))
        pg.add("sync", (lambda xb=xb, r0=r0, nr=nr: lambda e: e.dma_start(
            out=xb[0:nr, :], in_=x_loc.ap()[r0:r0 + nr, :]))(),
            wait=w, inc=(xsem, 16))
        xv = pg.val(xsem)
        pg.add("vector", (lambda xb=xb: lambda e: e.reduce_sum(
            s_cols[:, 0:1], xb[:], axis=AX))(),
            wait=[(xsem, xv)], inc=("p1v", 1))
        pg.add("scalar", (lambda xb=xb: lambda e: e.activation(
            s_sq[:], xb[:], AF.Square))(),
            wait=[(xsem, xv), ("p1v", sqmark)], inc=("p1s", 1))
        pg.add("vector", lambda e: e.reduce_sum(s_cols[:, 1:2], s_sq[:], axis=AX),
               wait=[L_("p1s")], inc=("p1v", 1))
        sqmark = pg.val("p1v")

        pg.add("vector", lambda e: e.tensor_scalar(s_cols[:, 2:3], s_cols[:, 0:1], 1.0 / D, None, ALU.mult), inc=("p1v", 1))
        pg.add("vector", lambda e: e.tensor_scalar(s_cols[:, 3:4], s_cols[:, 1:2], 1.0 / D, None, ALU.mult), inc=("p1v", 1))
        pg.add("vector", lambda e: e.tensor_tensor(s_cols[:, 4:5], s_cols[:, 2:3], s_cols[:, 2:3], ALU.mult), inc=("p1v", 1))
        pg.add("vector", lambda e: e.tensor_tensor(s_cols[:, 3:4], s_cols[:, 3:4], s_cols[:, 4:5], ALU.subtract), inc=("p1v", 1))
        pg.add("scalar", lambda e: e.activation(s_cols[:, 5:6], s_cols[:, 3:4],
                                                AF.Ln, bias=s_kon[:, 0:1]),
               wait=[L_("p1v")], inc=("p1s", 1))
        pg.add("scalar", lambda e: e.activation(s_cols[:, 6:7], s_cols[:, 5:6],
                                                AF.Exp, scale=s_kon[:, 1:2]), inc=("p1s", 1))
        pg.add("scalar", lambda e: e.activation(s_cols[:, 7:8], s_cols[:, 5:6],
                                                AF.Exp, scale=-1.0), inc=("p1s", 1))
        pg.add("vector", lambda e: e.tensor_tensor(s_cols[:, 8:9], s_cols[:, 3:4],
                                                   s_cols[:, 7:8], ALU.mult),
               wait=[L_("p1s")], inc=("p1v", 1))
        pg.add("scalar", lambda e: e.activation(s_cols[:, 9:10], s_cols[:, 8:9],
                                                AF.Ln, bias=s_kon[:, 0:1]),
               wait=[L_("p1v")], inc=("p1s", 1))
        pg.add("scalar", lambda e: e.activation(s_cols[:, 10:11], s_cols[:, 9:10],
                                                AF.Exp, scale=s_kon[:, 1:2]), inc=("p1s", 1))

        pg.add("vector", lambda e: e.tensor_tensor(s_cols[:, 11:12], s_cols[:, 6:7], s_cols[:, 10:11], ALU.mult), wait=[L_("p1s")], inc=("p1v", 1))
        pg.add("vector", lambda e: e.tensor_tensor(s_cols[:, 12:13], s_cols[:, 2:3], s_cols[:, 11:12], ALU.mult), inc=("p1v", 1))
        pg.add("vector", lambda e: e.tensor_scalar(s_cols[:, 12:13], s_cols[:, 12:13], -1.0, None, ALU.mult), inc=("p1v", 1))
        pg.add("vector", (lambda xb=xb: lambda e: e.scalar_tensor_tensor(
            s_xn2[:], xb[:], s_cols[:, 11:12],
            s_cols[:, 12:13].to_broadcast([128, 2048]),
            ALU.mult, ALU.add))(),
            wait=[L_("p1v")], inc=("p1v", 1))
        xmark[ti] = pg.val("p1v")
        xnv = pg.val("p1v")
        c0c = 0 if ti == 0 else 3 + 128 * (ti - 1)
        ncc = 3 if ti == 0 else 128
        for dt in range(16):
            sl = (dt + 16 * ti) % 2
            pg.add("tensor", (lambda dt=dt, sl=sl: lambda e: e.transpose(
                pTR2[sl], s_xn2[:, dt * 128:(dt + 1) * 128], s_idb[:]))(),
                wait=[("p1v", xnv), ("p1v", p1slot[sl]), ("smalls", SM),
                      ("p0v", P0V_DONE), ("p0t", P0T_DONE)],
                inc=("p1t", 1))
            pg.add("vector", (lambda dt=dt, sl=sl, c0c=c0c, ncc=ncc: lambda e: e.tensor_copy(
                s_xn2T[:, dt * 4099 + c0c:dt * 4099 + c0c + ncc],
                pTR2[sl][:, 0:ncc]))(),
                wait=[L_("p1t")], inc=("p1v", 1))
            p1slot[sl] = pg.val("p1v")
    LNT_V = pg.val("p1v")
    pg.cuts["p1a"] = {e: len(pg.seq[e]) for e in ENGS}

    pg.add("vector", lambda e: e.memset(s_xs[:], 0.0),
           wait=[("p1v", LNT_V)], inc=("p1v", 1))
    XSMEM_V = pg.val("p1v")
    win_marks = [0, 0]
    psum_marks = [0, 0, 0, 0]
    xin_mark = 0
    zbf_mark = 0
    u_mark = 0
    u_st_mark = 0
    mmctr = 0
    UZ_SB0 = None
    for sb in range(2):
        for ci in range(64):
            is_z = ci >= 32
            wb = s_win[ci % 2]
            rk = ci // 8
            w0 = [("cwag", 1), ("p1t", win_marks[ci % 2]),
                  ("p0v", P0V_DONE), ("p0t", P0T_DONE)]
            wsem = f"winld{ci % 2}"
            cb = ci % 8
            wv_t = arena.ap()[rk, OFF["win"] + cb * (128 * 2048):
                              OFF["win"] + (cb + 1) * (128 * 2048)].rearrange(
                "(p f) -> p f", p=128)
            pg.add("sync", (lambda wb=wb, wv_t=wv_t: lambda e: e.dma_start(
                out=wb[:], in_=wv_t))(),
                wait=w0, inc=(wsem, 16))
            wldv = pg.val(wsem)
            chunks = ([(0, 512), (512, 512), (1024, 512), (1536, 512), (2048, 3)]
                      if not is_z else
                      [(3, 512), (515, 512), (1027, 512), (1539, 512)])
            for (co, cw_) in chunks:
                pidx = mmctr % 4
                ps = [pA, pB, pE2[:, 0:512], pE2[:, 512:1024]][pidx]
                mmctr += 1
                base = sb * 2048
                for dt in range(16):
                    w = []
                    if dt == 0:
                        w = [(wsem, wldv), ("p1v", LNT_V),
                             ("p1v", psum_marks[pidx])]
                    pg.add("tensor", (lambda ps=ps, wb=wb, dt=dt, base=base, co=co, cw_=cw_: lambda e: e.matmul(
                        ps[:, 0:cw_], wb[:, dt * 128:(dt + 1) * 128],
                        s_xn2T[:, dt * 4099 + base + co:dt * 4099 + base + co + cw_],
                        start=(dt == 0), stop=(dt == 15)))(),
                        wait=w, inc=("p1t", 1) if dt == 15 else None)
                mmd = pg.val("p1t")
                if not is_z:
                    pg.add("vector", (lambda ps=ps, co=co, cw_=cw_: lambda e: e.tensor_copy(
                        s_xin[:, co:co + cw_], ps[:, 0:cw_]))(),
                        wait=[("p1t", mmd), ("p1s", xin_mark)], inc=("p1v", 1))
                else:
                    pg.add("vector", (lambda ps=ps, co=co, cw_=cw_: lambda e: e.tensor_copy(
                        s_zbf[:, co - 3:co - 3 + cw_], ps[:, 0:cw_]))(),
                        wait=[("p1t", mmd), L_("uzst")], inc=("p1v", 1))
                psum_marks[pidx] = pg.val("p1v")
            win_marks[ci % 2] = pg.val("p1t")
            evd = pg.val("p1v")
            pg.cuts[f"p1b_{sb}_{ci}_mm"] = {e: len(pg.seq[e]) for e in ENGS}
            if is_z:
                zi = ci - 32
                pg.add("sync", (lambda zi=zi, sb=sb: lambda e: e.dma_start(
                    out=z_scr.ap()[zi * 128:(zi + 1) * 128,
                                   sb * 2048:(sb + 1) * 2048],
                    in_=s_zbf[:]))(),
                    wait=[("p1v", evd)], inc=("uzst", 16))
                zbf_mark = pg.val("uzst")
            else:
                pg.add("vector", (lambda ci=ci: lambda e: e.tensor_scalar(
                    s_cv[:], s_xin[:, 0:2048], s_cw[:, ci * KC:ci * KC + 1],
                    None, ALU.mult))(),
                    wait=[("p1v", evd), ("p1s", xin_mark)], inc=("p1v", 1))
                for k in range(1, 4):
                    pg.add("vector", (lambda ci=ci, k=k: lambda e: e.scalar_tensor_tensor(
                        s_cv[:], s_xin[:, k:k + 2048],
                        s_cw[:, ci * KC + k:ci * KC + k + 1],
                        s_cv[:], ALU.mult, ALU.add))(), inc=("p1v", 1))
                cvd = pg.val("p1v")
                pg.add("scalar", (lambda ci=ci: lambda e: e.activation(
                    s_u[:], s_cv[:], AF.Silu, bias=s_cb[:, ci:ci + 1]))(),
                    wait=[("p1v", cvd), ("p1t", u_mark), L_("uzst")],
                    inc=("p1s", 1))
                xin_mark = pg.val("p1s")
                uv = pg.val("p1s")
                u_dst = u_in0 if sb == 0 else u_in1
                pg.add("sync", (lambda ci=ci, u_dst=u_dst: lambda e: e.dma_start(
                    out=u_dst.ap()[ci * 128:(ci + 1) * 128, :],
                    in_=s_u[:]))(),
                    wait=[("p1s", uv)], inc=("uzst", 16))
                u_st_mark = pg.val("uzst")
                for q in range(4):
                    w = [("p1s", uv), ("p0v", P0V_DONE)]
                    if q > 0:
                        w = [L_("p1v")]
                    pg.add("tensor", (lambda ci=ci, q=q: lambda e: e.matmul(
                        pE1[0:33, 0:512], s_xpTb[:, ci * 33:(ci + 1) * 33],
                        s_u[:, q * 512:(q + 1) * 512], start=True, stop=True))(),
                        wait=w, inc=("p1t", 1))
                    pg.add("vector", (lambda sb=sb, q=q: lambda e: e.tensor_tensor(
                        s_xs[:, sb * 2048 + q * 512:sb * 2048 + (q + 1) * 512],
                        s_xs[:, sb * 2048 + q * 512:sb * 2048 + (q + 1) * 512],
                        pE1[0:33, 0:512], ALU.add))(),
                        wait=[L_("p1t"), ("p1v", XSMEM_V)], inc=("p1v", 1))
                u_mark = pg.val("p1t")
            pg.cuts[f"p1b_{sb}_{ci}"] = {e: len(pg.seq[e]) for e in ENGS}
        if sb == 0:
            UZ_SB0 = pg.val("uzst")

    pg.add("vector", lambda e: e.tensor_copy(s_xs_bf[:], s_xs[:]),
           wait=[L_("p1v")], inc=("p1v", 1))
    pg.add("sync", lambda e: e.dma_start(out=xs_in.ap(), in_=s_xs_bf[:]),
           wait=[L_("p1v")], inc=("uzst", 16))
    UZ = pg.val("uzst")

    pg.add("gpsimd", lambda e: e.collective_compute(
        "AllGather", ALU.bypass, replica_groups=GROUPS4,
        ins=[u_in0.ap().opt()], outs=[u_ag2.ap().opt()]),
        wait=[("uzst", UZ_SB0)], inc=("cuag", 1))
    pg.add("gpsimd", lambda e: e.collective_compute(
        "AllGather", ALU.bypass, replica_groups=GROUPS4,
        ins=[u_in1.ap().opt()], outs=[u_ag3.ap().opt()]),
        wait=[("uzst", UZ)], inc=("cuagB", 1))
    pg.add("gpsimd", lambda e: e.collective_compute(
        "AllGather", ALU.bypass, replica_groups=GROUPS4,
        ins=[xs_in.ap().opt()], outs=[xs_ag.ap().opt()]),
        wait=[("uzst", UZ)], inc=("cxag", 1))

    pg.cuts["p1"] = {e: len(pg.seq[e]) for e in ENGS}

    pg.add("vector", lambda e: e.memset(s_st[:], 0.0),
           wait=[("cxag", 1)], inc=("p2v", 1))
    STMEM_V = pg.val("p2v")

    xs_v = xs_ag.ap()
    u_v2 = u_ag2.ap().rearrange("g (q c) t -> g q c t", q=4, c=CLOC)
    u_v3 = u_ag3.ap().rearrange("g (q c) t -> g q c t", q=4, c=CLOC)
    for rb in range(4):
        pg.add("gpsimd", (lambda rb=rb: lambda e: e.dma_start(
            out=u_loc.ap()[rb][:, 0:2048],
            in_=u_v2[rb][bass.ts(qdyn(e, "gpsimd"), 1)].squeeze(0)))(),
            wait=[("cuag", 1)], inc=(f"ucp{rb}", 16))
    for rb in range(4):
        pg.add("gpsimd", (lambda rb=rb: lambda e: e.dma_start(
            out=u_loc.ap()[rb][:, 2048:4096],
            in_=u_v3[rb][bass.ts(qdyn(e, "gpsimd"), 1)].squeeze(0)))(),
            wait=[("cuagB", 1)], inc=(f"ucp{rb}", 16))

    st_mark = {}
    bc_marks = [0, 0]
    bcg_marks = [0, 0]
    u2_marks = [0, 0]
    pe_marks = [0, 0]
    y2_marks = [0, 0]
    abht_marks = {"a": [0, 0], "b": [0, 0], "h": [0, 0], "tmp": [0, 0]}

    bc_ps = [pA, pB, pC, pD]
    bc_ps_vm = [0, 0, 0, 0]
    bc_ps_sm = [0, 0, 0, 0]
    browm = 0
    for tt in range(16):
        bb = tt % 2
        r_blk, toff = (tt * 1024) // TLOC, (tt * 1024) % TLOC
        w = [("cxag", 1), ("p2t", browm)]
        for m3 in range(3):
            nrows = 11 if m3 != 2 else 11
            pg.add("sync", (lambda m3=m3, r_blk=r_blk, toff=toff: lambda e: e.dma_start(
                out=s_brow[32 * m3:32 * m3 + 1, 0:11 * 1024].rearrange(
                    "p (r t) -> p r t", t=1024),
                in_=xs_v[r_blk, m3:33:3, toff:toff + 1024].unsqueeze(0)))(),
                wait=w if m3 == 0 else [], inc=("brow", 16))
        brv = pg.val("brow")
        nev_v = 0
        nev_s = 0
        for row in range(33):
            m3, slot = row % 3, row // 3
            if row < NST:
                dsts = [(s_Ball[bb], row)]
            elif row < 2 * NST:
                dsts = [(s_Call[bb], row - NST)]
            else:
                dsts = [(s_rbc[bb], 0)]
            for q in range(2):
                pi = (tt * 66 + row * 2 + q) % 4
                ps = bc_ps[pi]
                pg.add("tensor", (lambda m3=m3, slot=slot, q=q, ps=ps: lambda e: e.matmul(
                    ps[:, 0:512], s_ones[32 * m3:32 * m3 + 1, :],
                    s_brow[32 * m3:32 * m3 + 1,
                           slot * 1024 + q * 512:slot * 1024 + (q + 1) * 512],
                    start=True, stop=True))(),
                    wait=[("brow", brv), ("p1v", ONES_V),
                          ("p2v", bc_ps_vm[pi]), ("p2s", bc_ps_sm[pi])],
                    inc=("p2t", 1))
                mmv = pg.val("p2t")
                dst, drow = dsts[0]
                if row < NST:
                    pg.add("vector", (lambda dst=dst, drow=drow, q=q, ps=ps: lambda e: e.tensor_copy(
                        dst[:, drow * 1024 + q * 512:drow * 1024 + (q + 1) * 512],
                        ps[:, 0:512]))(),
                        wait=[("p2t", mmv), ("p2v", bc_marks[bb] if row == 0 and q == 0 else 0)],
                        inc=("p2v", 1))
                    bc_ps_vm[pi] = pg.val("p2v")
                    nev_v += 1
                else:
                    pg.add("scalar", (lambda dst=dst, drow=drow, q=q, ps=ps: lambda e: e.copy(
                        dst[:, drow * 1024 + q * 512:drow * 1024 + (q + 1) * 512],
                        ps[:, 0:512]))(),
                        wait=[("p2t", mmv), ("p2s", bcg_marks[bb] if row == NST and q == 0 else 0)],
                        inc=("p2s", 1))
                    bc_ps_sm[pi] = pg.val("p2s")
                    nev_s += 1
        browm = pg.val("p2t")
        bBv, bCv = pg.val("p2v"), pg.val("p2s")
        pg.add("vector", lambda e: e.nop(),
               wait=[("p2s", bCv)], inc=("p2v", 1))
        pg.add("gpsimd", lambda e: e.nop(),
               wait=[("p2v", bBv), ("p2s", bCv)], inc=("p2g", 1))
        for ct in range(8):
            ub = s_u2[ct % 2]
            usem = f"uld{ct % 2}"
            pg.add("sync", (lambda ub=ub, r_blk=r_blk, toff=toff, ct=ct: lambda e: e.dma_start(
                out=ub[:],
                in_=u_loc.ap()[r_blk][128 * ct:128 * (ct + 1),
                                      toff:toff + 1024]))(),
                wait=[("p2v", u2_marks[ct % 2]), (f"ucp{r_blk}", 32)],
                inc=(usem, 16))
            uldv = pg.val(usem)
            pg.add("vector", (lambda bb=bb, ct=ct: lambda e: e.scalar_tensor_tensor(
                s_arg[:], s_rbc[bb][:], s_dtw[:, ct:ct + 1],
                s_dtb[:, ct:ct + 1].to_broadcast([128, 1024]),
                ALU.mult, ALU.add))(),
                wait=[L_("p2s"), ("smalls", SM)],
                inc=("p2v", 1))
            pg.add("scalar", lambda e: e.nop(),
                   wait=[L_("p2v")], inc=("p2s", 1))
            pg.add("scalar", lambda e: e.activation(s_t2[:], s_arg[:], AF.Abs),
                   inc=("p2s", 1))
            pg.add("scalar", lambda e: e.activation(s_t2[:], s_t2[:], AF.Exp,
                                                    scale=-1.0), inc=("p2s", 1))
            pg.add("scalar", lambda e: e.activation(s_t2[:], s_t2[:], AF.Ln,
                                                    bias=1.0), inc=("p2s", 1))
            dtsv = pg.val("p2s")
            pg.add("vector", lambda e: e.tensor_scalar(s_dt[:], s_arg[:], 0.0,
                                                       None, ALU.max),
                   wait=[("p2s", dtsv)], inc=("p2v", 1))
            pg.add("vector", lambda e: e.tensor_tensor(s_dt[:], s_dt[:], s_t2[:],
                                                       ALU.add), inc=("p2v", 1))
            pg.add("vector", (lambda ub=ub: lambda e: e.tensor_tensor(
                s_w2[:], s_dt[:], ub[:], ALU.mult))(),
                wait=[(usem, uldv), L_("p2g")], inc=("p2v", 1))
            w2v = pg.val("p2v")
            u2_marks[ct % 2] = w2v
            pidx = ct % 2
            ps = pE1 if pidx == 0 else pE2

            def emit_b(n):
                nb = n % 2
                bbuf = s_b[nb]
                pg.add("gpsimd", (lambda bbuf=bbuf, bb=bb, n=n: lambda e: e.tensor_tensor(
                    bbuf[:], s_w2[:], s_Ball[bb][:, n * 1024:(n + 1) * 1024],
                    ALU.mult))(),
                    wait=[("p2v", max(w2v, abht_marks["b"][n % 2]))],
                    inc=("p2g", 1))

            emit_b(0)
            for n in range(NST):
                nb = n % 2
                ab, bbuf, hb, tb_ = s_a[nb], s_b[nb], s_h[nb], s_tmp[nb]
                tmp_on_g = (n % 3 == 0)
                pg.add("scalar", (lambda ab=ab, ct=ct, n=n: lambda e: e.activation(
                    ab[:], s_dt[:], AF.Exp,
                    scale=s_al[:, ct * NST + n:ct * NST + n + 1]))(),
                    wait=[("p2v", max(w2v, abht_marks["a"][nb]))],
                    inc=("p2s", 1))
                bgate = pg.val("p2g")
                wscan = [L_("p2s"), ("p2g", bgate),
                         ("p2t", abht_marks["h"][nb]),
                         ("p2v", st_mark.get((ct, n), 0))]
                pg.add("vector", (lambda ab=ab, bbuf=bbuf, hb=hb, ct=ct, n=n: lambda e: e.tensor_tensor_scan(
                    hb[:], ab[:], bbuf[:],
                    s_st[:, ct * NST + n:ct * NST + n + 1], ALU.mult, ALU.add))(),
                    wait=wscan,
                    inc=("p2v", 1))
                scan_v = pg.val("p2v")
                abht_marks["a"][nb] = scan_v
                abht_marks["b"][nb] = scan_v
                if n + 1 < NST:
                    emit_b(n + 1)
                pg.add("vector", (lambda hb=hb, ct=ct, n=n: lambda e: e.tensor_copy(
                    s_st[:, ct * NST + n:ct * NST + n + 1], hb[:, 1023:1024]))(),
                    inc=("p2v", 1))
                st_mark[(ct, n)] = pg.val("p2v")
                if tmp_on_g:
                    pg.add("gpsimd", (lambda hb=hb, tb_=tb_, bb=bb, n=n: lambda e: e.tensor_tensor(
                        tb_[:], hb[:], s_Call[bb][:, n * 1024:(n + 1) * 1024],
                        ALU.mult))(),
                        wait=[("p2v", scan_v), ("p2t", abht_marks["tmp"][nb])],
                        inc=("p2g", 1))
                    tmpw = [L_("p2g")]
                else:
                    pg.add("vector", (lambda hb=hb, tb_=tb_, bb=bb, n=n: lambda e: e.tensor_tensor(
                        tb_[:], hb[:], s_Call[bb][:, n * 1024:(n + 1) * 1024],
                        ALU.mult))(),
                        wait=[("p2t", abht_marks["tmp"][nb])],
                        inc=("p2v", 1))
                    tmpw = [L_("p2v")]
                for q in range(2):
                    w = list(tmpw)
                    if n == 0 and q == 0:
                        w.append(("p2v", pe_marks[pidx]))
                    pg.add("tensor", (lambda ps=ps, tb_=tb_, q=q, n=n: lambda e: e.matmul(
                        ps[:, q * 512:(q + 1) * 512], s_idb[:],
                        tb_[:, q * 512:(q + 1) * 512],
                        start=(n == 0), stop=(n == NST - 1)))(),
                        wait=w, inc=("p2t", 1))
                abht_marks["tmp"][nb] = pg.val("p2t")
                abht_marks["h"][nb] = pg.val("p2t")
            yb = s_y2[ct % 2]
            pg.add("vector", (lambda yb=yb, ub=ub, ps=ps, ct=ct: lambda e: e.scalar_tensor_tensor(
                yb[:], ub[:], s_dpr[:, ct:ct + 1], ps[:], ALU.mult, ALU.add))(),
                wait=[L_("p2t"), L_("yst")], inc=("p2v", 1))
            pe_marks[pidx] = pg.val("p2v")
            u2_marks[ct % 2] = pg.val("p2v")
            pg.add("scalar", (lambda yb=yb, ct=ct, tt=tt: lambda e: e.dma_start(
                out=y_in.ap()[ct * 128:(ct + 1) * 128,
                              tt * 1024:(tt + 1) * 1024], in_=yb[:]))(),
                wait=[L_("p2v")], inc=("yst", 16))
            y2_marks[ct % 2] = pg.val("yst")
        bc_marks[bb] = pg.val("p2v")
        bcg_marks[bb] = pg.val("p2g")
        if tt == 7:
            YST_H0 = pg.val("yst")

    YST = pg.val("yst")
    pg.add("gpsimd", lambda e: e.collective_compute(
        "AllGather", ALU.bypass, replica_groups=GROUPS4,
        ins=[y_in.ap().opt()], outs=[y_ag.ap().opt()]),
        wait=[("yst", YST)], inc=("cyag", 1))

    pg.cuts["p2"] = {e: len(pg.seq[e]) for e in ENGS}

    y_v = y_ag.ap().rearrange("g c (q t) -> g c q t", q=4, t=TLOC)
    for rb in range(4):
        pg.add("gpsimd", (lambda rb=rb: lambda e: e.dma_start(
            out=y_loc.ap()[rb], in_=y_v[rb][:, bass.ts(qdyn(e, "gpsimd"), 1)]
            .squeeze(1)))(),
            wait=[("cyag", 1)], inc=(f"ycp{rb}", 16))
    outst_total = 0
    xnst_total = 0
    x2_marks = {}
    yg_mark = 0
    wo_mark = 0
    zt_marks = [0, 0]
    xb_vmark = [0, 0]
    xb_smark = [0, 0]
    x2v_mark = 0
    x3f_mark = 0
    x3b_mark = 0
    sq3_mark = 0
    trslot_mark = [0, 0]

    for tb in range(8):
        ysem = f"yld{tb % 2}"
        for ci in range(32):
            rr, cc = ci // 8, (ci % 8) * 128
            w = []
            if ci % 8 == 0:
                w = [(f"ycp{rr}", 16)]
                if ci == 0:
                    w.append(("p3t", yg_mark))
            pg.add("sync", (lambda ci=ci, rr=rr, cc=cc, tb=tb: lambda e: e.dma_start(
                out=s_yg[ci][:],
                in_=y_loc.ap()[rr][cc:cc + 128, tb * 512:(tb + 1) * 512]))(),
                wait=w, inc=(ysem, 16))
        yldv = pg.val(ysem)
        for ci in range(32):
            zb = s_zt[ci % 2]
            zsem = f"zld{ci % 2}"
            pg.add("scalar", (lambda ci=ci, zb=zb, tb=tb: lambda e: e.dma_start(
                out=zb[:], in_=z_scr.ap()[ci * 128:(ci + 1) * 128,
                                          tb * 512:(tb + 1) * 512]))(),
                wait=[("p3v", zt_marks[ci % 2])], inc=(zsem, 16))
            pg.add("scalar", (lambda zb=zb: lambda e: e.activation(
                s_sz[:], zb[:], AF.Silu))(),
                wait=[L_(zsem), L_("p3v")], inc=("p3s", 1))
            pg.add("vector", (lambda ci=ci: lambda e: e.tensor_tensor(
                s_yg[ci][:], s_yg[ci][:], s_sz[:], ALU.mult))(),
                wait=[L_("p3s"), (ysem, yldv)], inc=("p3v", 1))
            zt_marks[ci % 2] = pg.val("p3v")
        ygv = pg.val("p3v")
        for tc in range(4):
            xb = s_xld[tc % 2]
            xsem = f"xld{tc % 2}"
            if tb == 0 and tc < 2:
                w = [("cyag", 1)]
            else:
                w = [("p3v", xb_vmark[tc % 2]), ("p3s", xb_smark[tc % 2])]
            pg.add("sync", (lambda xb=xb, tb=tb, tc=tc: lambda e: e.dma_start(
                out=xb[:], in_=x_loc.ap()[3 + tb * 512 + tc * 128:
                                          3 + tb * 512 + (tc + 1) * 128, :]))(),
                wait=w, inc=(xsem, 16))
            xv = pg.val(xsem)
            pg.add("vector", (lambda xb=xb: lambda e: e.reduce_sum(
                s_cols[:, 0:1], xb[:], axis=AX))(),
                wait=[(xsem, xv)], inc=("p3v", 1))
            pg.add("scalar", (lambda xb=xb: lambda e: e.activation(
                s_sq[:], xb[:], AF.Square))(),
                wait=[(xsem, xv), ("p3v", sq3_mark)], inc=("p3s", 1))
            xb_smark[tc % 2] = pg.val("p3s")
            pg.add("vector", lambda e: e.reduce_sum(s_cols[:, 1:2], s_sq[:], axis=AX),
                   wait=[L_("p3s")], inc=("p3v", 1))
            sq3_mark = pg.val("p3v")

            pg.add("vector", lambda e: e.tensor_scalar(s_cols[:, 2:3], s_cols[:, 0:1], 1.0 / D, None, ALU.mult), inc=("p3v", 1))
            pg.add("vector", lambda e: e.tensor_scalar(s_cols[:, 3:4], s_cols[:, 1:2], 1.0 / D, None, ALU.mult), inc=("p3v", 1))
            pg.add("vector", lambda e: e.tensor_tensor(s_cols[:, 4:5], s_cols[:, 2:3], s_cols[:, 2:3], ALU.mult), inc=("p3v", 1))
            pg.add("vector", lambda e: e.tensor_tensor(s_cols[:, 3:4], s_cols[:, 3:4], s_cols[:, 4:5], ALU.subtract), inc=("p3v", 1))
            pg.add("scalar", lambda e: e.activation(s_cols[:, 5:6], s_cols[:, 3:4],
                                                    AF.Ln, bias=s_kon[:, 0:1]),
                   wait=[L_("p3v")], inc=("p3s", 1))
            pg.add("scalar", lambda e: e.activation(s_cols[:, 6:7], s_cols[:, 5:6],
                                                    AF.Exp, scale=s_kon[:, 1:2]),
                   inc=("p3s", 1))

            pg.add("vector", lambda e: e.tensor_scalar(s_cols[:, 7:8], s_cols[:, 6:7], 1.0, None, ALU.add), wait=[L_("p3s")], inc=("p3v", 1))
            pg.add("vector", lambda e: e.tensor_tensor(s_cols[:, 8:9], s_cols[:, 2:3], s_cols[:, 6:7], ALU.mult), inc=("p3v", 1))
            pg.add("vector", lambda e: e.tensor_scalar(s_cols[:, 8:9], s_cols[:, 8:9], -1.0, None, ALU.mult), inc=("p3v", 1))
            pg.add("vector", (lambda xb=xb, tc=tc: lambda e: e.scalar_tensor_tensor(
                s_x2[tc][:], xb[:], s_cols[:, 7:8],
                s_cols[:, 8:9].to_broadcast([128, 2048]),
                ALU.mult, ALU.add))(),
                wait=[L_("p3v"), L_("outst")], inc=("p3v", 1))
            xb_vmark[tc % 2] = pg.val("p3v")
        resv = pg.val("p3v")
        for dc in range(4):
            w = [("cwag", 1), ("p3t", wo_mark)]
            osem2 = f"wold{dc % 2}"
            for ci in range(32):
                rr = (ci * 128) // 512
                ro = (ci * 128) % 512
                wv_ap = arena_view(rr, "wo")
                pg.add("sync", (lambda ci=ci, wv_ap=wv_ap, ro=ro, dc=dc: lambda e: e.dma_start(
                    out=s_wo[ci][:],
                    in_=wv_ap[ro:ro + 128, dc * 512:(dc + 1) * 512]))(),
                    wait=w if ci == 0 else [], inc=(osem2, 16))
            woldv = pg.val(osem2)
            for tc in range(4):
                pidx = (dc * 4 + tc) % 4
                ps = [pA, pB, pE2[:, 0:512], pE2[:, 512:1024]][pidx]
                for ci in range(32):
                    w = []
                    if ci == 0:
                        w = [(osem2, woldv), ("p3v", ygv), ("p3v", resv),
                             L_("p3v")]
                    pg.add("tensor", (lambda ps=ps, ci=ci, tc=tc: lambda e: e.matmul(
                        ps[:], s_yg[ci][:, tc * 128:(tc + 1) * 128],
                        s_wo[ci][:], start=(ci == 0), stop=(ci == 31)))(),
                        wait=w, inc=("p3t", 1) if ci == 31 else None)
                pg.add("vector", (lambda ps=ps, tc=tc, dc=dc: lambda e: e.tensor_tensor(
                    s_x2[tc][:, dc * 512:(dc + 1) * 512],
                    s_x2[tc][:, dc * 512:(dc + 1) * 512], ps[:], ALU.add))(),
                    wait=[L_("p3t")], inc=("p3v", 1))
            wo_mark = pg.val("p3t")
        yg_mark = pg.val("p3t")
        x2v = pg.val("p3v")
        for tc in range(4):
            pg.add("sync", (lambda tc=tc, tb=tb: lambda e: e.dma_start(
                out=out_loc.ap()[tb * 512 + tc * 128:tb * 512 + (tc + 1) * 128, :],
                in_=s_x2[tc][:]))(),
                wait=[("p3v", x2v)], inc=("outst", 16))
            outst_total += 16
            x2_marks[(tb, tc)] = outst_total
            pg.add("vector", (lambda tc=tc: lambda e: e.reduce_sum(
                s_cols[:, 0:1], s_x2[tc][:], axis=AX))(),
                wait=[("p3v", x2v)], inc=("p3v", 1))
            pg.add("scalar", (lambda tc=tc: lambda e: e.activation(
                s_sq[:], s_x2[tc][:], AF.Square))(),
                wait=[("p3v", x2v), ("p3v", sq3_mark)], inc=("p3s", 1))
            pg.add("vector", lambda e: e.reduce_sum(s_cols[:, 1:2], s_sq[:], axis=AX),
                   wait=[L_("p3s")], inc=("p3v", 1))
            sq3_mark = pg.val("p3v")

            pg.add("vector", lambda e: e.tensor_scalar(s_cols[:, 2:3], s_cols[:, 0:1], 1.0 / D, None, ALU.mult), inc=("p3v", 1))
            pg.add("vector", lambda e: e.tensor_scalar(s_cols[:, 3:4], s_cols[:, 1:2], 1.0 / D, None, ALU.mult), inc=("p3v", 1))
            pg.add("vector", lambda e: e.tensor_tensor(s_cols[:, 4:5], s_cols[:, 2:3], s_cols[:, 2:3], ALU.mult), inc=("p3v", 1))
            pg.add("vector", lambda e: e.tensor_tensor(s_cols[:, 3:4], s_cols[:, 3:4], s_cols[:, 4:5], ALU.subtract), inc=("p3v", 1))
            pg.add("scalar", lambda e: e.activation(s_cols[:, 5:6], s_cols[:, 3:4],
                                                    AF.Ln, bias=s_kon[:, 0:1]),
                   wait=[L_("p3v")], inc=("p3s", 1))
            pg.add("scalar", lambda e: e.activation(s_cols[:, 6:7], s_cols[:, 5:6],
                                                    AF.Exp, scale=s_kon[:, 1:2]),
                   inc=("p3s", 1))

            pg.add("vector", lambda e: e.tensor_tensor(s_cols[:, 8:9], s_cols[:, 2:3], s_cols[:, 6:7], ALU.mult), wait=[L_("p3s")], inc=("p3v", 1))
            pg.add("vector", lambda e: e.tensor_scalar(s_cols[:, 8:9], s_cols[:, 8:9], -1.0, None, ALU.mult), inc=("p3v", 1))
            pg.add("vector", (lambda tc=tc: lambda e: e.scalar_tensor_tensor(
                s_xn3[:], s_x2[tc][:], s_cols[:, 6:7],
                s_cols[:, 8:9].to_broadcast([128, 2048]),
                ALU.mult, ALU.add))(),
                wait=[L_("p3v")], inc=("p3v", 1))
            x2v_mark = pg.val("p3v")
            xn3v = pg.val("p3v")
            for dt in range(16):
                sl = dt % 2
                ptr_ = pC if sl == 0 else pD
                pg.add("tensor", (lambda dt=dt, ptr_=ptr_: lambda e: e.transpose(
                    ptr_[:, 0:128],
                    s_xn3[:, dt * 128:(dt + 1) * 128], s_idf[:]))(),
                    wait=[("p3v", xn3v), ("p3v", trslot_mark[sl]),
                          ("p3t", x3f_mark if dt == 0 else 0)],
                    inc=("p3t", 1))
                pg.add("vector", (lambda dt=dt, ptr_=ptr_: lambda e: e.tensor_copy(
                    s_x3f[:, dt * 128:(dt + 1) * 128],
                    ptr_[:, 0:128]))(),
                    wait=[L_("p3t")], inc=("p3v", 1))
                trslot_mark[sl] = pg.val("p3v")
                pg.add("vector", (lambda dt=dt, tc=tc: lambda e: e.tensor_copy(
                    s_x3b[:, dt * 512 + tc * 128:dt * 512 + (tc + 1) * 128],
                    s_x3f[:, dt * 128:(dt + 1) * 128]))(),
                    wait=[("xnst", x3b_mark)] if dt == 0 else [],
                    inc=("p3v", 1))
            trv = pg.val("p3v")
            for dt in range(16):
                w = [("p3v", trv), ("p0v", P0V_DONE)] if dt == 0 else []
                pg.add("tensor", (lambda dt=dt: lambda e: e.matmul(
                    pE1[:, 0:8], s_x3f[:, dt * 128:(dt + 1) * 128],
                    s_gwT[:, dt * 8:(dt + 1) * 8],
                    start=(dt == 0), stop=(dt == 15)))(),
                    wait=w, inc=("p3t", 1) if dt == 15 else None)
            x3f_mark = pg.val("p3t")
            gmv = pg.val("p3t")
            pg.add("vector", lambda e: e.reduce_max(s_cols[:, 16:17], pE1[:, 0:8],
                                                    axis=AX),
                   wait=[("p3t", gmv)], inc=("p3v", 1))
            pg.add("vector", lambda e: e.tensor_scalar(
                s_cols[:, 17:18], s_cols[:, 16:17], -1.0, None, ALU.mult),
                inc=("p3v", 1))
            pg.add("scalar", lambda e: e.activation(s_sc8[:], pE1[:, 0:8], AF.Exp,
                                                    bias=s_cols[:, 17:18]),
                   wait=[L_("p3v")], inc=("p3s", 1))
            pg.add("vector", lambda e: e.reduce_sum(s_cols[:, 18:19], s_sc8[:],
                                                    axis=AX),
                   wait=[L_("p3s")], inc=("p3v", 1))
            pg.add("vector", lambda e: e.reciprocal(s_cols[:, 19:20],
                                                    s_cols[:, 18:19]),
                   inc=("p3v", 1))
            pg.add("vector", lambda e: e.tensor_scalar(
                s_sc8[:], s_sc8[:], s_cols[:, 19:20], None, ALU.mult),
                inc=("p3v", 1))
            pg.add("vector", lambda e: e.max(s_m8[:], s_sc8[:]), inc=("p3v", 1))
            pg.add("vector", lambda e: e.tensor_copy(s_mv[:, 0:2], s_m8[:, 0:2]),
                   inc=("p3v", 1))
            pg.add("vector", lambda e: e.match_replace(s_mr[:], s_mv[:], s_sc8[:],
                                                       -1e30),
                   wait=[L_("p3v")], inc=("p3v", 1))
            pg.add("vector", lambda e: e.tensor_tensor(s_sc8b[:], s_sc8[:],
                                                       s_mr[:], ALU.not_equal),
                   inc=("p3v", 1))
            pg.add("vector", (lambda tb=tb, tc=tc: lambda e: e.tensor_tensor(
                s_comb[:, (tb * 4 + tc) * 8:(tb * 4 + tc + 1) * 8],
                s_sc8b[:], s_sc8[:], ALU.mult))(), inc=("p3v", 1))
        x3v = pg.val("p3v")
        for dt in range(16):
            pg.add("sync", (lambda dt=dt, tb=tb: lambda e: e.dma_start(
                out=xn3_scr.ap()[dt * 128:(dt + 1) * 128,
                                 tb * 512:(tb + 1) * 512],
                in_=s_x3b[:, dt * 512:(dt + 1) * 512]))(),
                wait=[("p3v", x3v)] if dt == 0 else [], inc=("xnst", 16))
            xnst_total += 16
        x3b_mark = xnst_total

    P3A_V = pg.val("p3v")
    P3A_T = pg.val("p3t")
    P3A_S = pg.val("p3s")

    pg.cuts["p3a"] = {e: len(pg.seq[e]) for e in ENGS}

    oacc_tot = {0: 0, 1: 0, 2: 0, 3: 0}
    last_acc = None
    g_mark = 0
    xt_marks = [0, 0]
    pe3_marks = [0, 0]
    pab_marks = {}
    sh_marks = [0, 0]
    wbuf_marks = [0]

    for e_idx in range(9):
        w1buf = s_w1e[0]
        w = [("p3v", P3A_V), ("p3t", P3A_T), ("p3s", P3A_S),
             ("outst", outst_total), ("xnst", xnst_total),
             ("p3t", wbuf_marks[0])]
        first = [set()]

        def wl(fn, q="sync"):
            ww = list(w) if q not in first[0] else []
            first[0].add(q)
            pg.add(q, fn, wait=ww, inc=("wexpg" if q == "gpsimd" else "wexp", 16))
        if e_idx < 8:
            v1 = arena_view(e_idx, "w1")
            v3 = arena_view(e_idx, "w3")
            v2 = arena_view(e_idx, "w2")
            for dt in range(16):
                wl((lambda dt=dt, v1=v1, w1buf=w1buf: lambda e: e.dma_start(
                    out=w1buf[:, dt * 1024:(dt + 1) * 1024],
                    in_=v1[dt * 128:(dt + 1) * 128, :]))())
            for dt in range(16):
                wl((lambda dt=dt, v3=v3: lambda e: e.dma_start(
                    out=s_w3e[:, dt * 1024:(dt + 1) * 1024],
                    in_=v3[dt * 128:(dt + 1) * 128, :]))(), q="scalar")
            for it in range(8):
                wl((lambda it=it, v2=v2: lambda e: e.dma_start(
                    out=s_w2e[:, it * 2048:(it + 1) * 2048],
                    in_=v2[it * 128:(it + 1) * 128, :]))(), q="gpsimd")
        else:
            for dt in range(16):
                for rr in range(8):
                    v = arena_view(rr, "ws1")
                    wl((lambda dt=dt, rr=rr, v=v, w1buf=w1buf: lambda e: e.dma_start(
                        out=w1buf[:, dt * 1024 + rr * 128:dt * 1024 + (rr + 1) * 128],
                        in_=v[dt * 128:(dt + 1) * 128, :]))())
            for dt in range(16):
                for rr in range(8):
                    v = arena_view(rr, "ws3")
                    wl((lambda dt=dt, rr=rr, v=v: lambda e: e.dma_start(
                        out=s_w3e[:, dt * 1024 + rr * 128:dt * 1024 + (rr + 1) * 128],
                        in_=v[dt * 128:(dt + 1) * 128, :]))(), q="scalar")
            for it in range(8):
                for rr in range(8):
                    v = arena_view(rr, "ws2")
                    wl((lambda it=it, rr=rr, v=v: lambda e: e.dma_start(
                        out=s_w2e[:, it * 2048 + rr * 256:it * 2048 + (rr + 1) * 256],
                        in_=v[it * 128:(it + 1) * 128, :]))(), q="gpsimd")
        wv = pg.val("wexp")
        wvg = pg.val("wexpg")
        for tb in range(8):
            xt = s_xt[tb % 2]
            xnsem = f"xnld{tb % 2}"
            w = [("xnst", xnst_total), ("p3t", xt_marks[tb % 2])]
            for dt in range(16):
                pg.add("sync", (lambda dt=dt, tb=tb, xt=xt: lambda e: e.dma_start(
                    out=xt[:, dt * 512:(dt + 1) * 512],
                    in_=xn3_scr.ap()[dt * 128:(dt + 1) * 128,
                                     tb * 512:(tb + 1) * 512]))(),
                    wait=w if dt == 0 else [], inc=(xnsem, 16))
            xnv = pg.val(xnsem)
            for it in range(8):
                ph = pA if it % 2 == 0 else pC
                pq = pB if it % 2 == 0 else pD
                hk = f"a{it % 2}"
                qk = f"b{it % 2}"
                for dt in range(16):
                    w = []
                    if dt == 0:
                        w = [("wexp", wv), (xnsem, xnv),
                             ("p3s", pab_marks.get(hk, 0))]
                    pg.add("tensor", (lambda w1buf=w1buf, xt=xt, it=it, dt=dt, ph=ph: lambda e: e.matmul(
                        ph[:], w1buf[:, dt * 1024 + it * 128:dt * 1024 + (it + 1) * 128],
                        xt[:, dt * 512:(dt + 1) * 512],
                        start=(dt == 0), stop=(dt == 15)))(),
                        wait=w, inc=("p3t", 1) if dt == 15 else None)
                for dt in range(16):
                    w = [("p3v", pab_marks.get(qk, 0))] if dt == 0 else []
                    pg.add("tensor", (lambda xt=xt, it=it, dt=dt, pq=pq: lambda e: e.matmul(
                        pq[:], s_w3e[:, dt * 1024 + it * 128:dt * 1024 + (it + 1) * 128],
                        xt[:, dt * 512:(dt + 1) * 512],
                        start=(dt == 0), stop=(dt == 15)))(),
                        wait=w, inc=("p3t", 1) if dt == 15 else None)
                hgv = pg.val("p3t")
                shb = s_sh[it % 2]
                pg.add("scalar", (lambda ph=ph, shb=shb: lambda e: e.activation(
                    shb[:], ph[:, 0:512], AF.Silu))(),
                       wait=[("p3t", hgv), ("p3v", sh_marks[it % 2])],
                       inc=("p3s", 1))
                pab_marks[hk] = pg.val("p3s")
                pg.add("vector", (lambda it=it, pq=pq, shb=shb: lambda e: e.tensor_tensor(
                    s_g[it][:], shb[:], pq[:, 0:512], ALU.mult))(),
                    wait=[L_("p3s"), ("p3t", g_mark if it == 0 else 0)],
                    inc=("p3v", 1))
                pab_marks[qk] = pg.val("p3v")
                sh_marks[it % 2] = pg.val("p3v")
            gv = pg.val("p3v")
            for tc in range(4):
                for dh in range(2):
                    ps = pE1 if dh == 0 else pE2
                    for q in range(2):
                        for it in range(8):
                            w = []
                            if it == 0:
                                w = [("p3v", gv), ("p3v", pe3_marks[dh]),
                                     ("wexpg", wvg)]
                            pg.add("tensor", (lambda ps=ps, it=it, tc=tc, dh=dh, q=q: lambda e: e.matmul(
                                ps[:, q * 512:(q + 1) * 512],
                                s_g[it][:, tc * 128:(tc + 1) * 128],
                                s_w2e[:, it * 2048 + dh * 1024 + q * 512:
                                      it * 2048 + dh * 1024 + (q + 1) * 512],
                                start=(it == 0), stop=(it == 7)))(),
                                wait=w, inc=("p3t", 1) if it == 7 else None)
                eov = pg.val("p3t")
                ye_war = [last_acc] if last_acc else []
                for dh in range(2):
                    ps = pE1 if dh == 0 else pE2
                    if e_idx < 8:
                        cc_ = s_comb[:, (tb * 4 + tc) * 8 + e_idx:
                                     (tb * 4 + tc) * 8 + e_idx + 1]
                        pg.add("vector", (lambda ps=ps, dh=dh, cc_=cc_: lambda e: e.tensor_scalar(
                            s_ye[:, dh * 1024:(dh + 1) * 1024], ps[:], cc_,
                            None, ALU.mult))(),
                            wait=([("p3t", eov)] + ye_war if dh == 0 else []),
                            inc=("p3v", 1))
                    else:
                        pg.add("vector", (lambda ps=ps, dh=dh: lambda e: e.tensor_copy(
                            s_ye[:, dh * 1024:(dh + 1) * 1024], ps[:]))(),
                            wait=([("p3t", eov)] + ye_war if dh == 0 else []),
                            inc=("p3v", 1))
                pe3_marks[0] = pg.val("p3v")
                pe3_marks[1] = pg.val("p3v")
                yev = pg.val("p3v")
                osem = f"oacc{tc}"
                pg.add("gpsimd", (lambda tb=tb, tc=tc: lambda e: e.dma_start(
                    out=out_loc.ap()[tb * 512 + tc * 128:
                                     tb * 512 + (tc + 1) * 128, :],
                    in_=s_ye[:], accum_op=ALU.add))(),
                    wait=[("p3v", yev), ("outst", outst_total),
                          (osem, oacc_tot[tc])],
                    inc=(osem, 16))
                oacc_tot[tc] += 16
                last_acc = (osem, oacc_tot[tc])
            g_mark = pg.val("p3t")
            xt_marks[tb % 2] = pg.val("p3t")
        wbuf_marks[0] = pg.val("p3t")

    final_oacc = [(f"oacc{tc}", oacc_tot[tc]) for tc in range(4)]
    pg.add("sync", lambda e: e.nop(),
           wait=[("outst", outst_total)] + final_oacc)
    pg.add("gpsimd", lambda e: e.nop(), wait=final_oacc)
    pg.add("vector", lambda e: e.nop(), wait=final_oacc)

    if stop is not None:
        for e in ENGS:
            pg.seq[e] = pg.seq[e][:pg.cuts[stop][e]]

    from contextlib import ExitStack
    with ExitStack() as ctx:
        sems = {n: ctx.enter_context(nc.semaphore(n)) for n in semnames}
        block = ctx.enter_context(nc.Block())

        CHAINED = ("vector", "scalar", "gpsimd")

        def runner(name):
            def run(eng):
                prev = None
                for waits, fn, inc, iv in pg.seq[name]:
                    seen = {}
                    if name in CHAINED and prev is not None:
                        seen[prev[0]] = prev[1]
                    for (s, v) in waits:
                        if v > 0 and seen.get(s, -1) < v:
                            seen[s] = v
                    for s, v in seen.items():
                        eng.wait_ge(sems[s], v)
                    instr = fn(eng)
                    if inc is not None:
                        assert instr is not None, f"op on {name} returned None but has inc {inc}"
                        instr.then_inc(sems[inc[0]], inc[1])
                        prev = (inc[0], iv)
            return run

        block.sync(runner("sync"))
        block.gpsimd(runner("gpsimd"))
        block.scalar(runner("scalar"))
        block.vector(runner("vector"))
        block.tensor(runner("tensor"))

    return nc


_LAST_PG = None
_NC_CACHE = None


def _get_nc():
    global _NC_CACHE
    if _NC_CACHE is None:
        _NC_CACHE = build_program()
    return _NC_CACHE


def kernel(L0_attn_x, L0_attn_norm_w, L0_attn_in_proj_w, L0_attn_conv_w,
           L0_attn_conv_b, L0_attn_x_proj_w, L0_attn_dt_w, L0_attn_dt_b,
           L0_attn_A_log, L0_attn_D_param, L0_attn_out_proj_w,
           L0_ffn_gate_w, L0_ffn_w1, L0_ffn_w2, L0_ffn_w3,
           L0_ffn_ws1, L0_ffn_ws2, L0_ffn_ws3):
    x = np.ascontiguousarray(L0_attn_x)
    identb = np.eye(128, dtype=BF)
    identf = np.eye(128, dtype=np.float32)
    in_maps = []
    for k in range(NCORE):
        b, q = k // 4, k % 4
        t0 = q * TLOC
        if t0 == 0:
            halo = np.zeros((3, D), np.float32)
        else:
            halo = x[b, t0 - 3:t0]
        x_l = np.ascontiguousarray(
            np.concatenate([halo, x[b, t0:t0 + TLOC]], 0).astype(np.float32))
        c0 = q * CLOC
        m = dict(
            x_loc=x_l,
            ip_sl=np.ascontiguousarray(L0_attn_in_proj_w[1024 * k:1024 * (k + 1)]),
            w1_sl=np.ascontiguousarray(L0_ffn_w1[k]),
            w2_sl=np.ascontiguousarray(L0_ffn_w2[k]),
            w3_sl=np.ascontiguousarray(L0_ffn_w3[k]),
            wo_sl=np.ascontiguousarray(L0_attn_out_proj_w[:, 512 * k:512 * (k + 1)]),
            ws1_sl=np.ascontiguousarray(L0_ffn_ws1[128 * k:128 * (k + 1)]),
            ws3_sl=np.ascontiguousarray(L0_ffn_ws3[128 * k:128 * (k + 1)]),
            ws2_sl=np.ascontiguousarray(L0_ffn_ws2[256 * k:256 * (k + 1)]),
            xp_full=np.ascontiguousarray(L0_attn_x_proj_w),
            gw_full=np.ascontiguousarray(L0_ffn_gate_w),
            nw_full=np.ascontiguousarray(L0_attn_norm_w),
            cw_full=np.ascontiguousarray(L0_attn_conv_w.reshape(DI, KC)),
            cb_full=np.ascontiguousarray(L0_attn_conv_b),
            alog_sl=np.ascontiguousarray(L0_attn_A_log[c0:c0 + CLOC]),
            dpar_sl=np.ascontiguousarray(L0_attn_D_param[c0:c0 + CLOC]),
            dtw_sl=np.ascontiguousarray(L0_attn_dt_w.reshape(DI)[c0:c0 + CLOC]),
            dtb_sl=np.ascontiguousarray(L0_attn_dt_b[c0:c0 + CLOC]),
            ident_bf=identb,
            ident_f=identf,
        )
        in_maps.append(m)

    nc = _get_nc()
    res = run_bass_kernel_spmd(nc, in_maps, core_ids=list(range(NCORE)))
    out = np.empty((B, L, D), np.float32)
    for k in range(NCORE):
        b, q = k // 4, k % 4
        out[b, q * TLOC:(q + 1) * TLOC] = res.results[k]["out_loc"]
    return out


# version 10
# speedup vs baseline: 1.1787x; 1.1240x over previous
import numpy as np
import ml_dtypes
import concourse.bass as bass
import concourse.mybir as mybir
from concourse.bass_utils import run_bass_kernel_spmd

F32 = mybir.dt.float32
BF16 = mybir.dt.bfloat16
AF = mybir.ActivationFunctionType
ALU = mybir.AluOpType
AX = mybir.AxisListType.X

NCORE = 8
B, L, D = 2, 16384, 2048
DI = 4096
NST = 16
KC = 4
IN = 1024
TLOC = 4096
CLOC = 1024
EPS = 1e-5
BF = ml_dtypes.bfloat16

SZ = {
    "win": (2048, 1024), "w1": (2048, 1024), "w3": (2048, 1024),
    "w2": (1024, 2048), "wo": (512, 2048), "ws1": (2048, 128),
    "ws3": (2048, 128), "ws2": (1024, 256),
}
OFF = {}
_o = 0
for _k, (_r, _c) in SZ.items():
    OFF[_k] = _o
    _o += _r * _c
CHUNK = _o

ENGS = ("sync", "gpsimd", "scalar", "vector", "tensor")
GROUPS4 = [[0, 1, 2, 3], [4, 5, 6, 7]]
GROUPS8 = [list(range(8))]


class Prog:
    def __init__(self):
        self.seq = {e: [] for e in ENGS}
        self.cnt = {}

    def val(self, sem):
        return self.cnt.get(sem, 0)

    def last(self, sem):
        return (sem, self.val(sem))

    def add(self, eng, fn, wait=None, inc=None):
        if inc is not None:
            self.cnt[inc[0]] = self.cnt.get(inc[0], 0) + inc[1]
            iv = self.cnt[inc[0]]
        else:
            iv = None
        self.seq[eng].append((list(wait or []), fn, inc, iv))


class SbAlloc:
    def __init__(self, nc, base):
        self.nc = nc
        self.off = base

    def t(self, name, shape, dtype, align=64):
        nbytes = int(np.prod(shape[1:])) * (2 if dtype == BF16 else 4)
        self.off = (self.off + align - 1) // align * align
        h = self.nc.alloc_sbuf_tensor_at(name, list(shape), dtype, offset=self.off)
        self.off += nbytes
        return h


def build_program(stop=None, shared_arena=True):
    nc = bass.Bass("TRN2", target_bir_lowering=False, debug=False)

    EIN = lambda n, s, d=F32: nc.dram_tensor(n, s, d, kind="ExternalInput")
    x_loc = EIN("x_loc", [TLOC + 3, D])
    ip_sl = EIN("ip_sl", [1024, 2048])
    w1_sl = EIN("w1_sl", [1024, 2048])
    w2_sl = EIN("w2_sl", [2048, 1024])
    w3_sl = EIN("w3_sl", [1024, 2048])
    wo_sl = EIN("wo_sl", [2048, 512])
    ws1_sl = EIN("ws1_sl", [128, 2048])
    ws3_sl = EIN("ws3_sl", [128, 2048])
    ws2_sl = EIN("ws2_sl", [256, 1024])
    xp_full = EIN("xp_full", [33, DI])
    gw_full = EIN("gw_full", [8, D])
    nw_full = EIN("nw_full", [D])
    cw_full = EIN("cw_full", [DI, KC])
    cb_full = EIN("cb_full", [DI])
    alog_sl = EIN("alog_sl", [CLOC, NST])
    dpar_sl = EIN("dpar_sl", [CLOC])
    dtw_sl = EIN("dtw_sl", [CLOC])
    dtb_sl = EIN("dtb_sl", [CLOC])
    ident_bf = EIN("ident_bf", [128, 128], BF16)
    ident_f = EIN("ident_f", [128, 128], F32)

    out_loc = nc.dram_tensor("out_loc", [TLOC, D], F32, kind="ExternalOutput")

    arena_in = nc.dram_tensor("arena_in", [CHUNK], BF16)
    arena = nc.dram_tensor(
        "arena", [NCORE, CHUNK], BF16,
        addr_space="Shared" if shared_arena else "Local")
    u_in0 = nc.dram_tensor("u_in0", [DI, TLOC // 2], BF16)
    u_in1 = nc.dram_tensor("u_in1", [DI, TLOC // 2], BF16)
    u_ag2 = nc.dram_tensor("u_ag2", [4, DI, TLOC // 2], BF16)
    u_ag3 = nc.dram_tensor("u_ag3", [4, DI, TLOC // 2], BF16)
    xs_in = nc.dram_tensor("xs_in", [33, TLOC], BF16)
    xs_ag = nc.dram_tensor("xs_ag", [4, 33, TLOC], BF16)
    z_scr = nc.dram_tensor("z_scr", [DI, TLOC], BF16)
    y_in = nc.dram_tensor("y_in", [CLOC, L], BF16)
    y_ag = nc.dram_tensor("y_ag", [4, CLOC, L], BF16)
    u_loc = nc.dram_tensor("u_loc", [4, CLOC, TLOC], BF16)
    y_loc = nc.dram_tensor("y_loc", [4, CLOC, TLOC], BF16)
    xn3_scr = nc.dram_tensor("xn3_scr", [D, TLOC], BF16)

    def arena_view(r, key):
        rows, cols = SZ[key]
        return arena.ap()[r, OFF[key]:OFF[key] + rows * cols].rearrange(
            "(r c) -> r c", c=cols)

    SB_BASE = 16512
    ga = SbAlloc(nc, SB_BASE)
    s_idb = ga.t("s_idb", [128, 128], BF16)
    s_idf = ga.t("s_idf", [128, 128], F32)
    s_nw = ga.t("s_nw", [128, 16], F32)
    s_xpTb = ga.t("s_xpTb", [128, 32 * 33], BF16)
    s_gwT = ga.t("s_gwT", [128, 16 * 8], F32)
    s_cw = ga.t("s_cw", [128, 32 * KC], F32)
    s_cb = ga.t("s_cb", [128, 32], F32)
    s_al = ga.t("s_al", [128, 8 * NST], F32)
    s_dtw = ga.t("s_dtw", [128, 8], F32)
    s_dtb = ga.t("s_dtb", [128, 8], F32)
    s_dpr = ga.t("s_dpr", [128, 8], F32)
    s_mv = ga.t("s_mv", [128, 8], F32)
    s_comb = ga.t("s_comb", [128, 32 * 8], F32)
    s_cols = ga.t("s_cols", [128, 32], F32)
    s_kon = ga.t("s_kon", [128, 2], F32)
    s_ones = ga.t("s_ones", [65, 128], BF16)
    GEN_END = ga.off

    p0 = SbAlloc(nc, GEN_END)
    p0_ld = [p0.t(f"p0_ld{i}", [128, 2048], F32) for i in range(2)]
    p0_bf = [p0.t(f"p0_bf{i}", [128, 2048], BF16) for i in range(2)]
    p0_ev = [p0.t(f"p0_ev{i}", [128, 128], BF16) for i in range(2)]
    P0_END = p0.off

    p1a = SbAlloc(nc, P0_END)
    s_xn2T = p1a.t("s_xn2T", [128, 16 * 4099], BF16)
    s_xld = [p1a.t(f"s_xld{i}", [128, 2048], F32) for i in range(2)]
    s_sq = p1a.t("s_sq", [128, 2048], F32)
    s_xn2 = p1a.t("s_xn2", [128, 2048], BF16)
    P1A_END = p1a.off
    _sxld_off = P0_END + ((16 * 4099 * 2 + 63) // 64) * 64
    _ssq_off = _sxld_off + 2 * 8192
    _sxn2_off = _ssq_off + 8192

    p1b = SbAlloc(nc, GEN_END)
    s_win = [p1b.t(f"s_win{i}", [128, 16 * 128], BF16) for i in range(2)]
    s_xin = p1b.t("s_xin", [128, 2052], F32)
    assert p1b.off <= P0_END, (p1b.off, P0_END)
    s_xs = nc.alloc_sbuf_tensor_at("s_xs", [33, 2 * 2048], F32, offset=_sxld_off)
    s_cv = nc.alloc_sbuf_tensor_at("s_cv", [128, 2048], F32, offset=_ssq_off)
    s_u = nc.alloc_sbuf_tensor_at("s_u", [128, 2048], BF16, offset=_sxn2_off)
    p1t_ = SbAlloc(nc, P1A_END)
    s_zbf = p1t_.t("s_zbf", [128, 2048], BF16)
    s_xs_bf = p1t_.t("s_xs_bf", [33, 2 * 2048], BF16)
    P1_END = p1t_.off

    p2 = SbAlloc(nc, GEN_END)
    s_Ball = [p2.t(f"s_Ball{i}", [128, NST * 1024], BF16) for i in range(2)]
    s_Call = [p2.t(f"s_Call{i}", [128, NST * 1024], BF16) for i in range(2)]
    s_rbc = [p2.t(f"s_rbc{i}", [128, 1024], BF16) for i in range(2)]
    s_brow = p2.t("s_brow", [65, 11 * 1024], BF16)
    s_u2 = [p2.t(f"s_u2{i}", [128, 1024], BF16) for i in range(2)]
    s_arg = p2.t("s_arg", [128, 1024], F32)
    s_t2 = p2.t("s_t2", [128, 1024], F32)
    s_dt = p2.t("s_dt", [128, 1024], F32)
    s_w2 = p2.t("s_w2", [128, 1024], BF16)
    s_a = [p2.t(f"s_a{i}", [128, 1024], F32) for i in range(2)]
    s_b = [p2.t(f"s_b{i}", [128, 1024], BF16) for i in range(2)]
    s_h = [p2.t(f"s_h{i}", [128, 1024], BF16) for i in range(2)]
    s_tmp = [p2.t(f"s_tmp{i}", [128, 1024], BF16) for i in range(2)]
    s_st = p2.t("s_st", [128, 8 * NST], F32)
    s_y2 = [p2.t(f"s_y2{i}", [128, 1024], BF16) for i in range(2)]
    P2_END = p2.off

    p3 = SbAlloc(nc, GEN_END)
    s_yg = [p3.t(f"s_yg{i}", [128, 512], BF16) for i in range(32)]
    s_sz = p3.t("s_sz", [128, 512], F32)
    s_zt = [p3.t(f"s_zt{i}", [128, 512], BF16) for i in range(2)]
    s_wo = [p3.t(f"s_wo{i}", [128, 512], BF16) for i in range(32)]
    s_x2 = [p3.t(f"s_x2{i}", [128, 2048], F32) for i in range(4)]
    s_xn3 = p3.t("s_xn3", [128, 2048], F32)
    s_x3f = p3.t("s_x3f", [128, 16 * 128], F32)
    s_x3b = p3.t("s_x3b", [128, 16 * 512], BF16)
    s_sc8 = p3.t("s_sc8", [128, 8], F32)
    s_sc8b = p3.t("s_sc8b", [128, 8], F32)
    s_m8 = p3.t("s_m8", [128, 8], F32)
    s_mr = p3.t("s_mr", [128, 8], F32)
    P3A_END = p3.off
    assert P3A_END <= _sxld_off, (P3A_END, _sxld_off)

    p3b = SbAlloc(nc, GEN_END)
    s_w1e = [p3b.t("s_w1e", [128, 16 * 1024], BF16)]
    s_w3e = p3b.t("s_w3e", [128, 16 * 1024], BF16)
    s_w2e = p3b.t("s_w2e", [128, 8 * 2048], BF16)
    s_xt = [p3b.t(f"s_xt{i}", [128, 16 * 512], BF16) for i in range(2)]
    s_g = [p3b.t(f"s_g{i}", [128, 512], BF16) for i in range(8)]
    s_sh = [p3b.t(f"s_sh{i}", [128, 512], F32) for i in range(2)]
    s_ye = p3b.t("s_ye", [128, 2048], F32)
    P3B_END = p3b.off

    assert max(P1A_END, P1_END, P2_END, P3A_END, P3B_END) <= 229300, (
        P0_END, P1A_END, P1_END, P2_END, P3A_END, P3B_END)

    pA = nc.alloc_psum_tensor("pA", [128, 512], F32)
    pB = nc.alloc_psum_tensor("pB", [128, 512], F32)
    pC = nc.alloc_psum_tensor("pC", [128, 512], F32)
    pD = nc.alloc_psum_tensor("pD", [128, 512], F32)
    pE1 = nc.alloc_psum_tensor("pE1", [128, 1024], F32)
    pE2 = nc.alloc_psum_tensor("pE2", [128, 1024], F32)
    pTR2 = [pC.bitcast(BF16)[:, 0:128], pD.bitcast(BF16)[:, 0:128]]

    pg = Prog()
    global _LAST_PG
    _LAST_PG = pg
    L_ = pg.last

    semnames = [
        "smalls", "p0ld", "p0v", "p0t", "p0st", "cwag",
        "xld0", "xld1", "p1v", "p1s", "p1t", "winld0", "winld1",
        "uzst", "cuag", "cuagB", "cxag",
        "uld0", "uld1", "brow", "bcev",
        "p2v", "p2s", "p2g", "p2t", "yst", "cyag", "cyagB",
        "yld0", "yld1", "zld0", "zld1", "wold0", "wold1",
        "p3v", "p3s", "p3t", "xnst", "xnld0", "xnld1",
        "wexp", "wexpg", "outst", "oacc0", "oacc1", "oacc2", "oacc3",
        "ucp0", "ucp1", "ucp2", "ucp3", "ycp0", "ycp1", "ycp2", "ycp3",
    ]

    _regs = {}

    def qdyn(eng, name):
        if name not in _regs:
            _regs[name] = eng.compute_val(eng.partition_id() % 4)
        return _regs[name]

    for j in range(16):
        pg.add("sync", (lambda j=j: lambda e: e.dma_start(
            out=s_nw[:, j:j + 1],
            in_=nw_full.ap()[128 * j:128 * (j + 1)].unsqueeze(1)))(),
            inc=("smalls", 16))
    pg.add("sync", lambda e: e.dma_start(out=s_idb[:], in_=ident_bf.ap()),
           inc=("smalls", 16))
    pg.add("sync", lambda e: e.dma_start(out=s_idf[:], in_=ident_f.ap()),
           inc=("smalls", 16))
    for ci in range(32):
        pg.add("sync", (lambda ci=ci: lambda e: e.dma_start(
            out=s_cw[:, ci * KC:(ci + 1) * KC],
            in_=cw_full.ap()[128 * ci:128 * (ci + 1), :]))(), inc=("smalls", 16))
        pg.add("sync", (lambda ci=ci: lambda e: e.dma_start(
            out=s_cb[:, ci:ci + 1],
            in_=cb_full.ap()[128 * ci:128 * (ci + 1)].unsqueeze(1)))(),
            inc=("smalls", 16))
    for ct in range(8):
        pg.add("sync", (lambda ct=ct: lambda e: e.dma_start(
            out=s_al[:, ct * NST:(ct + 1) * NST],
            in_=alog_sl.ap()[128 * ct:128 * (ct + 1), :]))(), inc=("smalls", 16))
        for (dst, src) in ((s_dtw, dtw_sl), (s_dtb, dtb_sl), (s_dpr, dpar_sl)):
            pg.add("sync", (lambda ct=ct, dst=dst, src=src: lambda e: e.dma_start(
                out=dst[:, ct:ct + 1],
                in_=src.ap()[128 * ct:128 * (ct + 1)].unsqueeze(1)))(),
                inc=("smalls", 16))
    SM = pg.val("smalls")
    pg.add("vector", lambda e: e.memset(s_kon[:, 0:1], EPS), inc=("p1v", 1))
    pg.add("vector", lambda e: e.memset(s_ones[:], 1.0), inc=("p1v", 1))
    ONES_V = pg.val("p1v")
    pg.add("vector", lambda e: e.memset(s_kon[:, 1:2], -0.5), inc=("p1v", 1))
    pg.add("vector", lambda e: e.memset(s_mv[:], -1e30), inc=("p3v", 1))
    pg.add("scalar", lambda e: e.activation(s_al[:], s_al[:], AF.Exp),
           wait=[("smalls", SM)], inc=("p2s", 1))
    pg.add("vector", lambda e: e.tensor_scalar(s_al[:], s_al[:], -1.0, None,
                                               ALU.mult),
           wait=[L_("p2s")], inc=("p2v", 1))
    ANEG_V = pg.val("p2v")

    p0n = dict(tr=0)
    ev_marks = [0, 0]
    slot_marks = [0, 0]

    def prep_weight(src_ap, key, scale_nw=False):
        rows, cols = SZ[key][1], SZ[key][0]
        rows, cols = src_ap.shape
        for rt in range(rows // 128 if rows >= 128 else 1):
            nr = min(128, rows)
            lb, bb = p0_ld[rt % 2], p0_bf[rt % 2]
            pg.add("sync", (lambda rt=rt, lb=lb, nr=nr, cols=cols: lambda e: e.dma_start(
                out=lb[0:nr, :cols], in_=src_ap[128 * rt:128 * rt + nr, :]))(),
                wait=[L_("p0v")], inc=("p0ld", 16))
            pg.add("vector", (lambda lb=lb, bb=bb, nr=nr, cols=cols: lambda e: e.tensor_copy(
                bb[0:nr, :cols], lb[0:nr, :cols]))(),
                wait=[L_("p0ld")], inc=("p0v", 1))
            castv = pg.val("p0v")
            for jt in range(cols // 128):
                slot = p0n["tr"] % 2
                p0n["tr"] += 1
                w = [("p0v", castv), ("smalls", SM), ("p0v", slot_marks[slot])]
                pg.add("tensor", (lambda bb=bb, jt=jt, slot=slot: lambda e: e.transpose(
                    pTR2[slot], bb[:, jt * 128:(jt + 1) * 128], s_idb[:]))(),
                    wait=w, inc=("p0t", 1))
                ev = p0_ev[p0n["tr"] % 2]
                wv = [L_("p0t"), L_("p0st")]
                if scale_nw:
                    pg.add("vector", (lambda ev=ev, slot=slot, jt=jt: lambda e: e.tensor_scalar(
                        ev[:], pTR2[slot], s_nw[:, jt:jt + 1], None, ALU.mult))(),
                        wait=wv, inc=("p0v", 1))
                else:
                    pg.add("vector", (lambda ev=ev, slot=slot: lambda e: e.tensor_copy(
                        ev[:], pTR2[slot]))(),
                        wait=wv, inc=("p0v", 1))
                slot_marks[slot] = pg.val("p0v")
                if key == "win":
                    dst = arena_in.ap()[
                        OFF[key] + rt * (128 * 2048):
                        OFF[key] + (rt + 1) * (128 * 2048)].rearrange(
                        "(p t j) -> p t j", p=128, t=16)[:, jt, :]
                else:
                    dst = arena_in.ap()[OFF[key]:OFF[key] + cols * rows].rearrange(
                        "(c r) -> c r", r=rows)[jt * 128:(jt + 1) * 128,
                                                128 * rt:128 * rt + nr]
                pg.add("sync", (lambda dst=dst, ev=ev, nr=nr: lambda e: e.dma_start(
                    out=dst, in_=ev[:, 0:nr]))(),
                    wait=[L_("p0v")], inc=("p0st", 16))
                ev_marks[p0n["tr"] % 2] = pg.val("p0st")

    pg.cuts = {}
    pg.cuts["sm"] = {e: len(pg.seq[e]) for e in ENGS}
    prep_weight(ip_sl.ap(), "win", scale_nw=True)
    pg.cuts["w_ip"] = {e: len(pg.seq[e]) for e in ENGS}
    prep_weight(w1_sl.ap(), "w1")
    prep_weight(w3_sl.ap(), "w3")
    prep_weight(w2_sl.ap(), "w2")
    prep_weight(wo_sl.ap(), "wo")
    prep_weight(ws1_sl.ap(), "ws1")
    prep_weight(ws3_sl.ap(), "ws3")
    prep_weight(ws2_sl.ap(), "ws2")

    pg.cuts["p0w"] = {e: len(pg.seq[e]) for e in ENGS}
    pg.add("gpsimd", lambda e: e.collective_compute(
        "AllGather", ALU.bypass, replica_groups=GROUPS8,
        ins=[arena_in.ap().opt()], outs=[arena.ap().opt()]),
        wait=[L_("p0st")], inc=("cwag", 1))
    pg.cuts["p0ag"] = {e: len(pg.seq[e]) for e in ENGS}

    pg.add("sync", lambda e: e.dma_start(out=p0_ld[0][0:8, :], in_=gw_full.ap()),
           wait=[L_("p0v")], inc=("p0ld", 16))
    gldv = pg.val("p0ld")
    for jt in range(16):
        pg.add("tensor", (lambda jt=jt: lambda e: e.transpose(
            pE2[:, 0:128], p0_ld[0][:, jt * 128:(jt + 1) * 128], s_idf[:]))(),
            wait=[("p0ld", gldv), ("smalls", SM), L_("p0v")], inc=("p0t", 1))
        pg.add("vector", (lambda jt=jt: lambda e: e.tensor_copy(
            s_gwT[:, jt * 8:(jt + 1) * 8], pE2[:, 0:8]))(),
            wait=[L_("p0t")], inc=("p0v", 1))
    for half in range(2):
        pg.add("sync", (lambda half=half: lambda e: e.dma_start(
            out=p0_ld[1][0:33, :],
            in_=xp_full.ap()[:, half * 2048:(half + 1) * 2048]))(),
            wait=[L_("p0v")], inc=("p0ld", 16))
        xldv = pg.val("p0ld")
        for jt in range(16):
            ci = half * 16 + jt
            pg.add("tensor", (lambda jt=jt: lambda e: e.transpose(
                pE2[:, 0:128], p0_ld[1][:, jt * 128:(jt + 1) * 128], s_idf[:]))(),
                wait=[("p0ld", xldv), L_("p0v")], inc=("p0t", 1))
            pg.add("vector", (lambda ci=ci: lambda e: e.tensor_copy(
                s_xpTb[:, ci * 33:(ci + 1) * 33], pE2[:, 0:33]))(),
                wait=[L_("p0t")], inc=("p0v", 1))
    P0V_DONE = pg.val("p0v")
    P0T_DONE = pg.val("p0t")

    pg.cuts["p0"] = {e: len(pg.seq[e]) for e in ENGS}

    pg.add("vector", lambda e: e.memset(s_xld[0][:], 0.0), inc=("p1v", 1))
    pg.add("vector", lambda e: e.memset(s_xld[1][:], 0.0), inc=("p1v", 1))
    XLD_INIT = pg.val("p1v")

    xmark = {}
    sqmark = 0
    p1slot = [0, 0]

    for ti in range(33):
        r0 = 0 if ti == 0 else 3 + 128 * (ti - 1)
        nr = 3 if ti == 0 else 128
        xb = s_xld[ti % 2]
        w = []
        if ti >= 2:
            w.append(("p1v", xmark[ti - 2]))
        xsem = f"xld{ti % 2}"
        if ti < 2:
            w.append(("p1v", XLD_INIT))
        pg.add("sync", (lambda xb=xb, r0=r0, nr=nr: lambda e: e.dma_start(
            out=xb[0:nr, :], in_=x_loc.ap()[r0:r0 + nr, :]))(),
            wait=w, inc=(xsem, 16))
        xv = pg.val(xsem)
        pg.add("vector", (lambda xb=xb: lambda e: e.reduce_sum(
            s_cols[:, 0:1], xb[:], axis=AX))(),
            wait=[(xsem, xv)], inc=("p1v", 1))
        pg.add("scalar", (lambda xb=xb: lambda e: e.activation(
            s_sq[:], xb[:], AF.Square))(),
            wait=[(xsem, xv), ("p1v", sqmark)], inc=("p1s", 1))
        pg.add("vector", lambda e: e.reduce_sum(s_cols[:, 1:2], s_sq[:], axis=AX),
               wait=[L_("p1s")], inc=("p1v", 1))
        sqmark = pg.val("p1v")

        pg.add("vector", lambda e: e.tensor_scalar(s_cols[:, 2:3], s_cols[:, 0:1], 1.0 / D, None, ALU.mult), inc=("p1v", 1))
        pg.add("vector", lambda e: e.tensor_scalar(s_cols[:, 3:4], s_cols[:, 1:2], 1.0 / D, None, ALU.mult), inc=("p1v", 1))
        pg.add("vector", lambda e: e.tensor_tensor(s_cols[:, 4:5], s_cols[:, 2:3], s_cols[:, 2:3], ALU.mult), inc=("p1v", 1))
        pg.add("vector", lambda e: e.tensor_tensor(s_cols[:, 3:4], s_cols[:, 3:4], s_cols[:, 4:5], ALU.subtract), inc=("p1v", 1))
        pg.add("scalar", lambda e: e.activation(s_cols[:, 5:6], s_cols[:, 3:4],
                                                AF.Ln, bias=s_kon[:, 0:1]),
               wait=[L_("p1v")], inc=("p1s", 1))
        pg.add("scalar", lambda e: e.activation(s_cols[:, 6:7], s_cols[:, 5:6],
                                                AF.Exp, scale=s_kon[:, 1:2]), inc=("p1s", 1))
        pg.add("scalar", lambda e: e.activation(s_cols[:, 7:8], s_cols[:, 5:6],
                                                AF.Exp, scale=-1.0), inc=("p1s", 1))
        pg.add("vector", lambda e: e.tensor_tensor(s_cols[:, 8:9], s_cols[:, 3:4],
                                                   s_cols[:, 7:8], ALU.mult),
               wait=[L_("p1s")], inc=("p1v", 1))
        pg.add("scalar", lambda e: e.activation(s_cols[:, 9:10], s_cols[:, 8:9],
                                                AF.Ln, bias=s_kon[:, 0:1]),
               wait=[L_("p1v")], inc=("p1s", 1))
        pg.add("scalar", lambda e: e.activation(s_cols[:, 10:11], s_cols[:, 9:10],
                                                AF.Exp, scale=s_kon[:, 1:2]), inc=("p1s", 1))

        pg.add("vector", lambda e: e.tensor_tensor(s_cols[:, 11:12], s_cols[:, 6:7], s_cols[:, 10:11], ALU.mult), wait=[L_("p1s")], inc=("p1v", 1))
        pg.add("vector", lambda e: e.tensor_tensor(s_cols[:, 12:13], s_cols[:, 2:3], s_cols[:, 11:12], ALU.mult), inc=("p1v", 1))
        pg.add("vector", lambda e: e.tensor_scalar(s_cols[:, 12:13], s_cols[:, 12:13], -1.0, None, ALU.mult), inc=("p1v", 1))
        pg.add("vector", (lambda xb=xb: lambda e: e.scalar_tensor_tensor(
            s_xn2[:], xb[:], s_cols[:, 11:12],
            s_cols[:, 12:13].to_broadcast([128, 2048]),
            ALU.mult, ALU.add))(),
            wait=[L_("p1v")], inc=("p1v", 1))
        xmark[ti] = pg.val("p1v")
        xnv = pg.val("p1v")
        c0c = 0 if ti == 0 else 3 + 128 * (ti - 1)
        ncc = 3 if ti == 0 else 128
        for dt in range(16):
            sl = (dt + 16 * ti) % 2
            pg.add("tensor", (lambda dt=dt, sl=sl: lambda e: e.transpose(
                pTR2[sl], s_xn2[:, dt * 128:(dt + 1) * 128], s_idb[:]))(),
                wait=[("p1v", xnv), ("p1v", p1slot[sl]), ("smalls", SM),
                      ("p0v", P0V_DONE), ("p0t", P0T_DONE)],
                inc=("p1t", 1))
            pg.add("vector", (lambda dt=dt, sl=sl, c0c=c0c, ncc=ncc: lambda e: e.tensor_copy(
                s_xn2T[:, dt * 4099 + c0c:dt * 4099 + c0c + ncc],
                pTR2[sl][:, 0:ncc]))(),
                wait=[L_("p1t")], inc=("p1v", 1))
            p1slot[sl] = pg.val("p1v")
    LNT_V = pg.val("p1v")
    pg.cuts["p1a"] = {e: len(pg.seq[e]) for e in ENGS}

    pg.add("vector", lambda e: e.memset(s_xs[:], 0.0),
           wait=[("p1v", LNT_V)], inc=("p1v", 1))
    XSMEM_V = pg.val("p1v")
    win_marks = [0, 0]
    psum_marks = [0, 0, 0, 0]
    xin_mark = 0
    zbf_mark = 0
    u_mark = 0
    u_st_mark = 0
    mmctr = 0
    UZ_SB0 = None
    for sb in range(2):
        for ci in range(64):
            is_z = ci >= 32
            wb = s_win[ci % 2]
            rk = ci // 8
            w0 = [("cwag", 1), ("p1t", win_marks[ci % 2]),
                  ("p0v", P0V_DONE), ("p0t", P0T_DONE)]
            wsem = f"winld{ci % 2}"
            cb = ci % 8
            wv_t = arena.ap()[rk, OFF["win"] + cb * (128 * 2048):
                              OFF["win"] + (cb + 1) * (128 * 2048)].rearrange(
                "(p f) -> p f", p=128)
            pg.add("sync", (lambda wb=wb, wv_t=wv_t: lambda e: e.dma_start(
                out=wb[:], in_=wv_t))(),
                wait=w0, inc=(wsem, 16))
            wldv = pg.val(wsem)
            chunks = ([(0, 512), (512, 512), (1024, 512), (1536, 512), (2048, 3)]
                      if not is_z else
                      [(3, 512), (515, 512), (1027, 512), (1539, 512)])
            for (co, cw_) in chunks:
                pidx = mmctr % 4
                ps = [pA, pB, pE2[:, 0:512], pE2[:, 512:1024]][pidx]
                mmctr += 1
                base = sb * 2048
                for dt in range(16):
                    w = []
                    if dt == 0:
                        w = [(wsem, wldv), ("p1v", LNT_V),
                             ("p1v", psum_marks[pidx])]
                    pg.add("tensor", (lambda ps=ps, wb=wb, dt=dt, base=base, co=co, cw_=cw_: lambda e: e.matmul(
                        ps[:, 0:cw_], wb[:, dt * 128:(dt + 1) * 128],
                        s_xn2T[:, dt * 4099 + base + co:dt * 4099 + base + co + cw_],
                        start=(dt == 0), stop=(dt == 15)))(),
                        wait=w, inc=("p1t", 1) if dt == 15 else None)
                mmd = pg.val("p1t")
                if not is_z:
                    pg.add("vector", (lambda ps=ps, co=co, cw_=cw_: lambda e: e.tensor_copy(
                        s_xin[:, co:co + cw_], ps[:, 0:cw_]))(),
                        wait=[("p1t", mmd), ("p1s", xin_mark)], inc=("p1v", 1))
                else:
                    pg.add("vector", (lambda ps=ps, co=co, cw_=cw_: lambda e: e.tensor_copy(
                        s_zbf[:, co - 3:co - 3 + cw_], ps[:, 0:cw_]))(),
                        wait=[("p1t", mmd), L_("uzst")], inc=("p1v", 1))
                psum_marks[pidx] = pg.val("p1v")
            win_marks[ci % 2] = pg.val("p1t")
            evd = pg.val("p1v")
            pg.cuts[f"p1b_{sb}_{ci}_mm"] = {e: len(pg.seq[e]) for e in ENGS}
            if is_z:
                zi = ci - 32
                pg.add("sync", (lambda zi=zi, sb=sb: lambda e: e.dma_start(
                    out=z_scr.ap()[zi * 128:(zi + 1) * 128,
                                   sb * 2048:(sb + 1) * 2048],
                    in_=s_zbf[:]))(),
                    wait=[("p1v", evd)], inc=("uzst", 16))
                zbf_mark = pg.val("uzst")
            else:
                pg.add("vector", (lambda ci=ci: lambda e: e.tensor_scalar(
                    s_cv[:], s_xin[:, 0:2048], s_cw[:, ci * KC:ci * KC + 1],
                    None, ALU.mult))(),
                    wait=[("p1v", evd), ("p1s", xin_mark)], inc=("p1v", 1))
                for k in range(1, 4):
                    pg.add("vector", (lambda ci=ci, k=k: lambda e: e.scalar_tensor_tensor(
                        s_cv[:], s_xin[:, k:k + 2048],
                        s_cw[:, ci * KC + k:ci * KC + k + 1],
                        s_cv[:], ALU.mult, ALU.add))(), inc=("p1v", 1))
                cvd = pg.val("p1v")
                pg.add("scalar", (lambda ci=ci: lambda e: e.activation(
                    s_u[:], s_cv[:], AF.Silu, bias=s_cb[:, ci:ci + 1]))(),
                    wait=[("p1v", cvd), ("p1t", u_mark), L_("uzst")],
                    inc=("p1s", 1))
                xin_mark = pg.val("p1s")
                uv = pg.val("p1s")
                u_dst = u_in0 if sb == 0 else u_in1
                pg.add("sync", (lambda ci=ci, u_dst=u_dst: lambda e: e.dma_start(
                    out=u_dst.ap()[ci * 128:(ci + 1) * 128, :],
                    in_=s_u[:]))(),
                    wait=[("p1s", uv)], inc=("uzst", 16))
                u_st_mark = pg.val("uzst")
                for q in range(4):
                    w = [("p1s", uv), ("p0v", P0V_DONE)]
                    if q > 0:
                        w = [L_("p1v")]
                    pg.add("tensor", (lambda ci=ci, q=q: lambda e: e.matmul(
                        pE1[0:33, 0:512], s_xpTb[:, ci * 33:(ci + 1) * 33],
                        s_u[:, q * 512:(q + 1) * 512], start=True, stop=True))(),
                        wait=w, inc=("p1t", 1))
                    pg.add("vector", (lambda sb=sb, q=q: lambda e: e.tensor_tensor(
                        s_xs[:, sb * 2048 + q * 512:sb * 2048 + (q + 1) * 512],
                        s_xs[:, sb * 2048 + q * 512:sb * 2048 + (q + 1) * 512],
                        pE1[0:33, 0:512], ALU.add))(),
                        wait=[L_("p1t"), ("p1v", XSMEM_V)], inc=("p1v", 1))
                u_mark = pg.val("p1t")
            pg.cuts[f"p1b_{sb}_{ci}"] = {e: len(pg.seq[e]) for e in ENGS}
        if sb == 0:
            UZ_SB0 = pg.val("uzst")

    pg.add("vector", lambda e: e.tensor_copy(s_xs_bf[:], s_xs[:]),
           wait=[L_("p1v")], inc=("p1v", 1))
    pg.add("sync", lambda e: e.dma_start(out=xs_in.ap(), in_=s_xs_bf[:]),
           wait=[L_("p1v")], inc=("uzst", 16))
    UZ = pg.val("uzst")

    pg.add("gpsimd", lambda e: e.collective_compute(
        "AllGather", ALU.bypass, replica_groups=GROUPS4,
        ins=[u_in0.ap().opt()], outs=[u_ag2.ap().opt()]),
        wait=[("uzst", UZ_SB0)], inc=("cuag", 1))
    pg.add("gpsimd", lambda e: e.collective_compute(
        "AllGather", ALU.bypass, replica_groups=GROUPS4,
        ins=[u_in1.ap().opt()], outs=[u_ag3.ap().opt()]),
        wait=[("uzst", UZ)], inc=("cuagB", 1))
    pg.add("gpsimd", lambda e: e.collective_compute(
        "AllGather", ALU.bypass, replica_groups=GROUPS4,
        ins=[xs_in.ap().opt()], outs=[xs_ag.ap().opt()]),
        wait=[("uzst", UZ)], inc=("cxag", 1))

    pg.cuts["p1"] = {e: len(pg.seq[e]) for e in ENGS}

    pg.add("vector", lambda e: e.memset(s_st[:], 0.0),
           wait=[("cxag", 1)], inc=("p2v", 1))
    STMEM_V = pg.val("p2v")

    xs_v = xs_ag.ap()
    u_v2 = u_ag2.ap().rearrange("g (q c) t -> g q c t", q=4, c=CLOC)
    u_v3 = u_ag3.ap().rearrange("g (q c) t -> g q c t", q=4, c=CLOC)
    for rb in range(4):
        pg.add("gpsimd", (lambda rb=rb: lambda e: e.dma_start(
            out=u_loc.ap()[rb][:, 0:2048],
            in_=u_v2[rb][bass.ts(qdyn(e, "gpsimd"), 1)].squeeze(0)))(),
            wait=[("cuag", 1)], inc=(f"ucp{rb}", 16))
    for rb in range(4):
        pg.add("gpsimd", (lambda rb=rb: lambda e: e.dma_start(
            out=u_loc.ap()[rb][:, 2048:4096],
            in_=u_v3[rb][bass.ts(qdyn(e, "gpsimd"), 1)].squeeze(0)))(),
            wait=[("cuagB", 1)], inc=(f"ucp{rb}", 16))

    st_mark = {}
    bc_marks = [0, 0]
    bcg_marks = [0, 0]
    u2_marks = [0, 0]
    pe_marks = [0, 0]
    y2_marks = [0, 0]
    abht_marks = {"a": [0, 0], "b": [0, 0], "h": [0, 0], "tmp": [0, 0]}

    bc_ps = [pA, pB, pC, pD]
    bc_ps_vm = [0, 0, 0, 0]
    bc_ps_sm = [0, 0, 0, 0]
    browm = 0
    for tt in range(16):
        bb = tt % 2
        r_blk, toff = (tt * 1024) // TLOC, (tt * 1024) % TLOC
        w = [("cxag", 1), ("p2t", browm)]
        for m3 in range(3):
            nrows = 11 if m3 != 2 else 11
            pg.add("sync", (lambda m3=m3, r_blk=r_blk, toff=toff: lambda e: e.dma_start(
                out=s_brow[32 * m3:32 * m3 + 1, 0:11 * 1024].rearrange(
                    "p (r t) -> p r t", t=1024),
                in_=xs_v[r_blk, m3:33:3, toff:toff + 1024].unsqueeze(0)))(),
                wait=w if m3 == 0 else [], inc=("brow", 16))
        brv = pg.val("brow")
        nev_v = 0
        nev_s = 0
        for row in range(33):
            m3, slot = row % 3, row // 3
            if row < NST:
                dsts = [(s_Ball[bb], row)]
            elif row < 2 * NST:
                dsts = [(s_Call[bb], row - NST)]
            else:
                dsts = [(s_rbc[bb], 0)]
            for q in range(2):
                pi = (tt * 66 + row * 2 + q) % 4
                ps = bc_ps[pi]
                pg.add("tensor", (lambda m3=m3, slot=slot, q=q, ps=ps: lambda e: e.matmul(
                    ps[:, 0:512], s_ones[32 * m3:32 * m3 + 1, :],
                    s_brow[32 * m3:32 * m3 + 1,
                           slot * 1024 + q * 512:slot * 1024 + (q + 1) * 512],
                    start=True, stop=True))(),
                    wait=[("brow", brv), ("p1v", ONES_V),
                          ("p2v", bc_ps_vm[pi]), ("p2s", bc_ps_sm[pi])],
                    inc=("p2t", 1))
                mmv = pg.val("p2t")
                dst, drow = dsts[0]
                if row < NST:
                    pg.add("vector", (lambda dst=dst, drow=drow, q=q, ps=ps: lambda e: e.tensor_copy(
                        dst[:, drow * 1024 + q * 512:drow * 1024 + (q + 1) * 512],
                        ps[:, 0:512]))(),
                        wait=[("p2t", mmv), ("p2v", bc_marks[bb] if row == 0 and q == 0 else 0)],
                        inc=("p2v", 1))
                    bc_ps_vm[pi] = pg.val("p2v")
                    nev_v += 1
                else:
                    pg.add("scalar", (lambda dst=dst, drow=drow, q=q, ps=ps: lambda e: e.copy(
                        dst[:, drow * 1024 + q * 512:drow * 1024 + (q + 1) * 512],
                        ps[:, 0:512]))(),
                        wait=[("p2t", mmv), ("p2s", bcg_marks[bb] if row == NST and q == 0 else 0)],
                        inc=("p2s", 1))
                    bc_ps_sm[pi] = pg.val("p2s")
                    nev_s += 1
        browm = pg.val("p2t")
        bBv, bCv = pg.val("p2v"), pg.val("p2s")
        pg.add("vector", lambda e: e.nop(),
               wait=[("p2s", bCv)], inc=("p2v", 1))
        pg.add("gpsimd", lambda e: e.nop(),
               wait=[("p2v", bBv), ("p2s", bCv)], inc=("p2g", 1))
        for ct in range(8):
            ub = s_u2[ct % 2]
            usem = f"uld{ct % 2}"
            pg.add("sync", (lambda ub=ub, r_blk=r_blk, toff=toff, ct=ct: lambda e: e.dma_start(
                out=ub[:],
                in_=u_loc.ap()[r_blk][128 * ct:128 * (ct + 1),
                                      toff:toff + 1024]))(),
                wait=[("p2v", u2_marks[ct % 2]), (f"ucp{r_blk}", 32)],
                inc=(usem, 16))
            uldv = pg.val(usem)
            pg.add("vector", (lambda bb=bb, ct=ct: lambda e: e.scalar_tensor_tensor(
                s_arg[:], s_rbc[bb][:], s_dtw[:, ct:ct + 1],
                s_dtb[:, ct:ct + 1].to_broadcast([128, 1024]),
                ALU.mult, ALU.add))(),
                wait=[L_("p2s"), ("smalls", SM)],
                inc=("p2v", 1))
            pg.add("scalar", lambda e: e.nop(),
                   wait=[L_("p2v")], inc=("p2s", 1))
            pg.add("scalar", lambda e: e.activation(s_t2[:], s_arg[:], AF.Abs),
                   inc=("p2s", 1))
            pg.add("scalar", lambda e: e.activation(s_t2[:], s_t2[:], AF.Exp,
                                                    scale=-1.0), inc=("p2s", 1))
            pg.add("scalar", lambda e: e.activation(s_t2[:], s_t2[:], AF.Ln,
                                                    bias=1.0), inc=("p2s", 1))
            dtsv = pg.val("p2s")
            pg.add("vector", lambda e: e.tensor_scalar(s_dt[:], s_arg[:], 0.0,
                                                       None, ALU.max),
                   wait=[("p2s", dtsv)], inc=("p2v", 1))
            pg.add("vector", lambda e: e.tensor_tensor(s_dt[:], s_dt[:], s_t2[:],
                                                       ALU.add), inc=("p2v", 1))
            pg.add("vector", (lambda ub=ub: lambda e: e.tensor_tensor(
                s_w2[:], s_dt[:], ub[:], ALU.mult))(),
                wait=[(usem, uldv), L_("p2g")], inc=("p2v", 1))
            w2v = pg.val("p2v")
            u2_marks[ct % 2] = w2v
            pidx = ct % 2
            ps = pE1 if pidx == 0 else pE2

            for n in range(NST):
                nb = n % 2
                ab, bbuf, hb, tb_ = s_a[nb], s_b[nb], s_h[nb], s_tmp[nb]
                pg.add("scalar", (lambda ab=ab, ct=ct, n=n: lambda e: e.activation(
                    ab[:], s_dt[:], AF.Exp,
                    scale=s_al[:, ct * NST + n:ct * NST + n + 1]))(),
                    wait=[("p2v", max(w2v, abht_marks["a"][nb]))],
                    inc=("p2s", 1))
                pg.add("vector", (lambda bbuf=bbuf, bb=bb, n=n: lambda e: e.tensor_tensor(
                    bbuf[:], s_w2[:], s_Ball[bb][:, n * 1024:(n + 1) * 1024],
                    ALU.mult))(),
                    inc=("p2v", 1))
                wscan = [L_("p2s"),
                         ("p2t", abht_marks["h"][nb]),
                         ("p2v", st_mark.get((ct, n), 0))]
                pg.add("vector", (lambda ab=ab, bbuf=bbuf, hb=hb, ct=ct, n=n: lambda e: e.tensor_tensor_scan(
                    hb[:], ab[:], bbuf[:],
                    s_st[:, ct * NST + n:ct * NST + n + 1], ALU.mult, ALU.add))(),
                    wait=wscan,
                    inc=("p2v", 1))
                scan_v = pg.val("p2v")
                abht_marks["a"][nb] = scan_v
                abht_marks["b"][nb] = scan_v
                pg.add("vector", (lambda hb=hb, ct=ct, n=n: lambda e: e.tensor_copy(
                    s_st[:, ct * NST + n:ct * NST + n + 1], hb[:, 1023:1024]))(),
                    inc=("p2v", 1))
                st_mark[(ct, n)] = pg.val("p2v")
                pg.add("vector", (lambda hb=hb, tb_=tb_, bb=bb, n=n: lambda e: e.tensor_tensor(
                    tb_[:], hb[:], s_Call[bb][:, n * 1024:(n + 1) * 1024],
                    ALU.mult))(),
                    wait=[("p2t", abht_marks["tmp"][nb])],
                    inc=("p2v", 1))
                tmpw = [L_("p2v")]
                for q in range(2):
                    w = list(tmpw)
                    if n == 0 and q == 0:
                        w.append(("p2v", pe_marks[pidx]))
                    pg.add("tensor", (lambda ps=ps, tb_=tb_, q=q, n=n: lambda e: e.matmul(
                        ps[:, q * 512:(q + 1) * 512], s_idb[:],
                        tb_[:, q * 512:(q + 1) * 512],
                        start=(n == 0), stop=(n == NST - 1)))(),
                        wait=w, inc=("p2t", 1))
                abht_marks["tmp"][nb] = pg.val("p2t")
                abht_marks["h"][nb] = pg.val("p2t")
            yb = s_y2[ct % 2]
            pg.add("vector", (lambda yb=yb, ub=ub, ps=ps, ct=ct: lambda e: e.scalar_tensor_tensor(
                yb[:], ub[:], s_dpr[:, ct:ct + 1], ps[:], ALU.mult, ALU.add))(),
                wait=[L_("p2t"), L_("yst")], inc=("p2v", 1))
            pe_marks[pidx] = pg.val("p2v")
            u2_marks[ct % 2] = pg.val("p2v")
            pg.add("scalar", (lambda yb=yb, ct=ct, tt=tt: lambda e: e.dma_start(
                out=y_in.ap()[ct * 128:(ct + 1) * 128,
                              tt * 1024:(tt + 1) * 1024], in_=yb[:]))(),
                wait=[L_("p2v")], inc=("yst", 16))
            y2_marks[ct % 2] = pg.val("yst")
        bc_marks[bb] = pg.val("p2v")
        bcg_marks[bb] = pg.val("p2g")
        if tt == 7:
            YST_H0 = pg.val("yst")

    YST = pg.val("yst")
    pg.add("gpsimd", lambda e: e.collective_compute(
        "AllGather", ALU.bypass, replica_groups=GROUPS4,
        ins=[y_in.ap().opt()], outs=[y_ag.ap().opt()]),
        wait=[("yst", YST)], inc=("cyag", 1))

    pg.cuts["p2"] = {e: len(pg.seq[e]) for e in ENGS}

    y_v = y_ag.ap().rearrange("g c (q t) -> g c q t", q=4, t=TLOC)
    for rb in range(4):
        pg.add("gpsimd", (lambda rb=rb: lambda e: e.dma_start(
            out=y_loc.ap()[rb], in_=y_v[rb][:, bass.ts(qdyn(e, "gpsimd"), 1)]
            .squeeze(1)))(),
            wait=[("cyag", 1)], inc=(f"ycp{rb}", 16))
    outst_total = 0
    xnst_total = 0
    x2_marks = {}
    yg_mark = 0
    wo_mark = 0
    zt_marks = [0, 0]
    xb_vmark = [0, 0]
    xb_smark = [0, 0]
    x2v_mark = 0
    x3f_mark = 0
    x3b_mark = 0
    sq3_mark = 0
    trslot_mark = [0, 0]

    for tb in range(8):
        ysem = f"yld{tb % 2}"
        for ci in range(32):
            rr, cc = ci // 8, (ci % 8) * 128
            w = []
            if ci % 8 == 0:
                w = [(f"ycp{rr}", 16)]
                if ci == 0:
                    w.append(("p3t", yg_mark))
            pg.add("sync", (lambda ci=ci, rr=rr, cc=cc, tb=tb: lambda e: e.dma_start(
                out=s_yg[ci][:],
                in_=y_loc.ap()[rr][cc:cc + 128, tb * 512:(tb + 1) * 512]))(),
                wait=w, inc=(ysem, 16))
        yldv = pg.val(ysem)
        for ci in range(32):
            zb = s_zt[ci % 2]
            zsem = f"zld{ci % 2}"
            pg.add("scalar", (lambda ci=ci, zb=zb, tb=tb: lambda e: e.dma_start(
                out=zb[:], in_=z_scr.ap()[ci * 128:(ci + 1) * 128,
                                          tb * 512:(tb + 1) * 512]))(),
                wait=[("p3v", zt_marks[ci % 2])], inc=(zsem, 16))
            pg.add("scalar", (lambda zb=zb: lambda e: e.activation(
                s_sz[:], zb[:], AF.Silu))(),
                wait=[L_(zsem), L_("p3v")], inc=("p3s", 1))
            pg.add("vector", (lambda ci=ci: lambda e: e.tensor_tensor(
                s_yg[ci][:], s_yg[ci][:], s_sz[:], ALU.mult))(),
                wait=[L_("p3s"), (ysem, yldv)], inc=("p3v", 1))
            zt_marks[ci % 2] = pg.val("p3v")
        ygv = pg.val("p3v")
        for tc in range(4):
            xb = s_xld[tc % 2]
            xsem = f"xld{tc % 2}"
            if tb == 0 and tc < 2:
                w = [("cyag", 1)]
            else:
                w = [("p3v", xb_vmark[tc % 2]), ("p3s", xb_smark[tc % 2])]
            pg.add("sync", (lambda xb=xb, tb=tb, tc=tc: lambda e: e.dma_start(
                out=xb[:], in_=x_loc.ap()[3 + tb * 512 + tc * 128:
                                          3 + tb * 512 + (tc + 1) * 128, :]))(),
                wait=w, inc=(xsem, 16))
            xv = pg.val(xsem)
            pg.add("vector", (lambda xb=xb: lambda e: e.reduce_sum(
                s_cols[:, 0:1], xb[:], axis=AX))(),
                wait=[(xsem, xv)], inc=("p3v", 1))
            pg.add("scalar", (lambda xb=xb: lambda e: e.activation(
                s_sq[:], xb[:], AF.Square))(),
                wait=[(xsem, xv), ("p3v", sq3_mark)], inc=("p3s", 1))
            xb_smark[tc % 2] = pg.val("p3s")
            pg.add("vector", lambda e: e.reduce_sum(s_cols[:, 1:2], s_sq[:], axis=AX),
                   wait=[L_("p3s")], inc=("p3v", 1))
            sq3_mark = pg.val("p3v")

            pg.add("vector", lambda e: e.tensor_scalar(s_cols[:, 2:3], s_cols[:, 0:1], 1.0 / D, None, ALU.mult), inc=("p3v", 1))
            pg.add("vector", lambda e: e.tensor_scalar(s_cols[:, 3:4], s_cols[:, 1:2], 1.0 / D, None, ALU.mult), inc=("p3v", 1))
            pg.add("vector", lambda e: e.tensor_tensor(s_cols[:, 4:5], s_cols[:, 2:3], s_cols[:, 2:3], ALU.mult), inc=("p3v", 1))
            pg.add("vector", lambda e: e.tensor_tensor(s_cols[:, 3:4], s_cols[:, 3:4], s_cols[:, 4:5], ALU.subtract), inc=("p3v", 1))
            pg.add("scalar", lambda e: e.activation(s_cols[:, 5:6], s_cols[:, 3:4],
                                                    AF.Ln, bias=s_kon[:, 0:1]),
                   wait=[L_("p3v")], inc=("p3s", 1))
            pg.add("scalar", lambda e: e.activation(s_cols[:, 6:7], s_cols[:, 5:6],
                                                    AF.Exp, scale=s_kon[:, 1:2]),
                   inc=("p3s", 1))

            pg.add("vector", lambda e: e.tensor_scalar(s_cols[:, 7:8], s_cols[:, 6:7], 1.0, None, ALU.add), wait=[L_("p3s")], inc=("p3v", 1))
            pg.add("vector", lambda e: e.tensor_tensor(s_cols[:, 8:9], s_cols[:, 2:3], s_cols[:, 6:7], ALU.mult), inc=("p3v", 1))
            pg.add("vector", lambda e: e.tensor_scalar(s_cols[:, 8:9], s_cols[:, 8:9], -1.0, None, ALU.mult), inc=("p3v", 1))
            pg.add("vector", (lambda xb=xb, tc=tc: lambda e: e.scalar_tensor_tensor(
                s_x2[tc][:], xb[:], s_cols[:, 7:8],
                s_cols[:, 8:9].to_broadcast([128, 2048]),
                ALU.mult, ALU.add))(),
                wait=[L_("p3v"), L_("outst")], inc=("p3v", 1))
            xb_vmark[tc % 2] = pg.val("p3v")
        resv = pg.val("p3v")
        for dc in range(4):
            w = [("cwag", 1), ("p3t", wo_mark)]
            osem2 = f"wold{dc % 2}"
            for ci in range(32):
                rr = (ci * 128) // 512
                ro = (ci * 128) % 512
                wv_ap = arena_view(rr, "wo")
                pg.add("sync", (lambda ci=ci, wv_ap=wv_ap, ro=ro, dc=dc: lambda e: e.dma_start(
                    out=s_wo[ci][:],
                    in_=wv_ap[ro:ro + 128, dc * 512:(dc + 1) * 512]))(),
                    wait=w if ci == 0 else [], inc=(osem2, 16))
            woldv = pg.val(osem2)
            for tc in range(4):
                pidx = (dc * 4 + tc) % 4
                ps = [pA, pB, pE2[:, 0:512], pE2[:, 512:1024]][pidx]
                for ci in range(32):
                    w = []
                    if ci == 0:
                        w = [(osem2, woldv), ("p3v", ygv), ("p3v", resv),
                             L_("p3v")]
                    pg.add("tensor", (lambda ps=ps, ci=ci, tc=tc: lambda e: e.matmul(
                        ps[:], s_yg[ci][:, tc * 128:(tc + 1) * 128],
                        s_wo[ci][:], start=(ci == 0), stop=(ci == 31)))(),
                        wait=w, inc=("p3t", 1) if ci == 31 else None)
                pg.add("vector", (lambda ps=ps, tc=tc, dc=dc: lambda e: e.tensor_tensor(
                    s_x2[tc][:, dc * 512:(dc + 1) * 512],
                    s_x2[tc][:, dc * 512:(dc + 1) * 512], ps[:], ALU.add))(),
                    wait=[L_("p3t")], inc=("p3v", 1))
            wo_mark = pg.val("p3t")
        yg_mark = pg.val("p3t")
        x2v = pg.val("p3v")
        for tc in range(4):
            pg.add("sync", (lambda tc=tc, tb=tb: lambda e: e.dma_start(
                out=out_loc.ap()[tb * 512 + tc * 128:tb * 512 + (tc + 1) * 128, :],
                in_=s_x2[tc][:]))(),
                wait=[("p3v", x2v)], inc=("outst", 16))
            outst_total += 16
            x2_marks[(tb, tc)] = outst_total
            pg.add("vector", (lambda tc=tc: lambda e: e.reduce_sum(
                s_cols[:, 0:1], s_x2[tc][:], axis=AX))(),
                wait=[("p3v", x2v)], inc=("p3v", 1))
            pg.add("scalar", (lambda tc=tc: lambda e: e.activation(
                s_sq[:], s_x2[tc][:], AF.Square))(),
                wait=[("p3v", x2v), ("p3v", sq3_mark)], inc=("p3s", 1))
            pg.add("vector", lambda e: e.reduce_sum(s_cols[:, 1:2], s_sq[:], axis=AX),
                   wait=[L_("p3s")], inc=("p3v", 1))
            sq3_mark = pg.val("p3v")

            pg.add("vector", lambda e: e.tensor_scalar(s_cols[:, 2:3], s_cols[:, 0:1], 1.0 / D, None, ALU.mult), inc=("p3v", 1))
            pg.add("vector", lambda e: e.tensor_scalar(s_cols[:, 3:4], s_cols[:, 1:2], 1.0 / D, None, ALU.mult), inc=("p3v", 1))
            pg.add("vector", lambda e: e.tensor_tensor(s_cols[:, 4:5], s_cols[:, 2:3], s_cols[:, 2:3], ALU.mult), inc=("p3v", 1))
            pg.add("vector", lambda e: e.tensor_tensor(s_cols[:, 3:4], s_cols[:, 3:4], s_cols[:, 4:5], ALU.subtract), inc=("p3v", 1))
            pg.add("scalar", lambda e: e.activation(s_cols[:, 5:6], s_cols[:, 3:4],
                                                    AF.Ln, bias=s_kon[:, 0:1]),
                   wait=[L_("p3v")], inc=("p3s", 1))
            pg.add("scalar", lambda e: e.activation(s_cols[:, 6:7], s_cols[:, 5:6],
                                                    AF.Exp, scale=s_kon[:, 1:2]),
                   inc=("p3s", 1))

            pg.add("vector", lambda e: e.tensor_tensor(s_cols[:, 8:9], s_cols[:, 2:3], s_cols[:, 6:7], ALU.mult), wait=[L_("p3s")], inc=("p3v", 1))
            pg.add("vector", lambda e: e.tensor_scalar(s_cols[:, 8:9], s_cols[:, 8:9], -1.0, None, ALU.mult), inc=("p3v", 1))
            pg.add("vector", (lambda tc=tc: lambda e: e.scalar_tensor_tensor(
                s_xn3[:], s_x2[tc][:], s_cols[:, 6:7],
                s_cols[:, 8:9].to_broadcast([128, 2048]),
                ALU.mult, ALU.add))(),
                wait=[L_("p3v")], inc=("p3v", 1))
            x2v_mark = pg.val("p3v")
            xn3v = pg.val("p3v")
            for dt in range(16):
                sl = dt % 2
                ptr_ = pC if sl == 0 else pD
                pg.add("tensor", (lambda dt=dt, ptr_=ptr_: lambda e: e.transpose(
                    ptr_[:, 0:128],
                    s_xn3[:, dt * 128:(dt + 1) * 128], s_idf[:]))(),
                    wait=[("p3v", xn3v), ("p3v", trslot_mark[sl]),
                          ("p3t", x3f_mark if dt == 0 else 0)],
                    inc=("p3t", 1))
                pg.add("vector", (lambda dt=dt, ptr_=ptr_: lambda e: e.tensor_copy(
                    s_x3f[:, dt * 128:(dt + 1) * 128],
                    ptr_[:, 0:128]))(),
                    wait=[L_("p3t")], inc=("p3v", 1))
                trslot_mark[sl] = pg.val("p3v")
                pg.add("vector", (lambda dt=dt, tc=tc: lambda e: e.tensor_copy(
                    s_x3b[:, dt * 512 + tc * 128:dt * 512 + (tc + 1) * 128],
                    s_x3f[:, dt * 128:(dt + 1) * 128]))(),
                    wait=[("xnst", x3b_mark)] if dt == 0 else [],
                    inc=("p3v", 1))
            trv = pg.val("p3v")
            for dt in range(16):
                w = [("p3v", trv), ("p0v", P0V_DONE)] if dt == 0 else []
                pg.add("tensor", (lambda dt=dt: lambda e: e.matmul(
                    pE1[:, 0:8], s_x3f[:, dt * 128:(dt + 1) * 128],
                    s_gwT[:, dt * 8:(dt + 1) * 8],
                    start=(dt == 0), stop=(dt == 15)))(),
                    wait=w, inc=("p3t", 1) if dt == 15 else None)
            x3f_mark = pg.val("p3t")
            gmv = pg.val("p3t")
            pg.add("vector", lambda e: e.reduce_max(s_cols[:, 16:17], pE1[:, 0:8],
                                                    axis=AX),
                   wait=[("p3t", gmv)], inc=("p3v", 1))
            pg.add("vector", lambda e: e.tensor_scalar(
                s_cols[:, 17:18], s_cols[:, 16:17], -1.0, None, ALU.mult),
                inc=("p3v", 1))
            pg.add("scalar", lambda e: e.activation(s_sc8[:], pE1[:, 0:8], AF.Exp,
                                                    bias=s_cols[:, 17:18]),
                   wait=[L_("p3v")], inc=("p3s", 1))
            pg.add("vector", lambda e: e.reduce_sum(s_cols[:, 18:19], s_sc8[:],
                                                    axis=AX),
                   wait=[L_("p3s")], inc=("p3v", 1))
            pg.add("vector", lambda e: e.reciprocal(s_cols[:, 19:20],
                                                    s_cols[:, 18:19]),
                   inc=("p3v", 1))
            pg.add("vector", lambda e: e.tensor_scalar(
                s_sc8[:], s_sc8[:], s_cols[:, 19:20], None, ALU.mult),
                inc=("p3v", 1))
            pg.add("vector", lambda e: e.max(s_m8[:], s_sc8[:]), inc=("p3v", 1))
            pg.add("vector", lambda e: e.tensor_copy(s_mv[:, 0:2], s_m8[:, 0:2]),
                   inc=("p3v", 1))
            pg.add("vector", lambda e: e.match_replace(s_mr[:], s_mv[:], s_sc8[:],
                                                       -1e30),
                   wait=[L_("p3v")], inc=("p3v", 1))
            pg.add("vector", lambda e: e.tensor_tensor(s_sc8b[:], s_sc8[:],
                                                       s_mr[:], ALU.not_equal),
                   inc=("p3v", 1))
            pg.add("vector", (lambda tb=tb, tc=tc: lambda e: e.tensor_tensor(
                s_comb[:, (tb * 4 + tc) * 8:(tb * 4 + tc + 1) * 8],
                s_sc8b[:], s_sc8[:], ALU.mult))(), inc=("p3v", 1))
        x3v = pg.val("p3v")
        for dt in range(16):
            pg.add("sync", (lambda dt=dt, tb=tb: lambda e: e.dma_start(
                out=xn3_scr.ap()[dt * 128:(dt + 1) * 128,
                                 tb * 512:(tb + 1) * 512],
                in_=s_x3b[:, dt * 512:(dt + 1) * 512]))(),
                wait=[("p3v", x3v)] if dt == 0 else [], inc=("xnst", 16))
            xnst_total += 16
        x3b_mark = xnst_total

    P3A_V = pg.val("p3v")
    P3A_T = pg.val("p3t")
    P3A_S = pg.val("p3s")

    pg.cuts["p3a"] = {e: len(pg.seq[e]) for e in ENGS}

    oacc_tot = {0: 0, 1: 0, 2: 0, 3: 0}
    last_acc = None
    g_mark = 0
    xt_marks = [0, 0]
    pe3_marks = [0, 0]
    pab_marks = {}
    sh_marks = [0, 0]
    wbuf_marks = [0]

    for e_idx in range(9):
        w1buf = s_w1e[0]
        w = [("p3v", P3A_V), ("p3t", P3A_T), ("p3s", P3A_S),
             ("outst", outst_total), ("xnst", xnst_total),
             ("p3t", wbuf_marks[0])]
        first = [set()]

        def wl(fn, q="sync"):
            ww = list(w) if q not in first[0] else []
            first[0].add(q)
            pg.add(q, fn, wait=ww, inc=("wexpg" if q == "gpsimd" else "wexp", 16))
        if e_idx < 8:
            v1 = arena_view(e_idx, "w1")
            v3 = arena_view(e_idx, "w3")
            v2 = arena_view(e_idx, "w2")
            for dt in range(16):
                wl((lambda dt=dt, v1=v1, w1buf=w1buf: lambda e: e.dma_start(
                    out=w1buf[:, dt * 1024:(dt + 1) * 1024],
                    in_=v1[dt * 128:(dt + 1) * 128, :]))())
            for dt in range(16):
                wl((lambda dt=dt, v3=v3: lambda e: e.dma_start(
                    out=s_w3e[:, dt * 1024:(dt + 1) * 1024],
                    in_=v3[dt * 128:(dt + 1) * 128, :]))(), q="scalar")
            for it in range(8):
                wl((lambda it=it, v2=v2: lambda e: e.dma_start(
                    out=s_w2e[:, it * 2048:(it + 1) * 2048],
                    in_=v2[it * 128:(it + 1) * 128, :]))(), q="gpsimd")
        else:
            for dt in range(16):
                for rr in range(8):
                    v = arena_view(rr, "ws1")
                    wl((lambda dt=dt, rr=rr, v=v, w1buf=w1buf: lambda e: e.dma_start(
                        out=w1buf[:, dt * 1024 + rr * 128:dt * 1024 + (rr + 1) * 128],
                        in_=v[dt * 128:(dt + 1) * 128, :]))())
            for dt in range(16):
                for rr in range(8):
                    v = arena_view(rr, "ws3")
                    wl((lambda dt=dt, rr=rr, v=v: lambda e: e.dma_start(
                        out=s_w3e[:, dt * 1024 + rr * 128:dt * 1024 + (rr + 1) * 128],
                        in_=v[dt * 128:(dt + 1) * 128, :]))(), q="scalar")
            for it in range(8):
                for rr in range(8):
                    v = arena_view(rr, "ws2")
                    wl((lambda it=it, rr=rr, v=v: lambda e: e.dma_start(
                        out=s_w2e[:, it * 2048 + rr * 256:it * 2048 + (rr + 1) * 256],
                        in_=v[it * 128:(it + 1) * 128, :]))(), q="gpsimd")
        wv = pg.val("wexp")
        wvg = pg.val("wexpg")
        for tb in range(8):
            xt = s_xt[tb % 2]
            xnsem = f"xnld{tb % 2}"
            w = [("xnst", xnst_total), ("p3t", xt_marks[tb % 2])]
            for dt in range(16):
                pg.add("sync", (lambda dt=dt, tb=tb, xt=xt: lambda e: e.dma_start(
                    out=xt[:, dt * 512:(dt + 1) * 512],
                    in_=xn3_scr.ap()[dt * 128:(dt + 1) * 128,
                                     tb * 512:(tb + 1) * 512]))(),
                    wait=w if dt == 0 else [], inc=(xnsem, 16))
            xnv = pg.val(xnsem)
            for it in range(8):
                ph = pA if it % 2 == 0 else pC
                pq = pB if it % 2 == 0 else pD
                hk = f"a{it % 2}"
                qk = f"b{it % 2}"
                for dt in range(16):
                    w = []
                    if dt == 0:
                        w = [("wexp", wv), (xnsem, xnv),
                             ("p3s", pab_marks.get(hk, 0))]
                    pg.add("tensor", (lambda w1buf=w1buf, xt=xt, it=it, dt=dt, ph=ph: lambda e: e.matmul(
                        ph[:], w1buf[:, dt * 1024 + it * 128:dt * 1024 + (it + 1) * 128],
                        xt[:, dt * 512:(dt + 1) * 512],
                        start=(dt == 0), stop=(dt == 15)))(),
                        wait=w, inc=("p3t", 1) if dt == 15 else None)
                for dt in range(16):
                    w = [("p3v", pab_marks.get(qk, 0))] if dt == 0 else []
                    pg.add("tensor", (lambda xt=xt, it=it, dt=dt, pq=pq: lambda e: e.matmul(
                        pq[:], s_w3e[:, dt * 1024 + it * 128:dt * 1024 + (it + 1) * 128],
                        xt[:, dt * 512:(dt + 1) * 512],
                        start=(dt == 0), stop=(dt == 15)))(),
                        wait=w, inc=("p3t", 1) if dt == 15 else None)
                hgv = pg.val("p3t")
                shb = s_sh[it % 2]
                pg.add("scalar", (lambda ph=ph, shb=shb: lambda e: e.activation(
                    shb[:], ph[:, 0:512], AF.Silu))(),
                       wait=[("p3t", hgv), ("p3v", sh_marks[it % 2])],
                       inc=("p3s", 1))
                pab_marks[hk] = pg.val("p3s")
                pg.add("vector", (lambda it=it, pq=pq, shb=shb: lambda e: e.tensor_tensor(
                    s_g[it][:], shb[:], pq[:, 0:512], ALU.mult))(),
                    wait=[L_("p3s"), ("p3t", g_mark if it == 0 else 0)],
                    inc=("p3v", 1))
                pab_marks[qk] = pg.val("p3v")
                sh_marks[it % 2] = pg.val("p3v")
            gv = pg.val("p3v")
            for tc in range(4):
                for dh in range(2):
                    ps = pE1 if dh == 0 else pE2
                    for q in range(2):
                        for it in range(8):
                            w = []
                            if it == 0:
                                w = [("p3v", gv), ("p3v", pe3_marks[dh]),
                                     ("wexpg", wvg)]
                            pg.add("tensor", (lambda ps=ps, it=it, tc=tc, dh=dh, q=q: lambda e: e.matmul(
                                ps[:, q * 512:(q + 1) * 512],
                                s_g[it][:, tc * 128:(tc + 1) * 128],
                                s_w2e[:, it * 2048 + dh * 1024 + q * 512:
                                      it * 2048 + dh * 1024 + (q + 1) * 512],
                                start=(it == 0), stop=(it == 7)))(),
                                wait=w, inc=("p3t", 1) if it == 7 else None)
                eov = pg.val("p3t")
                ye_war = [last_acc] if last_acc else []
                for dh in range(2):
                    ps = pE1 if dh == 0 else pE2
                    if e_idx < 8:
                        cc_ = s_comb[:, (tb * 4 + tc) * 8 + e_idx:
                                     (tb * 4 + tc) * 8 + e_idx + 1]
                        pg.add("vector", (lambda ps=ps, dh=dh, cc_=cc_: lambda e: e.tensor_scalar(
                            s_ye[:, dh * 1024:(dh + 1) * 1024], ps[:], cc_,
                            None, ALU.mult))(),
                            wait=([("p3t", eov)] + ye_war if dh == 0 else []),
                            inc=("p3v", 1))
                    else:
                        pg.add("vector", (lambda ps=ps, dh=dh: lambda e: e.tensor_copy(
                            s_ye[:, dh * 1024:(dh + 1) * 1024], ps[:]))(),
                            wait=([("p3t", eov)] + ye_war if dh == 0 else []),
                            inc=("p3v", 1))
                pe3_marks[0] = pg.val("p3v")
                pe3_marks[1] = pg.val("p3v")
                yev = pg.val("p3v")
                osem = f"oacc{tc}"
                pg.add("gpsimd", (lambda tb=tb, tc=tc: lambda e: e.dma_start(
                    out=out_loc.ap()[tb * 512 + tc * 128:
                                     tb * 512 + (tc + 1) * 128, :],
                    in_=s_ye[:], accum_op=ALU.add))(),
                    wait=[("p3v", yev), ("outst", outst_total),
                          (osem, oacc_tot[tc])],
                    inc=(osem, 16))
                oacc_tot[tc] += 16
                last_acc = (osem, oacc_tot[tc])
            g_mark = pg.val("p3t")
            xt_marks[tb % 2] = pg.val("p3t")
        wbuf_marks[0] = pg.val("p3t")

    final_oacc = [(f"oacc{tc}", oacc_tot[tc]) for tc in range(4)]
    pg.add("sync", lambda e: e.nop(),
           wait=[("outst", outst_total)] + final_oacc)
    pg.add("gpsimd", lambda e: e.nop(), wait=final_oacc)
    pg.add("vector", lambda e: e.nop(), wait=final_oacc)

    if stop is not None:
        for e in ENGS:
            pg.seq[e] = pg.seq[e][:pg.cuts[stop][e]]

    from contextlib import ExitStack
    with ExitStack() as ctx:
        sems = {n: ctx.enter_context(nc.semaphore(n)) for n in semnames}
        block = ctx.enter_context(nc.Block())

        CHAINED = ("vector", "scalar", "gpsimd")

        def runner(name):
            def run(eng):
                prev = None
                for waits, fn, inc, iv in pg.seq[name]:
                    seen = {}
                    if name in CHAINED and prev is not None:
                        seen[prev[0]] = prev[1]
                    for (s, v) in waits:
                        if v > 0 and seen.get(s, -1) < v:
                            seen[s] = v
                    for s, v in seen.items():
                        eng.wait_ge(sems[s], v)
                    instr = fn(eng)
                    if inc is not None:
                        assert instr is not None, f"op on {name} returned None but has inc {inc}"
                        instr.then_inc(sems[inc[0]], inc[1])
                        prev = (inc[0], iv)
            return run

        block.sync(runner("sync"))
        block.gpsimd(runner("gpsimd"))
        block.scalar(runner("scalar"))
        block.vector(runner("vector"))
        block.tensor(runner("tensor"))

    return nc


_LAST_PG = None
_NC_CACHE = None


def _get_nc():
    global _NC_CACHE
    if _NC_CACHE is None:
        _NC_CACHE = build_program()
    return _NC_CACHE


def kernel(L0_attn_x, L0_attn_norm_w, L0_attn_in_proj_w, L0_attn_conv_w,
           L0_attn_conv_b, L0_attn_x_proj_w, L0_attn_dt_w, L0_attn_dt_b,
           L0_attn_A_log, L0_attn_D_param, L0_attn_out_proj_w,
           L0_ffn_gate_w, L0_ffn_w1, L0_ffn_w2, L0_ffn_w3,
           L0_ffn_ws1, L0_ffn_ws2, L0_ffn_ws3):
    x = np.ascontiguousarray(L0_attn_x)
    identb = np.eye(128, dtype=BF)
    identf = np.eye(128, dtype=np.float32)
    in_maps = []
    for k in range(NCORE):
        b, q = k // 4, k % 4
        t0 = q * TLOC
        if t0 == 0:
            halo = np.zeros((3, D), np.float32)
        else:
            halo = x[b, t0 - 3:t0]
        x_l = np.ascontiguousarray(
            np.concatenate([halo, x[b, t0:t0 + TLOC]], 0).astype(np.float32))
        c0 = q * CLOC
        m = dict(
            x_loc=x_l,
            ip_sl=np.ascontiguousarray(L0_attn_in_proj_w[1024 * k:1024 * (k + 1)]),
            w1_sl=np.ascontiguousarray(L0_ffn_w1[k]),
            w2_sl=np.ascontiguousarray(L0_ffn_w2[k]),
            w3_sl=np.ascontiguousarray(L0_ffn_w3[k]),
            wo_sl=np.ascontiguousarray(L0_attn_out_proj_w[:, 512 * k:512 * (k + 1)]),
            ws1_sl=np.ascontiguousarray(L0_ffn_ws1[128 * k:128 * (k + 1)]),
            ws3_sl=np.ascontiguousarray(L0_ffn_ws3[128 * k:128 * (k + 1)]),
            ws2_sl=np.ascontiguousarray(L0_ffn_ws2[256 * k:256 * (k + 1)]),
            xp_full=np.ascontiguousarray(L0_attn_x_proj_w),
            gw_full=np.ascontiguousarray(L0_ffn_gate_w),
            nw_full=np.ascontiguousarray(L0_attn_norm_w),
            cw_full=np.ascontiguousarray(L0_attn_conv_w.reshape(DI, KC)),
            cb_full=np.ascontiguousarray(L0_attn_conv_b),
            alog_sl=np.ascontiguousarray(L0_attn_A_log[c0:c0 + CLOC]),
            dpar_sl=np.ascontiguousarray(L0_attn_D_param[c0:c0 + CLOC]),
            dtw_sl=np.ascontiguousarray(L0_attn_dt_w.reshape(DI)[c0:c0 + CLOC]),
            dtb_sl=np.ascontiguousarray(L0_attn_dt_b[c0:c0 + CLOC]),
            ident_bf=identb,
            ident_f=identf,
        )
        in_maps.append(m)

    nc = _get_nc()
    res = run_bass_kernel_spmd(nc, in_maps, core_ids=list(range(NCORE)))
    out = np.empty((B, L, D), np.float32)
    for k in range(NCORE):
        b, q = k // 4, k % 4
        out[b, q * TLOC:(q + 1) * TLOC] = res.results[k]["out_loc"]
    return out
